# version 03
# baseline (speedup 1.0000x reference)
import numpy as np
import ml_dtypes
import concourse.bass as bass
import concourse.mybir as mybir
from concourse.bass_utils import run_bass_kernel_spmd
from contextlib import ExitStack

F32 = mybir.dt.float32
BF16 = mybir.dt.bfloat16
AF = mybir.ActivationFunctionType
ALU = mybir.AluOpType
AX = mybir.AxisListType

D = 1024
T = 2048
CH = D // 128
NT = T // 512
PI = float(np.pi)
M23 = 8388608.0
M15 = 12582912.0
LN_EPS = 1e-5


class Sem:
    __slots__ = ("h", "n")
    def __init__(self, h):
        self.h = h
        self.n = 0
    @property
    def ev(self):
        return (self, self.n)


class MK:
    ENGS = ("tensor", "vector", "scalar", "gpsimd", "sync")

    def __init__(self, nc):
        self.nc = nc
        self.ctx = ExitStack()
        self.q = {e: [] for e in self.ENGS}
        self._semn = 0

    def sem(self, name=None):
        self._semn += 1
        name = name or f"s{self._semn}"
        return Sem(self.ctx.enter_context(self.nc.semaphore(name)))

    def sb(self, name, shape, dtype):
        return self.ctx.enter_context(self.nc.sbuf_tensor(name, shape, dtype))

    def psum(self, name, shape, dtype):
        return self.ctx.enter_context(self.nc.psum_tensor(name, shape, dtype))

    def op(self, eng, fn, waits=(), inc=None, dma=False):
        k = 16 if dma else 1
        tgt = None
        if inc is not None:
            inc.n += k
            tgt = (inc, k)
        waits = [(s, v) for (s, v) in waits if s is not None]
        tail_drain = (not dma) and eng in ("vector", "scalar", "gpsimd")

        def run(e):
            for (s, v) in waits:
                e.wait_ge(s.h, v)
            r = fn(e)
            if tail_drain:
                r = e.drain()
            if tgt is not None:
                (r if r is not None else e.nop()).then_inc(tgt[0].h, tgt[1])
        self.q[eng].append(run)
        return None if inc is None else inc.ev

    def finish(self):
        with self.nc.Block() as block:
            for eng in self.ENGS:
                if not self.q[eng]:
                    continue
                def make(lst):
                    def body(e):
                        for fn in lst:
                            fn(e)
                    return body
                getattr(block, eng)(make(self.q[eng]))
        self.ctx.close()


class Arena:
    def __init__(self, t, cols):
        self.t = t
        self.cols = cols
        self.pos = 0
        self.marks = []

    def alloc(self, n):
        assert self.pos + n <= self.cols, f"arena overflow {self.pos}+{n}>{self.cols}"
        s = self.t[:, self.pos:self.pos + n]
        self.pos += n
        return s

    def push(self):
        self.marks.append(self.pos)

    def pop(self):
        self.pos = self.marks.pop()


class WStream:
    def __init__(self, mk, slots, cols):
        self.mk = mk
        self.slots = slots
        self.sems = [mk.sem() for _ in slots]
        self.consume_ev = [None] * len(slots)
        self.rr = 0

    def load(self, mkaps, extra_waits=()):
        slot = self.rr % len(self.slots)
        self.rr += 1
        w = list(extra_waits)
        if self.consume_ev[slot] is not None:
            w.append(self.consume_ev[slot])
        sem = self.sems[slot]

        def fn(e, slot=slot):
            out_ap, in_ap = mkaps(self.slots[slot])
            return e.dma_start(out=out_ap, in_=in_ap)
        self.mk.op("gpsimd", fn, waits=w, inc=sem, dma=True)
        return (slot, sem, sem.n)

    def consumed(self, slot, ev):
        self.consume_ev[slot] = ev


class State:
    def __init__(self, mk: MK):
        self.mk = mk
        self.x = mk.sb("x_arena", [128, CH * T], F32)
        self.xln = mk.sb("xln_arena", [128, CH * T], BF16)
        self.A = Arena(mk.sb("arena_a", [128, 16384], BF16), 16384)
        self.B = Arena(mk.sb("arena_b", [128, 16384], BF16), 16384)
        self.Cf = Arena(mk.sb("arena_c", [128, 2048], F32), 2048)
        self.W = Arena(mk.sb("warena", [128, 8192], BF16), 8192)
        self.pb = [mk.psum(f"pb{i}", [128, 512], F32) for i in range(8)]
        self.cons = mk.sb("cons", [128, 256], F32)
        self.cbf = mk.sb("cbf", [128, 448], BF16)
        self.stat = mk.sb("stat", [128, 2048], F32)
        self.f2s = mk.sb("f2s", [128, 2816], BF16)
        self.lnscr = mk.sb("lnscr", [128, 1472], F32)
        self.ident_f = self.cons[:, 0:128]
        self.ones_f = self.cons[0:1, 128:256]
        self.ones_f32 = self.cons[32:33, 128:256]
        self.ident_b = self.cbf[:, 0:128]
        self.ones_b = self.cbf[:, 128:129]
        self.ones_brow = self.cbf[0:1, 192:320]
        g = mk.sem("init")

        def init(e):
            e.memset(self.ident_f, 1.0)
            e.drain()
            e.affine_select(self.ident_f, self.ident_f, [[-1, 128]], ALU.is_equal,
                            0.0, channel_multiplier=1)
            e.drain()
            e.memset(self.ones_f, 1.0)
            e.drain()
            e.memset(self.ones_f32, 1.0)
            e.drain()
            e.memset(self.ident_b, 1.0)
            e.drain()
            e.affine_select(self.ident_b, self.ident_b, [[-1, 128]], ALU.is_equal,
                            0.0, channel_multiplier=1)
            e.drain()
            e.memset(self.ones_b, 1.0)
            e.drain()
            return e.memset(self.ones_brow, 1.0)
        self.init_ev = mk.op("gpsimd", init, inc=g)
        for eng in ("tensor", "vector", "scalar", "sync"):
            mk.op(eng, lambda e: None, waits=[self.init_ev])

    def mean_nt(self, nt):
        return self.stat[0:1, nt * 512:(nt + 1) * 512]

    def rstd_nt(self, nt):
        return self.stat[32:33, nt * 512:(nt + 1) * 512]



def emit_ln(mk: MK, st: State, ready, sems):
    sv, sa, sp = sems["v"], sems["a"], sems["p"]
    st.B.push()
    xb = st.B.alloc(4096)
    sqb = st.B.alloc(4096)
    tf = st.lnscr[:, 0:512]
    var = st.lnscr[0:1, 512:1024]

    prev_stats_done = None
    prev_apply_done = None
    prev_mm_done = None
    xln_ev = None
    for nt in range(NT):
        c0 = nt * 512
        mean, rstd = st.mean_nt(nt), st.rstd_nt(nt)
        w = [x for x in (ready, prev_mm_done) if x]

        def conv(e, c0=c0):
            for c in range(CH):
                xc = st.x[:, c * T + c0: c * T + c0 + 512]
                e.activation(xb[:, c * 512:(c + 1) * 512], xc, AF.Copy)
                r = e.activation(sqb[:, c * 512:(c + 1) * 512], xc, AF.Square)
            return r
        conv_ev = mk.op("scalar", conv, waits=w, inc=sa)
        w = [x for x in (conv_ev, prev_stats_done) if x]

        def mmstat(e):
            for c in range(CH):
                e.matmul(st.pb[4][0:1, :], st.ones_b, xb[:, c * 512:(c + 1) * 512],
                         start=(c == 0), stop=(c == CH - 1), skip_group_check=True)
            for c in range(CH):
                r = e.matmul(st.pb[5][0:1, :], st.ones_b, sqb[:, c * 512:(c + 1) * 512],
                             start=(c == 0), stop=(c == CH - 1), skip_group_check=True)
            return r
        prev_mm_done = mk.op("tensor", mmstat, waits=w, inc=sp)

        def smath(e, mean=mean):
            e.tensor_scalar(mean, st.pb[4][0:1, :], 1.0 / D, None, ALU.mult)
            e.drain()
            e.tensor_tensor(var, mean, mean, ALU.mult)
            e.drain()
            e.scalar_tensor_tensor(var, st.pb[5][0:1, :], 1.0 / D, var,
                                   ALU.mult, ALU.subtract)
            e.drain()
            return e.tensor_scalar(var, var, LN_EPS, None, ALU.add)
        ev = mk.op("vector", smath, waits=[prev_mm_done], inc=sv)
        prev_stats_done = ev
        ev = mk.op("scalar", lambda e, rstd=rstd: e.activation(rstd, var, AF.Sqrt),
                   waits=[ev], inc=sa)
        ev = mk.op("vector", lambda e, rstd=rstd: e.reciprocal(rstd, rstd),
                   waits=[ev], inc=sv)
        w = [x for x in (ev, prev_apply_done) if x]

        def bcast(e, mean=mean, rstd=rstd):
            e.matmul(st.pb[0][:, :], st.ones_f, mean, start=True, stop=True,
                     skip_group_check=True)
            return e.matmul(st.pb[1][:, :], st.ones_f32, rstd, start=True, stop=True,
                            skip_group_check=True)
        ev = mk.op("tensor", bcast, waits=w, inc=sp)

        def apply(e, c0=c0):
            for c in range(CH):
                xc = st.x[:, c * T + c0: c * T + c0 + 512]
                xlc = st.xln[:, c * T + c0: c * T + c0 + 512]
                e.tensor_tensor(tf, xc, st.pb[0][:, :], ALU.subtract)
                e.drain()
                e.tensor_tensor(xlc, tf, st.pb[1][:, :], ALU.mult)
                e.drain()
            return None
        xln_ev = mk.op("vector", apply, waits=[ev], inc=sv)
        prev_apply_done = xln_ev
    st.B.pop()
    return xln_ev



def emit_f0_prep(mk: MK, st: State, w0t, sems, wsig, gate=None):
    sa = sems["a"]
    st.A.push()
    tmp = st.A.alloc(8192)
    ld = mk.sem()
    gw_ = [gate] if gate else []
    for k in range(CH):
        mk.op("gpsimd", lambda e, k=k: e.dma_start(
            out=tmp[:, k * 1024:(k + 1) * 1024],
            in_=w0t[k * 128:(k + 1) * 128, :]), waits=gw_, inc=ld, dma=True)

    def sign(e):
        for k in range(CH):
            r = e.activation(wsig[:, k * 1024:(k + 1) * 1024],
                             tmp[:, k * 1024:(k + 1) * 1024], AF.Sign)
        return r
    ev = mk.op("scalar", sign, waits=[ld.ev], inc=sa)
    st.A.pop()
    return ev


def emit_f0(mk: MK, st: State, xln_ev, wsig_ev, wsig, sems):
    sv, sp = sems["v"], sems["p"]
    ev_prev = {}
    done = None
    for o in range(CH):
        for nt in range(NT):
            i = o * NT + nt
            bank = st.pb[i % 4]
            w = [xln_ev, wsig_ev]
            if i >= 4:
                w.append(ev_prev[i % 4])

            def mms(e, o=o, nt=nt, bank=bank):
                for k in range(CH):
                    r = e.matmul(bank[:, :],
                                 wsig[:, k * 1024 + o * 128: k * 1024 + (o + 1) * 128],
                                 st.xln[:, k * T + nt * 512: k * T + nt * 512 + 512],
                                 start=(k == 0), stop=(k == CH - 1),
                                 skip_group_check=True)
                return r
            mev = mk.op("tensor", mms, waits=w, inc=sp)

            def evac(e, o=o, nt=nt, bank=bank):
                xc = st.x[:, o * T + nt * 512: o * T + nt * 512 + 512]
                return e.scalar_tensor_tensor(xc, bank[:, :], 0.5, xc,
                                              ALU.mult, ALU.add)
            done = mk.op("vector", evac, waits=[mev], inc=sv)
            ev_prev[i % 4] = done
    return done



def emit_f1(mk: MK, st: State, x_ready, xln_ev, wd, sems):
    sv, sa, sp = sems["v"], sems["a"], sems["p"]
    st.A.push(); st.B.push(); st.Cf.push(); st.W.push()
    xh = st.A.alloc(4096)
    xl = st.A.alloc(4096)
    gates = st.A.alloc(2048)
    sel = st.A.alloc(1024)
    ebsb = st.A.alloc(1024)
    wh = st.A.alloc(64)
    wl = st.A.alloc(64)
    wslot = [st.W.alloc(2048) for _ in range(4)]
    rw_f = st.Cf.alloc(64)
    swn = st.Cf.alloc(1)
    rbc = st.Cf.alloc(1)
    lgsb = st.Cf.alloc(512)
    scr = st.Cf.alloc(16)
    m1, m1n, ssum, srec = scr[:, 0:1], scr[:, 1:2], scr[:, 2:3], scr[:, 3:4]
    ohm = scr[:, 4:12]
    gtm = st.B.alloc(8)

    ld0 = mk.sem()
    gw_ = [x_ready] if x_ready else []
    mk.op("gpsimd", lambda e: e.dma_start(
        out=rw_f.rearrange("p (c e) -> p c e", e=8),
        in_=wd["rwt"].rearrange("(c p) e -> p c e", p=128)), waits=gw_, inc=ld0, dma=True)
    mk.op("gpsimd", lambda e: e.dma_start(out=rbc[0:8, :], in_=wd["rb"][:, :]),
          waits=gw_, inc=ld0, dma=True)
    mk.op("gpsimd", lambda e: e.dma_start(out=ebsb[0:8, :], in_=wd["eb"][:, :]),
          waits=gw_, inc=ld0, dma=True)

    def mksel(e):
        e.memset(sel, 1.0)
        e.drain()
        for ex in range(8):
            e.affine_select(sel[:, ex * 128:(ex + 1) * 128],
                            sel[:, ex * 128:(ex + 1) * 128],
                            [[0, 128]], ALU.is_equal, 0.0, base=-ex,
                            channel_multiplier=1)
            e.drain()
        return None
    sel_ev = mk.op("gpsimd", mksel, waits=[st.init_ev] + gw_, inc=sems["g"])

    whl_ev = mk.op("scalar", lambda e: e.activation(wh, rw_f, AF.Copy),
                   waits=[ld0.ev], inc=sa)
    whl_ev = mk.op("vector", lambda e: e.tensor_tensor(wl, rw_f, wh, ALU.subtract),
                   waits=[whl_ev], inc=sv)

    def swmm(e):
        for c in range(CH):
            e.matmul(st.pb[3][0:8, 0:1], wh[:, c * 8:(c + 1) * 8], st.ones_b,
                     start=(c == 0), stop=False, skip_group_check=True)
        for c in range(CH):
            r = e.matmul(st.pb[3][0:8, 0:1], wl[:, c * 8:(c + 1) * 8], st.ones_b,
                         start=False, stop=(c == CH - 1), skip_group_check=True)
        return r
    ev = mk.op("tensor", swmm, waits=[whl_ev, st.init_ev], inc=sp)
    swn_ev = mk.op("vector", lambda e: e.tensor_scalar(
        swn[0:8, :], st.pb[3][0:8, 0:1], -1.0, None, ALU.mult), waits=[ev], inc=sv)

    gates_ev = None
    prev = None
    for nt in range(NT):
        c0 = nt * 512

        def split(e, c0=c0):
            for c in range(CH):
                r = e.activation(xh[:, c * 512:(c + 1) * 512],
                                 st.x[:, c * T + c0:c * T + c0 + 512], AF.Copy)
            return r
        w = [x_ready]
        if prev:
            w.append(prev)
        ev_h = mk.op("scalar", split, waits=w, inc=sa)

        def split2(e, c0=c0):
            for c in range(CH):
                e.tensor_tensor(xl[:, c * 512:(c + 1) * 512],
                                st.x[:, c * T + c0:c * T + c0 + 512],
                                xh[:, c * 512:(c + 1) * 512], ALU.subtract)
                e.drain()
            return None
        ev_l = mk.op("vector", split2, waits=[ev_h], inc=sv)

        def lgmm(e):
            for (wop, xop) in ((wh, xh), (wl, xh), (wh, xl)):
                for c in range(CH):
                    r = e.matmul(st.pb[2][0:8, :], wop[:, c * 8:(c + 1) * 8],
                                 xop[:, c * 512:(c + 1) * 512],
                                 start=(wop is wh and xop is xh and c == 0),
                                 stop=(wop is wh and xop is xl and c == CH - 1),
                                 skip_group_check=True)
            return r
        prev = mk.op("tensor", lgmm, waits=[ev_l, swn_ev], inc=sp)
        ev = mk.op("vector", lambda e: e.tensor_copy(lgsb[0:8, :], st.pb[2][0:8, :]),
                   waits=[prev, xln_ev], inc=sv)
        def bc2(e, nt=nt):
            e.matmul(st.pb[0][0:8, :], st.ones_f[0:1, 0:8], st.mean_nt(nt),
                     start=True, stop=True, skip_group_check=True)
            return e.matmul(st.pb[1][0:8, :], st.ones_f32[0:1, 0:8], st.rstd_nt(nt),
                            start=True, stop=True, skip_group_check=True)
        ev = mk.op("tensor", bc2, waits=[ev], inc=sp)

        def corr(e):
            e.scalar_tensor_tensor(lgsb[0:8, :], st.pb[0][0:8, :], swn[0:8, :],
                                   lgsb[0:8, :], ALU.mult, ALU.add)
            e.drain()
            e.tensor_tensor(lgsb[0:8, :], lgsb[0:8, :], st.pb[1][0:8, :], ALU.mult)
            e.drain()
            return e.tensor_scalar(lgsb[0:8, :], lgsb[0:8, :], rbc[0:8, :], None,
                                   ALU.add)
        ev = mk.op("vector", corr, waits=[ev, swn_ev], inc=sv)

        def ltr(e):
            for gq in range(4):
                r = e.transpose(st.pb[2][:, gq * 8:(gq + 1) * 8],
                                lgsb[0:8, gq * 128:(gq + 1) * 128],
                                st.ident_f[0:8, 0:8])
            return r
        ev = mk.op("tensor", ltr, waits=[ev], inc=sp)
        for gq in range(4):
            tile = st.pb[2][:, gq * 8:(gq + 1) * 8]

            def sm1(e, tile=tile):
                e.tensor_reduce(m1, tile, AX.X, ALU.max)
                e.drain()
                return e.tensor_scalar(m1n, m1, -1.0, None, ALU.mult)
            ev = mk.op("vector", sm1, waits=[ev], inc=sv)
            ev = mk.op("scalar", lambda e, tile=tile: e.activation(
                ohm, tile, AF.Exp, bias=m1n, accum_out=ssum), waits=[ev], inc=sa)

            def sm2(e, tile=tile):
                e.reciprocal(srec, ssum)
                e.drain()
                e.tensor_scalar(ohm, tile, m1, None, ALU.is_equal)
                e.drain()
                return e.tensor_scalar(gtm, ohm, srec, None, ALU.mult)
            ev = mk.op("vector", sm2, waits=[ev], inc=sv)
            ev = mk.op("tensor", lambda e, gq=gq: e.transpose(
                st.pb[3][0:8, 0:256].bitcast(BF16)[:, gq * 64:gq * 64 + 64].rearrange(
                    "p (a b) -> p (a b)") if False else
                st.pb[3][:, 0:256].bitcast(BF16)[0:8, gq * 128:(gq + 1) * 128],
                gtm, st.ident_b), waits=[ev], inc=sp)
            ev = mk.op("vector", lambda e, gq=gq, nt=nt: e.tensor_copy(
                gates[0:8, nt * 512 + gq * 128: nt * 512 + (gq + 1) * 128],
                st.pb[3][:, 0:256].bitcast(BF16)[0:8, gq * 128:(gq + 1) * 128]),
                waits=[ev], inc=sv)
        gates_ev = ev

    gbs = st.B.alloc(4096)
    ue = [st.B.alloc(4096), st.B.alloc(4096)]
    ws = WStream(mk, wslot, 1024)
    ue_consumed = [None, None]
    done = None
    pb2_free_for_gb = gates_ev
    for nt in range(NT):
        c0 = nt * 512
        ev_gb = gates_ev
        for ex in range(8):
            w = [ev_gb, sel_ev]
            if done is not None:
                w.append(done)
            ev_mm = mk.op("tensor", lambda e, ex=ex, nt=nt: e.matmul(
                st.pb[6][:, :], sel[0:8, ex * 128:(ex + 1) * 128],
                gates[0:8, nt * 512:(nt + 1) * 512],
                start=True, stop=True, skip_group_check=True), waits=w, inc=sp)
            ev_gb = mk.op("vector", lambda e, ex=ex: e.tensor_copy(
                gbs[:, ex * 512:(ex + 1) * 512], st.pb[6][:, :]),
                waits=[ev_mm], inc=sv)
        gbs_ev = ev_gb
        mm_prev = None
        for ex in range(8):
            u = ue[ex % 2]
            w = [gbs_ev]
            if ue_consumed[ex % 2]:
                w.append(ue_consumed[ex % 2])

            def mkue(e, u=u, ex=ex, c0=c0):
                for c in range(CH):
                    e.tensor_tensor(u[:, c * 512:(c + 1) * 512],
                                    st.xln[:, c * T + c0:c * T + c0 + 512],
                                    gbs[:, ex * 512:(ex + 1) * 512], ALU.mult)
                    e.drain()
                return None
            uev = mk.op("vector", mkue, waits=w, inc=sv)
            toks = []
            for h in range(4):
                r0 = (ex * 8 + h * 2) * 128
                toks.append(ws.load(lambda slot_t, r0=r0: (
                    slot_t.rearrange("p (c n) -> p c n", n=1024),
                    wd["ewt"][r0:r0 + 2 * 128, :].rearrange("(c p) n -> p c n", p=128))))

            def mme(e, ex=ex, nt=nt, uu=u, toks=toks):
                r = None
                for h in range(4):
                    slot, sem, val = toks[h]
                    e.wait_ge(sem.h, val)
                    for c2 in range(2):
                        c = h * 2 + c2
                        for o in range(8):
                            r = e.matmul(
                                st.pb[o][:, :],
                                wslot[slot][:, c2 * 1024 + o * 128:
                                            c2 * 1024 + (o + 1) * 128],
                                uu[:, c * 512:(c + 1) * 512],
                                start=(ex == 0 and c == 0), stop=False,
                                skip_group_check=True)
                return r
            mm_prev = mk.op("tensor", mme, waits=[uev], inc=sp)
            ue_consumed[ex % 2] = mm_prev
            for h in range(4):
                ws.consumed(toks[h][0], mm_prev)

        def mmb(e, nt=nt):
            for o in range(8):
                r = e.matmul(st.pb[o][:, :], ebsb[0:8, o * 128:(o + 1) * 128],
                             gates[0:8, nt * 512:(nt + 1) * 512],
                             start=False, stop=True, skip_group_check=True)
            return r
        mm_prev = mk.op("tensor", mmb, waits=[mm_prev], inc=sp)

        def evac(e, nt=nt):
            for o in range(8):
                xc = st.x[:, o * T + nt * 512: o * T + nt * 512 + 512]
                e.scalar_tensor_tensor(xc, st.pb[o][:, :], 0.5, xc,
                                       ALU.mult, ALU.add)
                e.drain()
            return None
        done = mk.op("vector", evac, waits=[mm_prev], inc=sv)

    st.A.pop(); st.B.pop(); st.Cf.pop(); st.W.pop()
    return done



def emit_rot_perm(mk: MK, st: State, dst, kind, sems, scr, ready=None):
    sv = sems["v"]
    scr.push()
    jv = scr.alloc(128)
    pv = scr.alloc(1)
    t1 = scr.alloc(128)
    par = scr.alloc(128)
    mm_ = scr.alloc(128)
    mp_ = scr.alloc(128)
    gp = mk.sem()
    w0 = [st.init_ev] + ([ready] if ready else [])
    ev = mk.op("gpsimd", lambda e: (
        e.iota(jv, [[1, 128]], channel_multiplier=0,
               allow_small_or_imprecise_dtypes=True),
        e.drain(),
        e.iota(pv, [[0, 1]], channel_multiplier=1,
               allow_small_or_imprecise_dtypes=True))[-1], waits=w0, inc=gp)

    def build(e):
        e.tensor_scalar(t1, jv, -1.0, pv, ALU.mult, ALU.add)
        e.drain()
        if kind == "il64":
            e.tensor_scalar(par, jv, 0.5, -0.25, ALU.mult, ALU.add)
            e.drain()
            e.tensor_scalar(par, par, M15, None, ALU.add)
            e.drain()
            e.tensor_scalar(par, par, M15, None, ALU.subtract)
            e.drain()
            e.scalar_tensor_tensor(par, par, -2.0, jv, ALU.mult, ALU.add)
            e.drain()
            e.tensor_scalar(mm_, t1, 1.0, None, ALU.is_equal)
            e.drain()
            e.tensor_scalar(par, par, 0.5, None, ALU.is_lt)
            e.drain()
            e.tensor_tensor(mm_, mm_, par, ALU.mult)
            e.drain()
            e.tensor_scalar(mp_, t1, -1.0, None, ALU.is_equal)
            e.drain()
            e.tensor_scalar(par, par, 0.5, None, ALU.is_lt)
            e.drain()
            e.tensor_tensor(mp_, mp_, par, ALU.mult)
            e.drain()
            e.scalar_tensor_tensor(dst, mm_, -1.0, mp_, ALU.mult, ALU.add)
            e.drain()
        elif kind == "half64":
            jb = par
            e.tensor_scalar(jb, jv, 64.0, None, ALU.is_ge)
            e.drain()
            e.scalar_tensor_tensor(jb, jb, -64.0, jv, ALU.mult, ALU.add)
            e.drain()
            e.tensor_scalar(mm_, t1, 32.0, None, ALU.is_equal)
            e.drain()
            e.tensor_scalar(mp_, jb, 32.0, None, ALU.is_lt)
            e.drain()
            e.tensor_tensor(mm_, mm_, mp_, ALU.mult)
            e.drain()
            e.tensor_scalar(mp_, t1, -32.0, None, ALU.is_equal)
            e.drain()
            e.tensor_scalar(jb, jb, 32.0, None, ALU.is_ge)
            e.drain()
            e.tensor_tensor(mp_, mp_, jb, ALU.mult)
            e.drain()
            e.scalar_tensor_tensor(dst, mm_, -1.0, mp_, ALU.mult, ALU.add)
            e.drain()
        else:
            raise ValueError(kind)
        return None
    ev = mk.op("vector", build, waits=[ev], inc=sv)
    scr.pop()
    return ev


def emit_rope_tables(mk: MK, st: State, Ct, St_, kind, sems, ready=None):
    sv, sa = sems["v"], sems["a"]
    st.Cf.push()
    pos = st.Cf.alloc(1024)
    frq = st.Cf.alloc(512)
    rr = st.Cf.alloc(512)
    invf = st.lnscr[:, 1040:1041]
    pmr = st.lnscr[:, 1041:1042]
    tmp32 = st.lnscr[:, 1042:1043]
    gp = mk.sem()
    w0 = [st.init_ev] + ([ready] if ready else [])
    ev0 = mk.op("gpsimd", lambda e: (
        e.iota(pos, [[1, 1024]], channel_multiplier=0,
               allow_small_or_imprecise_dtypes=True),
        e.drain(),
        e.iota(invf, [[0, 1]], channel_multiplier=1,
               allow_small_or_imprecise_dtypes=True))[-1], waits=w0, inc=gp)

    def mkinv(e):
        e.tensor_scalar(pmr, invf, 64.0, None, ALU.is_ge)
        e.drain()
        e.scalar_tensor_tensor(pmr, pmr, -64.0, invf, ALU.mult, ALU.add)
        e.drain()
        if kind == "il64":
            e.tensor_scalar(pmr, pmr, 0.5, -0.25, ALU.mult, ALU.add)
            e.drain()
            e.tensor_scalar(pmr, pmr, M15, None, ALU.add)
            e.drain()
            e.tensor_scalar(pmr, pmr, M15, None, ALU.subtract)
            e.drain()
        else:
            e.tensor_scalar(tmp32, pmr, 32.0, None, ALU.is_ge)
            e.drain()
            e.scalar_tensor_tensor(pmr, tmp32, -32.0, pmr, ALU.mult, ALU.add)
            e.drain()
        return None
    ev = mk.op("vector", mkinv, waits=[ev0], inc=sv)
    ev = mk.op("scalar", lambda e: e.activation(
        invf, pmr, AF.Exp, scale=-2.0 * float(np.log(10000.0)) / 64.0),
        waits=[ev], inc=sa)
    evs = []
    for half in range(2):
        off = half * 1024
        for piece in range(2):
            po = piece * 512

            def mkfr(e, off=off, po=po):
                e.scalar_tensor_tensor(frq, pos[:, po:po + 512],
                                       0.0 if off == 0 else 0.0, pos[:, po:po + 512],
                                       ALU.mult, ALU.bypass) if False else None
                e.tensor_scalar(frq, pos[:, po:po + 512], float(off), invf,
                                ALU.add, ALU.mult)
                e.drain()
                return None
            ev = mk.op("vector", mkfr, waits=[ev], inc=sv)

            def redsin(e):
                e.tensor_scalar(rr, frq, 1.0 / (2 * PI), M23, ALU.mult, ALU.add)
                e.drain()
                e.tensor_scalar(rr, rr, M23, None, ALU.subtract)
                e.drain()
                e.scalar_tensor_tensor(rr, rr, -2 * PI, frq, ALU.mult, ALU.add)
                e.drain()
                e.tensor_scalar(rr, rr, PI, -PI, ALU.min, ALU.max)
                e.drain()
                return None
            ev = mk.op("vector", redsin, waits=[ev], inc=sv)
            ev = mk.op("scalar", lambda e, off=off, po=po: e.activation(
                St_[:, off + po: off + po + 512], rr, AF.Sin), waits=[ev], inc=sa)

            def redcos(e):
                e.tensor_scalar(rr, frq, 1.0 / (2 * PI), 0.25, ALU.mult, ALU.add)
                e.drain()
                e.tensor_scalar(rr, rr, M23, None, ALU.add)
                e.drain()
                e.tensor_scalar(rr, rr, M23, None, ALU.subtract)
                e.drain()
                e.scalar_tensor_tensor(rr, rr, -2 * PI, frq, ALU.mult, ALU.add)
                e.drain()
                e.tensor_scalar(rr, rr, PI / 2, None, ALU.add)
                e.drain()
                e.tensor_scalar(rr, rr, PI, -PI, ALU.min, ALU.max)
                e.drain()
                return None
            ev = mk.op("vector", redcos, waits=[ev], inc=sv)
            ev = mk.op("scalar", lambda e, off=off, po=po: e.activation(
                Ct[:, off + po: off + po + 512], rr, AF.Sin), waits=[ev], inc=sa)
    st.Cf.pop()
    return ev



def emit_attn_l1(mk: MK, st: State, xln_ev, wd, sems):
    sv, sa, sp = sems["v"], sems["a"], sems["p"]
    st.A.push(); st.B.push(); st.W.push()
    q_sb = st.A.alloc(2048)
    k_sb = st.A.alloc(2048)
    v_sb = st.A.alloc(2048)
    vT = st.A.alloc(2048)
    Ct = st.A.alloc(2048)
    St_ = st.A.alloc(2048)
    P_sb = st.A.alloc(2048)
    PT = st.A.alloc(2048)
    Rm = st.cbf[:, 320:448]
    O_all = st.B.alloc(16384)
    wslot = [st.W.alloc(1024) for _ in range(4)]
    ws = WStream(mk, wslot, 1024)
    mx4 = st.lnscr[:, 1024:1028]
    m1 = st.lnscr[:, 1028:1029]
    m1s = st.lnscr[:, 1029:1030]
    dnc = st.lnscr[:, 1032:1036]
    den = st.lnscr[:, 1036:1037]
    rec = st.lnscr[:, 1037:1038]

    tab_ev = emit_rope_tables(mk, st, Ct, St_, "il64", sems, ready=xln_ev)
    perm_ev = emit_rot_perm(mk, st, Rm, "il64", sems, st.Cf, ready=tab_ev)

    pb67_free = [None, None]
    attn_done = None
    exp_done_prev = None
    pt_bank_free = None
    pv_bank_free = None
    for hg in range(CH):
        qkv_ev = {}
        for gi, (blk, dst) in enumerate(((hg, q_sb), (8 + hg, k_sb), (16 + hg, v_sb))):
            tok = ws.load(lambda slot_t, blk=blk: (
                slot_t, wd["wqkv_t"][blk * 128:(blk + 1) * 128, :]),
                extra_waits=[xln_ev])
            ev_pr = None
            for nt in range(NT):
                bank = st.pb[6 + nt % 2]
                w = [xln_ev]
                if pb67_free[nt % 2]:
                    w.append(pb67_free[nt % 2])

                def mmp(e, tok=tok, nt=nt, bank=bank):
                    slot, sem, val = tok
                    e.wait_ge(sem.h, val)
                    for c in range(CH):
                        r = e.matmul(bank[:, :],
                                     wslot[slot][:, c * 128:(c + 1) * 128],
                                     st.xln[:, c * T + nt * 512: c * T + nt * 512 + 512],
                                     start=(c == 0), stop=(c == CH - 1),
                                     skip_group_check=True)
                    return r
                mev = mk.op("tensor", mmp, waits=w, inc=sp)
                ev_pr = mk.op("vector", lambda e, dst=dst, nt=nt, bank=bank:
                              e.tensor_copy(dst[:, nt * 512:(nt + 1) * 512], bank[:, :]),
                              waits=[mev], inc=sv)
                pb67_free[nt % 2] = ev_pr
            qkv_ev[gi] = ev_pr
            ws.consumed(tok[0], mev)
        rope_evs = []
        for (src, ev_src) in ((q_sb, qkv_ev[0]), (k_sb, qkv_ev[1])):
            for nt in range(NT):
                w = [ev_src, tab_ev, perm_ev]
                if attn_done:
                    w.append(attn_done)
                if pb67_free[nt % 2]:
                    w.append(pb67_free[nt % 2])
                bank = st.pb[6 + nt % 2]
                rev = mk.op("tensor", lambda e, src=src, nt=nt, bank=bank: e.matmul(
                    bank[:, :], Rm, src[:, nt * 512:(nt + 1) * 512],
                    start=True, stop=True, skip_group_check=True), waits=w, inc=sp)

                def rope(e, src=src, nt=nt, bank=bank):
                    s_ = slice(nt * 512, (nt + 1) * 512)
                    e.tensor_tensor(P_sb[:, s_], src[:, s_], Ct[:, s_], ALU.mult)
                    e.drain()
                    e.tensor_tensor(PT[:, s_], bank[:, :], St_[:, s_], ALU.mult)
                    e.drain()
                    e.tensor_tensor(src[:, s_], P_sb[:, s_], PT[:, s_], ALU.add)
                    e.drain()
                    return None
                rev = mk.op("vector", rope, waits=[rev], inc=sv)
                pb67_free[nt % 2] = rev
                rope_evs.append(rev)
        rope_done = rope_evs[-1]
        vt_evs = []
        for kc2 in range(2):
            w = [qkv_ev[2]]
            if pt_bank_free:
                w.append(pt_bank_free)
            if attn_done:
                w.append(attn_done)

            def vtr(e, kc2=kc2):
                pbf = st.pb[4][:, :].bitcast(BF16)
                for j in range(8):
                    kc = kc2 * 8 + j
                    r = e.transpose(pbf[:, j * 128:(j + 1) * 128],
                                    v_sb[:, kc * 128:(kc + 1) * 128], st.ident_b)
                return r
            tev = mk.op("tensor", vtr, waits=w, inc=sp)

            def vte(e, kc2=kc2):
                pbf = st.pb[4][:, :].bitcast(BF16)
                for j in range(8):
                    kc = kc2 * 8 + j
                    e.tensor_copy(vT[:, kc * 128:(kc + 1) * 128],
                                  pbf[:, j * 128:(j + 1) * 128])
                return None
            pt_bank_free = mk.op("vector", vte, waits=[tev], inc=sv)
            vt_evs.append(pt_bank_free)
        vt_done = pt_bank_free
        for h2 in range(2):
            hb = h2 * 64
            for qt in range(16):
                w = [rope_done]
                if exp_done_prev:
                    w.append(exp_done_prev)

                def smm(e, hb=hb, qt=qt):
                    for n2 in range(4):
                        r = e.matmul(st.pb[n2][:, :],
                                     q_sb[hb:hb + 64, qt * 128:(qt + 1) * 128],
                                     k_sb[hb:hb + 64, n2 * 512:(n2 + 1) * 512],
                                     start=True, stop=True, skip_group_check=True)
                    return r
                sev = mk.op("tensor", smm, waits=w, inc=sp)

                def rmax(e):
                    for n2 in range(4):
                        e.tensor_reduce(mx4[:, n2:n2 + 1], st.pb[n2][:, :],
                                        AX.X, ALU.max)
                        e.drain()
                    e.tensor_reduce(m1, mx4, AX.X, ALU.max)
                    e.drain()
                    e.tensor_scalar(m1s, m1, -0.125, None, ALU.mult)
                    e.drain()
                    return None
                mev = mk.op("vector", rmax, waits=[sev], inc=sv)

                def expp(e):
                    for n2 in range(4):
                        e.activation(P_sb[:, n2 * 512:(n2 + 1) * 512],
                                     st.pb[n2][:, :], AF.Exp, bias=m1s, scale=0.125,
                                     accum_out=dnc[:, n2:n2 + 1])
                        e.drain()
                    return None
                exp_done_prev = mk.op("scalar", expp, waits=[mev], inc=sa)

                def norm(e):
                    e.tensor_tensor(den, dnc[:, 0:1], dnc[:, 1:2], ALU.add)
                    e.drain()
                    e.tensor_tensor(rec, dnc[:, 2:3], dnc[:, 3:4], ALU.add)
                    e.drain()
                    e.tensor_tensor(den, den, rec, ALU.add)
                    e.drain()
                    e.reciprocal(rec, den)
                    e.drain()
                    e.tensor_scalar(P_sb[:, :], P_sb[:, :], rec, None, ALU.mult)
                    e.drain()
                    return None
                nev = mk.op("vector", norm, waits=[exp_done_prev], inc=sv)
                ptev = None
                for kc2 in range(2):
                    w = [nev]
                    if pv_bank_free:
                        w.append(pv_bank_free)

                    def ptr(e, kc2=kc2):
                        pbf = st.pb[5][:, :].bitcast(BF16)
                        for j in range(8):
                            kc = kc2 * 8 + j
                            r = e.transpose(pbf[:, j * 128:(j + 1) * 128],
                                            P_sb[:, kc * 128:(kc + 1) * 128],
                                            st.ident_b)
                        return r
                    tev = mk.op("tensor", ptr, waits=w, inc=sp)
                    eng = "vector" if kc2 == 0 else "scalar"

                    def pte(e, kc2=kc2, eng=eng):
                        pbf = st.pb[5][:, :].bitcast(BF16)
                        for j in range(8):
                            kc = kc2 * 8 + j
                            if eng == "vector":
                                e.tensor_copy(PT[:, kc * 128:(kc + 1) * 128],
                                              pbf[:, j * 128:(j + 1) * 128])
                            else:
                                e.activation(PT[:, kc * 128:(kc + 1) * 128],
                                             pbf[:, j * 128:(j + 1) * 128], AF.Copy)
                        return None
                    pv_bank_free = mk.op(eng, pte, waits=[tev],
                                         inc=sv if eng == "vector" else sa)
                    ptev = pv_bank_free
                w = [ptev, vt_done]
                if attn_done:
                    w.append(attn_done)

                def pvm(e, hb=hb):
                    for kc in range(16):
                        r = e.matmul(st.pb[4][0:64, 0:128],
                                     vT[:, kc * 128 + hb: kc * 128 + hb + 64],
                                     PT[:, kc * 128:(kc + 1) * 128],
                                     start=(kc == 0), stop=(kc == 15),
                                     skip_group_check=True)
                    return r
                pvev = mk.op("tensor", pvm, waits=w, inc=sp)
                attn_done = mk.op("vector", lambda e, hg=hg, hb=hb, qt=qt:
                                  e.tensor_copy(
                                      O_all[hb:hb + 64,
                                            hg * 2048 + qt * 128: hg * 2048 + (qt + 1) * 128],
                                      st.pb[4][0:64, 0:128]),
                                  waits=[pvev], inc=sv)
    done = None
    ev_prev = {}
    for o in range(CH):
        tok = ws.load(lambda slot_t, o=o: (
            slot_t, wd["wproj_t"][o * 128:(o + 1) * 128, :]), extra_waits=[attn_done])
        for nt in range(NT):
            i = o * NT + nt
            bank = st.pb[i % 4]
            w = [attn_done]
            if i >= 4:
                w.append(ev_prev[i % 4])

            def mmo(e, tok=tok, nt=nt, bank=bank):
                slot, sem, val = tok
                e.wait_ge(sem.h, val)
                for c in range(CH):
                    r = e.matmul(bank[:, :],
                                 wslot[slot][:, c * 128:(c + 1) * 128],
                                 O_all[:, c * 2048 + nt * 512: c * 2048 + nt * 512 + 512],
                                 start=(c == 0), stop=(c == CH - 1),
                                 skip_group_check=True)
                return r
            mev = mk.op("tensor", mmo, waits=w, inc=sp)
            if nt == NT - 1:
                ws.consumed(tok[0], mev)

            def evac(e, o=o, nt=nt, bank=bank):
                xc = st.x[:, o * T + nt * 512: o * T + nt * 512 + 512]
                return e.tensor_tensor(xc, xc, bank[:, :], ALU.add)
            done = mk.op("vector", evac, waits=[mev], inc=sv)
            ev_prev[i % 4] = done
    st.A.pop(); st.B.pop(); st.W.pop()
    return done


def _mk_l3_perm(mk: MK, st: State, dst, sems, scr, ready=None):
    sv = sems["v"]
    scr.push()
    jv = scr.alloc(128)
    pv = scr.alloc(1)
    t1 = scr.alloc(128)
    jb = scr.alloc(128)
    mm_ = scr.alloc(128)
    mp_ = scr.alloc(128)
    gp = mk.sem()
    w0 = [st.init_ev] + ([ready] if ready else [])
    ev = mk.op("gpsimd", lambda e: (
        e.iota(jv, [[1, 128]], channel_multiplier=0,
               allow_small_or_imprecise_dtypes=True),
        e.drain(),
        e.iota(pv, [[0, 1]], channel_multiplier=1,
               allow_small_or_imprecise_dtypes=True))[-1], waits=w0, inc=gp)

    def build(e):
        e.tensor_scalar(jb, jv, 64.0, None, ALU.is_ge)
        e.drain()
        e.scalar_tensor_tensor(jb, jb, -64.0, jv, ALU.mult, ALU.add)
        e.drain()
        e.tensor_scalar(t1, jv, -1.0, pv, ALU.mult, ALU.add)
        e.drain()
        e.tensor_tensor(t1, t1, jb, ALU.add)
        e.drain()
        e.tensor_scalar(mm_, jb, 2.0, 1.0, ALU.mult, ALU.add)
        e.drain()
        e.tensor_tensor(mm_, t1, mm_, ALU.is_equal)
        e.drain()
        e.tensor_scalar(mp_, jb, 32.0, None, ALU.is_lt)
        e.drain()
        e.tensor_tensor(mm_, mm_, mp_, ALU.mult)
        e.drain()
        e.tensor_scalar(mp_, jb, 2.0, -64.0, ALU.mult, ALU.add)
        e.drain()
        e.tensor_tensor(mp_, t1, mp_, ALU.is_equal)
        e.drain()
        e.tensor_scalar(jb, jb, 32.0, None, ALU.is_ge)
        e.drain()
        e.tensor_tensor(mp_, mp_, jb, ALU.mult)
        e.drain()
        e.tensor_tensor(jb, t1, jb if False else jb, ALU.is_equal) if False else None
        e.scalar_tensor_tensor(mp_, mm_, -1.0, mp_, ALU.mult, ALU.add)
        e.drain()
        e.tensor_scalar(t1, jv, -1.0, pv, ALU.mult, ALU.add)
        e.drain()
        e.tensor_scalar(t1, t1, 0.0, None, ALU.is_equal)
        e.drain()
        e.tensor_tensor(dst, mp_, t1, ALU.add)
        e.drain()
        return None
    ev = mk.op("vector", build, waits=[ev], inc=sv)
    scr.pop()
    return ev


def emit_attn_l23(mk: MK, st: State, xln_ev, wd, sems, layer):
    sv, sa, sp = sems["v"], sems["a"], sems["p"]
    st.A.push(); st.B.push(); st.W.push(); st.Cf.push()
    q_sb = st.A.alloc(2048)
    k_sb = st.A.alloc(2048)
    v_sb = k_sb if layer == 2 else st.A.alloc(2048)
    vT = st.A.alloc(2048)
    Ct = st.A.alloc(2048)
    St_ = st.A.alloc(2048)
    P_sb = st.A.alloc(2048)
    PT = st.A.alloc(2048)
    Rm = st.cbf[:, 320:448]
    O_all = st.B.alloc(16384)
    wslot = [st.W.alloc(1024) for _ in range(4)]
    ws = WStream(mk, wslot, 1024)
    bias_c = st.lnscr[:, 1056:1088]
    mx4 = st.lnscr[:, 1024:1028]
    m1 = st.lnscr[:, 1028:1029]
    m1s = st.lnscr[:, 1029:1030]
    dnc = st.lnscr[:, 1032:1036]
    den = st.lnscr[:, 1036:1037]
    rec = st.lnscr[:, 1037:1038]

    tab_ev = emit_rope_tables(mk, st, Ct, St_, "half32", sems, ready=xln_ev)
    if layer == 2:
        perm_ev = emit_rot_perm(mk, st, Rm, "half64", sems, st.Cf, ready=tab_ev)
    else:
        perm_ev = _mk_l3_perm(mk, st, Rm, sems, st.Cf, ready=tab_ev)
        mlo = st.lnscr[:, 1043:1044]
        mhi = st.lnscr[:, 1044:1045]

        def mkmask(e):
            e.tensor_scalar(mhi, mlo, 64.0, None, ALU.is_ge)
            e.drain()
            e.scalar_tensor_tensor(mhi, mhi, -64.0, mlo, ALU.mult, ALU.add)
            e.drain()
            e.tensor_scalar(mhi, mhi, 32.0, None, ALU.is_ge)
            e.drain()
            e.tensor_scalar(mlo, mhi, -1.0, 1.0, ALU.mult, ALU.add)
            e.drain()
            return None
        gp2 = mk.sem()
        evm = mk.op("gpsimd", lambda e: e.iota(mlo, [[0, 1]], channel_multiplier=1,
                                               allow_small_or_imprecise_dtypes=True),
                    waits=[tab_ev], inc=gp2)
        evm = mk.op("vector", mkmask, waits=[evm], inc=sv)

        def mkE(e):
            e.tensor_scalar(St_[:, :], St_[:, :], mlo, None, ALU.mult)
            e.drain()
            e.scalar_tensor_tensor(Ct[:, :], Ct[:, :], mhi, St_[:, :],
                                   ALU.mult, ALU.add)
            e.drain()
            return None
        tab_ev = mk.op("vector", mkE, waits=[evm, tab_ev], inc=sv)

    ldb = mk.sem()
    if layer == 2:
        mk.op("gpsimd", lambda e: e.dma_start(out=bias_c[:, 0:8],
                                              in_=wd["bout"][:, :]),
              waits=[xln_ev], inc=ldb, dma=True)
    else:
        mk.op("gpsimd", lambda e: e.dma_start(out=bias_c[:, 0:8],
                                              in_=wd["bout"][:, :]),
              waits=[xln_ev], inc=ldb, dma=True)
        mk.op("gpsimd", lambda e: e.dma_start(out=bias_c[:, 8:32],
                                              in_=wd["bqkv"][:, :]),
              waits=[xln_ev], inc=ldb, dma=True)
    bias_ev = ldb.ev

    pb67_free = [None, None]
    attn_done = None
    exp_done_prev = None
    pt_bank_free = None
    pv_bank_free = None
    scale = 0.125
    for hg in range(CH):
        if layer == 2:
            projs = [(q_sb, "wq_t", hg, None), (k_sb, "wkv_t", hg, None)]
        else:
            projs = [(q_sb, "wqkv_t", hg, 8 + hg), (k_sb, "wqkv_t", 8 + hg, 16 + hg),
                     (v_sb, "wqkv_t", 16 + hg, None)]
        qkv_ev = {}
        for gi, spec in enumerate(projs):
            dst, wkey, blk = spec[0], spec[1], spec[2]
            tok = ws.load(lambda slot_t, wkey=wkey, blk=blk: (
                slot_t, wd[wkey][blk * 128:(blk + 1) * 128, :]), extra_waits=[xln_ev])
            ev_pr = None
            for nt in range(NT):
                bank = st.pb[6 + nt % 2]
                w = [xln_ev]
                if pb67_free[nt % 2]:
                    w.append(pb67_free[nt % 2])

                def mmp(e, tok=tok, nt=nt, bank=bank):
                    slot, sem, val = tok
                    e.wait_ge(sem.h, val)
                    for c in range(CH):
                        r = e.matmul(bank[:, :],
                                     wslot[slot][:, c * 128:(c + 1) * 128],
                                     st.xln[:, c * T + nt * 512: c * T + nt * 512 + 512],
                                     start=(c == 0), stop=(c == CH - 1),
                                     skip_group_check=True)
                    return r
                mev = mk.op("tensor", mmp, waits=w, inc=sp)
                if layer == 3:
                    blkq = blk

                    def evp(e, dst=dst, nt=nt, bank=bank, blkq=blkq):
                        return e.tensor_scalar(dst[:, nt * 512:(nt + 1) * 512],
                                               bank[:, :],
                                               bias_c[:, 8 + blkq:9 + blkq], None,
                                               ALU.add)
                    ev_pr = mk.op("vector", evp, waits=[mev, bias_ev], inc=sv)
                else:
                    ev_pr = mk.op("vector", lambda e, dst=dst, nt=nt, bank=bank:
                                  e.tensor_copy(dst[:, nt * 512:(nt + 1) * 512],
                                                bank[:, :]), waits=[mev], inc=sv)
                pb67_free[nt % 2] = ev_pr
            qkv_ev[gi] = ev_pr
            ws.consumed(tok[0], mev)
        rope_evs = []
        if layer == 2:
            rope_srcs = ((q_sb, qkv_ev[0]), (k_sb, qkv_ev[1]))
        else:
            rope_srcs = ((q_sb, qkv_ev[0]), (k_sb, qkv_ev[1]))
        for (srcb, ev_src) in rope_srcs:
            for nt in range(NT):
                w = [ev_src, tab_ev, perm_ev]
                if attn_done:
                    w.append(attn_done)
                if pb67_free[nt % 2]:
                    w.append(pb67_free[nt % 2])
                bank = st.pb[6 + nt % 2]
                rev = mk.op("tensor", lambda e, srcb=srcb, nt=nt, bank=bank: e.matmul(
                    bank[:, :], Rm, srcb[:, nt * 512:(nt + 1) * 512],
                    start=True, stop=True, skip_group_check=True), waits=w, inc=sp)
                if layer == 2:
                    def rope(e, srcb=srcb, nt=nt, bank=bank):
                        s_ = slice(nt * 512, (nt + 1) * 512)
                        e.tensor_tensor(P_sb[:, s_], srcb[:, s_], Ct[:, s_], ALU.mult)
                        e.drain()
                        e.tensor_tensor(PT[:, s_], bank[:, :], St_[:, s_], ALU.mult)
                        e.drain()
                        e.tensor_tensor(srcb[:, s_], P_sb[:, s_], PT[:, s_], ALU.add)
                        e.drain()
                        return None
                else:
                    def rope(e, srcb=srcb, nt=nt, bank=bank):
                        s_ = slice(nt * 512, (nt + 1) * 512)
                        e.tensor_tensor(srcb[:, s_], bank[:, :], Ct[:, s_], ALU.mult)
                        e.drain()
                        return None
                rev = mk.op("vector", rope, waits=[rev], inc=sv)
                pb67_free[nt % 2] = rev
                rope_evs.append(rev)
        rope_done = rope_evs[-1]
        v_ready = rope_done if layer == 2 else qkv_ev[2]
        for kc2 in range(2):
            w = [v_ready]
            if pt_bank_free:
                w.append(pt_bank_free)
            if attn_done:
                w.append(attn_done)

            def vtr(e, kc2=kc2):
                pbf = st.pb[4][:, :].bitcast(BF16)
                for j in range(8):
                    kc = kc2 * 8 + j
                    r = e.transpose(pbf[:, j * 128:(j + 1) * 128],
                                    v_sb[:, kc * 128:(kc + 1) * 128], st.ident_b)
                return r
            tev = mk.op("tensor", vtr, waits=w, inc=sp)

            def vte(e, kc2=kc2):
                pbf = st.pb[4][:, :].bitcast(BF16)
                for j in range(8):
                    kc = kc2 * 8 + j
                    e.tensor_copy(vT[:, kc * 128:(kc + 1) * 128],
                                  pbf[:, j * 128:(j + 1) * 128])
                return None
            pt_bank_free = mk.op("vector", vte, waits=[tev], inc=sv)
        vt_done = pt_bank_free
        for h2 in range(2):
            hb = h2 * 64
            for qt in range(16):
                w = [rope_done, vt_done]
                if exp_done_prev:
                    w.append(exp_done_prev)

                def smm(e, hb=hb, qt=qt):
                    for n2 in range(4):
                        r = e.matmul(st.pb[n2][:, :],
                                     q_sb[hb:hb + 64, qt * 128:(qt + 1) * 128],
                                     k_sb[hb:hb + 64, n2 * 512:(n2 + 1) * 512],
                                     start=True, stop=True, skip_group_check=True)
                    return r
                sev = mk.op("tensor", smm, waits=w, inc=sp)

                def rmax(e):
                    for n2 in range(4):
                        e.tensor_reduce(mx4[:, n2:n2 + 1], st.pb[n2][:, :],
                                        AX.X, ALU.max)
                        e.drain()
                    e.tensor_reduce(m1, mx4, AX.X, ALU.max)
                    e.drain()
                    e.tensor_scalar(m1s, m1, -scale, None, ALU.mult)
                    e.drain()
                    return None
                mev = mk.op("vector", rmax, waits=[sev], inc=sv)

                def expp(e):
                    for n2 in range(4):
                        e.activation(P_sb[:, n2 * 512:(n2 + 1) * 512],
                                     st.pb[n2][:, :], AF.Exp, bias=m1s, scale=scale,
                                     accum_out=dnc[:, n2:n2 + 1])
                        e.drain()
                    return None
                exp_done_prev = mk.op("scalar", expp, waits=[mev], inc=sa)

                def norm(e):
                    e.tensor_tensor(den, dnc[:, 0:1], dnc[:, 1:2], ALU.add)
                    e.drain()
                    e.tensor_tensor(rec, dnc[:, 2:3], dnc[:, 3:4], ALU.add)
                    e.drain()
                    e.tensor_tensor(den, den, rec, ALU.add)
                    e.drain()
                    e.reciprocal(rec, den)
                    e.drain()
                    e.tensor_scalar(P_sb[:, :], P_sb[:, :], rec, None, ALU.mult)
                    e.drain()
                    return None
                nev = mk.op("vector", norm, waits=[exp_done_prev], inc=sv)
                ptev = None
                for kc2 in range(2):
                    w = [nev]
                    if pv_bank_free:
                        w.append(pv_bank_free)

                    def ptr(e, kc2=kc2):
                        pbf = st.pb[5][:, :].bitcast(BF16)
                        for j in range(8):
                            kc = kc2 * 8 + j
                            r = e.transpose(pbf[:, j * 128:(j + 1) * 128],
                                            P_sb[:, kc * 128:(kc + 1) * 128],
                                            st.ident_b)
                        return r
                    tev = mk.op("tensor", ptr, waits=w, inc=sp)
                    eng = "vector" if kc2 == 0 else "scalar"

                    def pte(e, kc2=kc2, eng=eng):
                        pbf = st.pb[5][:, :].bitcast(BF16)
                        for j in range(8):
                            kc = kc2 * 8 + j
                            if eng == "vector":
                                e.tensor_copy(PT[:, kc * 128:(kc + 1) * 128],
                                              pbf[:, j * 128:(j + 1) * 128])
                            else:
                                e.activation(PT[:, kc * 128:(kc + 1) * 128],
                                             pbf[:, j * 128:(j + 1) * 128], AF.Copy)
                        return None
                    pv_bank_free = mk.op(eng, pte, waits=[tev],
                                         inc=sv if eng == "vector" else sa)
                    ptev = pv_bank_free
                w = [ptev, vt_done]
                if attn_done:
                    w.append(attn_done)

                def pvm(e, hb=hb):
                    for kc in range(16):
                        r = e.matmul(st.pb[4][0:64, 0:128],
                                     vT[:, kc * 128 + hb: kc * 128 + hb + 64],
                                     PT[:, kc * 128:(kc + 1) * 128],
                                     start=(kc == 0), stop=(kc == 15),
                                     skip_group_check=True)
                    return r
                pvev = mk.op("tensor", pvm, waits=w, inc=sp)
                attn_done = mk.op("vector", lambda e, hg=hg, hb=hb, qt=qt:
                                  e.tensor_copy(
                                      O_all[hb:hb + 64,
                                            hg * 2048 + qt * 128: hg * 2048 + (qt + 1) * 128],
                                      st.pb[4][0:64, 0:128]),
                                  waits=[pvev], inc=sv)
    done = None
    ev_prev = {}
    for o in range(CH):
        tok = ws.load(lambda slot_t, o=o: (
            slot_t, wd["wout_t"][o * 128:(o + 1) * 128, :]), extra_waits=[attn_done])
        for nt in range(NT):
            i = o * NT + nt
            bank = st.pb[i % 4]
            w = [attn_done, bias_ev]
            if i >= 4:
                w.append(ev_prev[i % 4])

            def mmo(e, tok=tok, nt=nt, bank=bank):
                slot, sem, val = tok
                e.wait_ge(sem.h, val)
                for c in range(CH):
                    r = e.matmul(bank[:, :],
                                 wslot[slot][:, c * 128:(c + 1) * 128],
                                 O_all[:, c * 2048 + nt * 512: c * 2048 + nt * 512 + 512],
                                 start=(c == 0), stop=(c == CH - 1),
                                 skip_group_check=True)
                return r
            mev = mk.op("tensor", mmo, waits=w, inc=sp)
            if nt == NT - 1:
                ws.consumed(tok[0], mev)

            def evac(e, o=o, nt=nt, bank=bank):
                xc = st.x[:, o * T + nt * 512: o * T + nt * 512 + 512]
                e.scalar_tensor_tensor(xc, bank[:, :], bias_c[:, o:o + 1], xc,
                                       ALU.add, ALU.add)
                e.drain()
                return None
            done = mk.op("vector", evac, waits=[mev], inc=sv)
            ev_prev[i % 4] = done
    st.A.pop(); st.B.pop(); st.W.pop(); st.Cf.pop()
    return done



def emit_a0(mk: MK, st: State, xln_ev, wd, sems):
    sv, sa, sp = sems["v"], sems["a"], sems["p"]
    st.A.push(); st.B.push(); st.Cf.push()
    l1 = st.A.alloc(8192)
    h1 = st.A.alloc(8192)
    l2 = st.B.alloc(4096)
    h2 = st.B.alloc(4096)
    q_sb = st.B.alloc(4096)
    att = st.B.alloc(4096)
    filt = st.Cf.alloc(128)
    fcomb = st.Cf.alloc(64)
    denr = st.Cf.alloc(512)
    gb = st.Cf.alloc(16)
    ln2scr = st.Cf.alloc(512)
    var2 = st.Cf.alloc(512)

    ldf = mk.sem()
    for fi, key in enumerate(["hf", "gf", "hs", "gs"]):
        mk.op("gpsimd", lambda e, fi=fi, key=key: e.dma_start(
            out=filt[:, fi * 32:(fi + 1) * 32].rearrange("p (c k) -> p c k", k=4),
            in_=wd[key].rearrange("(c p) k -> p c k", p=128)), waits=[xln_ev],
            inc=ldf, dma=True)
    mk.op("gpsimd", lambda e: e.dma_start(out=gb[:, 0:8], in_=wd["gamma"][:, :]),
          waits=[xln_ev], inc=ldf, dma=True)
    mk.op("gpsimd", lambda e: e.dma_start(out=gb[:, 8:16], in_=wd["beta"][:, :]),
          waits=[xln_ev], inc=ldf, dma=True)
    fev = ldf.ev

    def fc(fi, c, k):
        return filt[:, fi * 32 + c * 4 + k: fi * 32 + c * 4 + k + 1]

    def mkcomb(e):
        for fi2, fi in enumerate((2, 3)):
            for c in range(CH):
                base = fi2 * 24 + c * 3
                e.tensor_tensor(fcomb[:, base:base + 1], fc(fi, c, 1), fc(fi, c, 2), ALU.add)
                e.drain()
                e.tensor_tensor(fcomb[:, base + 1:base + 2], fc(fi, c, 0), fc(fi, c, 1), ALU.add)
                e.drain()
                e.tensor_tensor(fcomb[:, base + 2:base + 3], fc(fi, c, 2), fc(fi, c, 3), ALU.add)
                e.drain()
        return None
    comb_ev = mk.op("vector", mkcomb, waits=[fev], inc=sv)

    def down(e, src_get, dst, dstw, N, fi_l, fi_h):
        Nh = N // 2
        for c in range(CH):
            s = src_get(c)
            for (fi, d) in ((fi_l, dst), (fi_h, dstw)):
                dc = d[:, c * Nh:(c + 1) * Nh]
                e.tensor_scalar(dc, s[:, 0:N:2], fc(fi, c, 1), None, ALU.mult)
                e.drain()
                e.scalar_tensor_tensor(dc, s[:, 1:N:2], fc(fi, c, 2), dc,
                                       ALU.mult, ALU.add)
                e.drain()
                e.scalar_tensor_tensor(dc[:, 1:Nh], s[:, 1:N - 2:2], fc(fi, c, 0),
                                       dc[:, 1:Nh], ALU.mult, ALU.add)
                e.drain()
                e.scalar_tensor_tensor(dc[:, 0:Nh - 1], s[:, 2:N:2], fc(fi, c, 3),
                                       dc[:, 0:Nh - 1], ALU.mult, ALU.add)
                e.drain()
        return None

    ev = mk.op("vector", lambda e: down(
        e, lambda c: st.xln[:, c * T:(c + 1) * T], l1, h1, 2048, 0, 1),
        waits=[xln_ev, fev], inc=sv)
    ev = mk.op("vector", lambda e: down(
        e, lambda c: l1[:, c * 1024:(c + 1) * 1024], l2, h2, 1024, 0, 1),
        waits=[ev], inc=sv)
    lev_ev = ev

    def mkq(e):
        for c in range(CH):
            lc = l2[:, c * 512:(c + 1) * 512]
            e.tensor_scalar(q_sb[:, c * 512:(c + 1) * 512], lc, 0.0, None, ALU.min)
            e.drain()
        return None
    ev = mk.op("vector", mkq, waits=[lev_ev], inc=sv)

    def mkq2(e):
        for c in range(CH):
            qc = q_sb[:, c * 512:(c + 1) * 512]
            e.activation(qc, qc, AF.Exp)
            e.drain()
        return None
    ev = mk.op("scalar", mkq2, waits=[ev], inc=sa)

    def mkq3(e):
        for c in range(CH):
            lc = l2[:, c * 512:(c + 1) * 512]
            qc = q_sb[:, c * 512:(c + 1) * 512]
            e.tensor_scalar(ln2scr, lc, 0.0, None, ALU.max)
            e.drain()
            e.tensor_tensor(qc, qc, ln2scr, ALU.add)
            e.drain()
        return None
    q_ev = mk.op("vector", mkq3, waits=[ev], inc=sv)

    qTvT = l1[:, 0:512]
    kvs = l1[:, 512:576]
    zbf = l1[:, 576:577]
    numb = l1[:, 640:1152]
    att_done = None
    tp_free = None
    for h in range(16):
        c = h // 2
        hb = (h % 2) * 64
        w = [q_ev]
        if tp_free:
            w.append(tp_free)

        def tr(e, c=c, hb=hb):
            pbf = st.pb[4][:, :].bitcast(BF16)
            idl = st.ident_b[hb:hb + 64, hb:hb + 64]
            for j in range(4):
                e.transpose(pbf[:, j * 64:(j + 1) * 64],
                            q_sb[hb:hb + 64, c * 512 + j * 128: c * 512 + (j + 1) * 128],
                            idl)
            for j in range(4):
                r = e.transpose(pbf[:, 256 + j * 64: 256 + (j + 1) * 64],
                                l2[hb:hb + 64, c * 512 + j * 128: c * 512 + (j + 1) * 128],
                                idl)
            return r
        tev = mk.op("tensor", tr, waits=w, inc=sp)

        def tre(e):
            pbf = st.pb[4][:, :].bitcast(BF16)
            e.tensor_copy(qTvT[:, :], pbf[:, 0:512])
            e.drain()
            return None
        trev = mk.op("vector", tre, waits=[tev], inc=sv)
        tp_free = trev

        def kvm(e):
            for j in range(4):
                r = e.matmul(st.pb[3][0:64, 0:64], qTvT[:, j * 64:(j + 1) * 64],
                             qTvT[:, 256 + j * 64: 256 + (j + 1) * 64],
                             start=(j == 0), stop=(j == 3), skip_group_check=True)
            return r
        kev = mk.op("tensor", kvm, waits=[trev], inc=sp)
        zrow = st.lnscr[hb:hb + 64, 1100:1101]

        def zred(e, c=c, hb=hb):
            e.tensor_reduce(zrow, q_sb[hb:hb + 64, c * 512:(c + 1) * 512],
                            AX.X, ALU.add)
            e.drain()
            e.tensor_copy(zbf[hb:hb + 64, :], zrow)
            e.drain()
            e.tensor_copy(kvs[hb:hb + 64, :], st.pb[3][0:64, 0:64])
            e.drain()
            return None
        zev = mk.op("vector", zred, waits=[kev, q_ev], inc=sv)

        def numm(e, c=c, hb=hb):
            e.matmul(st.pb[0][0:64, :], kvs[hb:hb + 64, :],
                     q_sb[hb:hb + 64, c * 512:(c + 1) * 512],
                     start=True, stop=True, skip_group_check=True)
            return e.matmul(st.pb[1][0:1, :], zbf[hb:hb + 64, :],
                            q_sb[hb:hb + 64, c * 512:(c + 1) * 512],
                            start=True, stop=True, skip_group_check=True)
        nev = mk.op("tensor", numm, waits=[zev], inc=sp)

        def dinv(e):
            e.tensor_scalar(denr[0:1, :], st.pb[1][0:1, :], 1e-6, None, ALU.add)
            e.drain()
            e.reciprocal(denr[0:1, :], denr[0:1, :])
            e.drain()
            e.tensor_copy(numb, st.pb[0][0:128, :])
            e.drain()
            return None
        dev = mk.op("vector", dinv, waits=[nev], inc=sv)
        bev = mk.op("tensor", lambda e: e.matmul(
            st.pb[2][:, :], st.ones_f, denr[0:1, :], start=True, stop=True,
            skip_group_check=True), waits=[dev], inc=sp)
        att_done = mk.op("vector", lambda e, c=c, hb=hb: e.tensor_tensor(
            att[hb:hb + 64, c * 512:(c + 1) * 512], numb[0:64, :],
            st.pb[2][0:64, :], ALU.mult), waits=[bev], inc=sv)

    low1 = l1[:, 0:8192]

    def up(e, srcl, srch, dst, N):
        Nh = N // 2
        for c in range(CH):
            sl = srcl[:, c * Nh:(c + 1) * Nh]
            sh = srch[:, c * Nh:(c + 1) * Nh]
            dc = dst[:, c * N:(c + 1) * N]
            for fi2, s in ((0, sl), (1, sh)):
                cb = lambda j: fcomb[:, fi2 * 24 + c * 3 + j: fi2 * 24 + c * 3 + j + 1]
                f0 = fc(2 + fi2, c, 0)
                f3 = fc(2 + fi2, c, 3)
                if fi2 == 0:
                    e.tensor_scalar(dc[:, 0:N:2], s, cb(0), None, ALU.mult)
                    e.drain()
                    e.tensor_scalar(dc[:, 1:N:2], s, cb(1), None, ALU.mult)
                    e.drain()
                else:
                    e.scalar_tensor_tensor(dc[:, 0:N:2], s, cb(0), dc[:, 0:N:2],
                                           ALU.mult, ALU.add)
                    e.drain()
                    e.scalar_tensor_tensor(dc[:, 1:N:2], s, cb(1), dc[:, 1:N:2],
                                           ALU.mult, ALU.add)
                    e.drain()
                e.scalar_tensor_tensor(dc[:, 2:N:2], s[:, 0:Nh - 1], f0,
                                       dc[:, 2:N:2], ALU.mult, ALU.add)
                e.drain()
                e.scalar_tensor_tensor(dc[:, 0:N - 2:2], s[:, 1:Nh], f3,
                                       dc[:, 0:N - 2:2], ALU.mult, ALU.add)
                e.drain()
                e.scalar_tensor_tensor(dc[:, 1:N - 2:2], s[:, 1:Nh], cb(2),
                                       dc[:, 1:N - 2:2], ALU.mult, ALU.add)
                e.drain()
        return None
    ev = mk.op("vector", lambda e: up(e, att, h2, low1, 1024),
               waits=[att_done, comb_ev], inc=sv)
    low0 = st.B.t[:, 0:16384]
    ev = mk.op("vector", lambda e: up(e, low1, h1, low0, 2048), waits=[ev], inc=sv)

    def mks(e):
        for c in range(CH):
            e.tensor_tensor(low0[:, c * T:(c + 1) * T], low0[:, c * T:(c + 1) * T],
                            st.xln[:, c * T:(c + 1) * T], ALU.add)
            e.drain()
        return None
    s_ev = mk.op("vector", mks, waits=[ev], inc=sv)

    sqb = h1
    prev_stats_done = None
    prev_apply_done = None
    prev_mm_done = None
    done = None
    mean = st.lnscr[0:1, 1104:1616] if False else None
    for nt in range(NT):
        c0 = nt * 512
        mean = st.lnscr[0:1, 512:1024]
        mean = ln2scr[0:1, :]
        rstd = var2[0:1, :]
        w = [s_ev]
        if prev_mm_done:
            w.append(prev_mm_done)

        def conv(e, c0=c0):
            for c in range(CH):
                sc = low0[:, c * T + c0: c * T + c0 + 512]
                r = e.activation(sqb[:, c * 512:(c + 1) * 512], sc, AF.Square)
            return r
        conv_ev = mk.op("scalar", conv, waits=w, inc=sa)
        w = [conv_ev]
        if prev_stats_done:
            w.append(prev_stats_done)

        def mmstat(e, c0=c0):
            for c in range(CH):
                e.matmul(st.pb[4][0:1, :], st.ones_b,
                         low0[:, c * T + c0: c * T + c0 + 512],
                         start=(c == 0), stop=(c == CH - 1), skip_group_check=True)
            for c in range(CH):
                r = e.matmul(st.pb[5][0:1, :], st.ones_b, sqb[:, c * 512:(c + 1) * 512],
                             start=(c == 0), stop=(c == CH - 1), skip_group_check=True)
            return r
        prev_mm_done = mk.op("tensor", mmstat, waits=w, inc=sp)

        def smath(e, mean=mean, rstd=rstd):
            e.tensor_scalar(mean, st.pb[4][0:1, :], 1.0 / D, None, ALU.mult)
            e.drain()
            e.tensor_tensor(rstd, mean, mean, ALU.mult)
            e.drain()
            e.scalar_tensor_tensor(rstd, st.pb[5][0:1, :], 1.0 / D, rstd,
                                   ALU.mult, ALU.subtract)
            e.drain()
            return e.tensor_scalar(rstd, rstd, LN_EPS, None, ALU.add)
        ev = mk.op("vector", smath, waits=[prev_mm_done], inc=sv)
        prev_stats_done = ev
        ev = mk.op("scalar", lambda e, rstd=rstd: e.activation(rstd, rstd, AF.Sqrt),
                   waits=[ev], inc=sa)
        ev = mk.op("vector", lambda e, rstd=rstd: e.reciprocal(rstd, rstd),
                   waits=[ev], inc=sv)
        w = [ev]
        if prev_apply_done:
            w.append(prev_apply_done)

        def bcast(e, mean=mean, rstd=rstd):
            e.matmul(st.pb[0][:, :], st.ones_f, mean, start=True, stop=True,
                     skip_group_check=True)
            return e.matmul(st.pb[1][:, :], st.ones_f, rstd, start=True, stop=True,
                            skip_group_check=True)
        ev = mk.op("tensor", bcast, waits=w, inc=sp)

        def apply(e, c0=c0):
            tf = st.lnscr[:, 0:512]
            for c in range(CH):
                sc = low0[:, c * T + c0: c * T + c0 + 512]
                xc = st.x[:, c * T + c0: c * T + c0 + 512]
                e.tensor_tensor(tf, sc, st.pb[0][:, :], ALU.subtract)
                e.drain()
                e.tensor_tensor(tf, tf, st.pb[1][:, :], ALU.mult)
                e.drain()
                e.tensor_scalar(tf, tf, gb[:, c:c + 1], gb[:, 8 + c:9 + c],
                                ALU.mult, ALU.add)
                e.drain()
                e.tensor_tensor(xc, xc, tf, ALU.add)
                e.drain()
            return None
        done = mk.op("vector", apply, waits=[ev], inc=sv)
        prev_apply_done = done
    st.A.pop(); st.B.pop(); st.Cf.pop()
    return done






GRID0 = -1.2
GRIDS = 1.4 / 7.0


def emit_f2(mk, st, xln_ev, wd, sems):
    sv, sa, sp = sems["v"], sems["a"], sems["p"]
    st.A.push(); st.B.push(); st.Cf.push(); st.W.push()
    gates = st.A.alloc(2048)
    gbs = st.A.alloc(4096)
    acc = st.A.alloc(4096)
    un = st.A.alloc(4096)
    wslot = [st.W.alloc(2048) for _ in range(4)]
    ws = WStream(mk, wslot, 2048)
    sc0 = st.Cf.alloc(32)
    sc1 = st.Cf.alloc(64)
    b0h = st.Cf.alloc(32)
    b1h = st.Cf.alloc(64)
    b0kc = st.Cf.alloc(64)
    b1kc = st.Cf.alloc(32)
    b1c = st.Cf.alloc(128)
    g0cc = st.Cf.alloc(32)
    g1cc = st.Cf.alloc(64)
    gneg = st.Cf.alloc(8)
    bc0 = st.Cf.alloc(64)
    bc1 = st.Cf.alloc(128)
    r_mu = st.Cf.alloc(512)
    r_rs = st.Cf.alloc(512)
    mrow = r_mu[0:1, :]
    rrow = r_rs[0:1, :]
    m1 = st.lnscr[:, 1088:1089]
    m2 = st.lnscr[:, 1089:1090]
    dn = st.lnscr[:, 1090:1091]
    p8 = st.lnscr[:, 1096:1104]
    msk = st.lnscr[:, 1104:1112]
    gtm = st.lnscr[:, 1112:1120]
    swn = st.Cf.alloc(1)
    gwh = st.f2s[:, 2624:2688]
    gwl = st.f2s[:, 2688:2752]
    xh = un
    xl = acc
    sel4 = st.f2s[:, 64:576]
    sel4k = st.f2s[:, 576:1088]
    b2r_sb = st.f2s[:, 1088:2112]
    tmpk = st.f2s[:, 2112:2624]

    ld0 = mk.sem()
    gw_ = [xln_ev]
    for (dst, key) in ((b1c, "b1"), (g0cc, "g0c"), (b0h, "b0c"),
                       (g1cc, "g1c"), (b1h, "b1cc"), (b0kc, "b0k"),
                       (b1kc, "b1k")):
        mk.op("gpsimd", lambda e, dst=dst, key=key: e.dma_start(
            out=dst[:, :], in_=wd[key][:, :]), waits=gw_, inc=ld0, dma=True)
    mk.op("gpsimd", lambda e: e.dma_start(out=b2r_sb[0:4, :], in_=wd["b2r"][:, :]),
          waits=gw_, inc=ld0, dma=True)
    gw_f = bc1[:, 0:64]
    mk.op("gpsimd", lambda e: e.dma_start(
        out=gw_f.rearrange("p (c e) -> p c e", e=8),
        in_=wd["gw"].rearrange("(c p) e -> p c e", p=128)), waits=gw_, inc=ld0,
        dma=True)
    gp2 = mk.sem()
    ev_g = mk.op("gpsimd", lambda e: e.iota(gneg, [[1, 8]], channel_multiplier=0,
                                            allow_small_or_imprecise_dtypes=True),
                 waits=[st.init_ev, xln_ev], inc=gp2)

    def mkconsts(e):
        e.tensor_scalar(gneg, gneg, -0.5 * GRIDS, -0.5 * GRID0, ALU.mult, ALU.add)
        e.drain()
        e.tensor_scalar(sc0, g0cc, 0.5, None, ALU.mult)
        e.drain()
        e.tensor_scalar(sc1, g1cc, 0.5, None, ALU.mult)
        e.drain()
        e.tensor_scalar(b0h, b0h, 0.5, None, ALU.mult)
        e.drain()
        e.tensor_scalar(b1h, b1h, 0.5, None, ALU.mult)
        e.drain()
        return None
    cst_ev = mk.op("vector", mkconsts, waits=[ld0.ev, ev_g], inc=sv)
    whl_ev = mk.op("scalar", lambda e: e.activation(gwh, gw_f, AF.Copy),
                   waits=[ld0.ev], inc=sa)
    whl_ev = mk.op("vector", lambda e: e.tensor_tensor(gwl, gw_f, gwh, ALU.subtract),
                   waits=[whl_ev], inc=sv)

    def swmm(e):
        for c in range(CH):
            e.matmul(st.pb[3][0:8, 0:1], gwh[:, c * 8:(c + 1) * 8], st.ones_b,
                     start=(c == 0), stop=False, skip_group_check=True)
        for c in range(CH):
            r = e.matmul(st.pb[3][0:8, 0:1], gwl[:, c * 8:(c + 1) * 8], st.ones_b,
                         start=False, stop=(c == CH - 1), skip_group_check=True)
        return r
    swev = mk.op("tensor", swmm, waits=[whl_ev, st.init_ev], inc=sp)
    swn_ev = mk.op("vector", lambda e: e.tensor_scalar(
        swn[0:8, :], st.pb[3][0:8, 0:1], -1.0, None, ALU.mult), waits=[swev], inc=sv)

    def mksel(e):
        e.memset(sel4, 1.0)
        e.drain()
        for ex in range(4):
            e.affine_select(sel4[:, ex * 128:(ex + 1) * 128],
                            sel4[:, ex * 128:(ex + 1) * 128],
                            [[0, 128]], ALU.is_equal, 0.0, base=-ex,
                            channel_multiplier=1)
            e.drain()
        e.memset(sel4k, 1.0)
        e.drain()
        for ex in range(4):
            e.affine_select(sel4k[:, ex * 128:(ex + 1) * 128],
                            sel4k[:, ex * 128:(ex + 1) * 128],
                            [[0, 128]], ALU.is_equal, 0.0, base=-(32 + ex),
                            channel_multiplier=1)
            e.drain()
        return None
    sel_ev = mk.op("gpsimd", mksel, waits=[st.init_ev] + gw_, inc=gp2)

    gates_ev = None
    prev_pb3 = None
    prev_split = None
    for nt in range(NT):
        c0r = nt * 512

        def xsplit(e, c0r=c0r):
            for c in range(CH):
                r = e.activation(xh[:, c * 512:(c + 1) * 512],
                                 st.x[:, c * T + c0r: c * T + c0r + 512], AF.Copy)
            return r
        w = [xln_ev]
        if prev_split:
            w.append(prev_split)
        ev_h = mk.op("scalar", xsplit, waits=w, inc=sa)

        def xsplit2(e, c0r=c0r):
            for c in range(CH):
                e.tensor_tensor(xl[:, c * 512:(c + 1) * 512],
                                st.x[:, c * T + c0r: c * T + c0r + 512],
                                xh[:, c * 512:(c + 1) * 512], ALU.subtract)
                e.drain()
            return None
        ev_l = mk.op("vector", xsplit2, waits=[ev_h], inc=sv)

        def lgmm(e):
            first = True
            for (wop, xop) in ((gwh, xh), (gwl, xh), (gwh, xl)):
                for c in range(CH):
                    r = e.matmul(st.pb[2][0:8, :], wop[:, c * 8:(c + 1) * 8],
                                 xop[:, c * 512:(c + 1) * 512],
                                 start=first,
                                 stop=(wop is gwh and xop is xl and c == CH - 1),
                                 skip_group_check=True)
                    first = False
            return r
        lev = mk.op("tensor", lgmm, waits=[ev_l, swn_ev], inc=sp)
        prev_split = lev
        lg_sb = r_mu
        eev = mk.op("vector", lambda e: e.tensor_copy(lg_sb[0:8, :], st.pb[2][0:8, :]),
                    waits=[lev, cst_ev], inc=sv)

        def lbc(e, nt=nt):
            e.matmul(st.pb[0][0:8, :], st.ones_f[0:1, 0:8], st.mean_nt(nt),
                     start=True, stop=True, skip_group_check=True)
            return e.matmul(st.pb[1][0:8, :], st.ones_f32[0:1, 0:8], st.rstd_nt(nt),
                            start=True, stop=True, skip_group_check=True)
        bev2 = mk.op("tensor", lbc, waits=[eev], inc=sp)

        def lcorr(e):
            e.scalar_tensor_tensor(lg_sb[0:8, :], st.pb[0][0:8, :], swn[0:8, :],
                                   lg_sb[0:8, :], ALU.mult, ALU.add)
            e.drain()
            e.tensor_tensor(lg_sb[0:8, :], lg_sb[0:8, :], st.pb[1][0:8, :], ALU.mult)
            e.drain()
            return None
        eev = mk.op("vector", lcorr, waits=[bev2, swn_ev], inc=sv)
        w = [eev]
        if prev_pb3:
            w.append(prev_pb3)

        def ltr2(e):
            for gq in range(4):
                r = e.transpose(st.pb[3][:, gq * 8:(gq + 1) * 8],
                                lg_sb[0:8, gq * 128:(gq + 1) * 128],
                                st.ident_f[0:8, 0:8])
            return r
        ev = mk.op("tensor", ltr2, waits=w, inc=sp)
        for gq in range(4):
            tile = st.pb[3][:, gq * 8:(gq + 1) * 8]

            def gtop2(e, tile=tile):
                e.tensor_reduce(m1, tile, AX.X, ALU.max)
                e.drain()
                e.tensor_scalar(msk, tile, m1, None, ALU.is_equal)
                e.drain()
                e.scalar_tensor_tensor(msk, msk, -1e30, tile, ALU.mult, ALU.add)
                e.drain()
                e.tensor_reduce(m2, msk, AX.X, ALU.max)
                e.drain()
                e.tensor_scalar(dn, m2, m1, None, ALU.subtract)
                e.drain()
                return None
            ev = mk.op("vector", gtop2, waits=[ev], inc=sv)
            ev = mk.op("scalar", lambda e: e.activation(dn, dn, AF.Exp),
                       waits=[ev], inc=sa)

            def gm1n(e):
                e.tensor_scalar(dn, dn, 1.0, None, ALU.add)
                e.drain()
                e.reciprocal(dn, dn)
                e.drain()
                e.tensor_scalar(m1, m1, -1.0, None, ALU.mult)
                e.drain()
                return None
            ev = mk.op("vector", gm1n, waits=[ev], inc=sv)
            ev = mk.op("scalar", lambda e, tile=tile: e.activation(
                p8, tile, AF.Exp, bias=m1), waits=[ev], inc=sa)

            def ggate(e, tile=tile):
                e.tensor_scalar(msk, tile, m2, None, ALU.is_ge)
                e.drain()
                e.tensor_tensor(msk, msk, p8, ALU.mult)
                e.drain()
                e.tensor_scalar(gtm, msk, dn, None, ALU.mult)
                e.drain()
                return None
            ev = mk.op("vector", ggate, waits=[ev], inc=sv)

            def gtr(e):
                e.transpose(st.pb[4][0:4, 0:128], gtm[:, 0:4], st.ident_f)
                return e.transpose(st.pb[5][0:4, 0:128], gtm[:, 4:8], st.ident_f)
            ev = mk.op("tensor", gtr, waits=[ev], inc=sp)

            def gev(e, gq=gq, nt=nt):
                e.tensor_copy(gates[0:4, nt * 512 + gq * 128: nt * 512 + (gq + 1) * 128],
                              st.pb[4][0:4, 0:4].rearrange("p n -> p n") if False
                              else st.pb[4][0:4, 0:4])
                e.drain()
                return None
            def gev2(e, gq=gq, nt=nt):
                e.tensor_copy(gates[0:4, nt * 512 + gq * 128: nt * 512 + (gq + 1) * 128],
                              st.pb[4][0:4, 0:128])
                e.drain()
                e.tensor_copy(gates[32:36, nt * 512 + gq * 128: nt * 512 + (gq + 1) * 128],
                              st.pb[5][0:4, 0:128])
                e.drain()
                return None
            ev = mk.op("vector", gev2, waits=[ev], inc=sv)
        prev_pb3 = ev
        gates_ev = ev

    done = None
    banks_free = None
    for nt in range(NT):
        c0 = nt * 512
        ev_gb = gates_ev
        for ex in range(8):
            base = 0 if ex < 4 else 32
            e2 = ex % 4
            w = [ev_gb, sel_ev]
            if banks_free:
                w.append(banks_free)
            ev_mm = mk.op("tensor", lambda e, e2=e2, base=base, nt=nt: e.matmul(
                st.pb[6][:, :],
                (sel4[0:4, e2 * 128:(e2 + 1) * 128] if base == 0
                 else sel4k[32:36, e2 * 128:(e2 + 1) * 128]),
                gates[base:base + 4, nt * 512:(nt + 1) * 512],
                start=True, stop=True, skip_group_check=True), waits=w, inc=sp)
            ev_gb = mk.op("vector", lambda e, ex=ex: e.tensor_copy(
                gbs[:, ex * 512:(ex + 1) * 512], st.pb[6][:, :]),
                waits=[ev_mm], inc=sv)
        gbs_ev = ev_gb

        h_sb = st.B.t[:, 0:16384]
        acc_ev = None
        prev_w2 = None
        for e4 in range(4):
            h_evs = []
            for og in range(4):
                toks = []
                for obh in range(4):
                    r0 = (e4 * 32 + og * 8 + obh * 2) * 128
                    toks.append(ws.load(lambda s_, r0=r0: (
                        s_.rearrange("p (b n) -> p b n", n=1024),
                        wd["w1t"][r0:r0 + 256, :].rearrange("(b p) n -> p b n", p=128)),
                        extra_waits=[xln_ev]))
                w = [gbs_ev]
                if og == 0 and prev_w2:
                    w.append(prev_w2)
                if h_evs:
                    w.append(h_evs[-1])

                def hmm(e, toks=toks, c0=c0):
                    r = None
                    for obh in range(4):
                        slot, sem, val = toks[obh]
                        e.wait_ge(sem.h, val)
                        for o2 in range(2):
                            bank = obh * 2 + o2
                            for c in range(CH):
                                r = e.matmul(
                                    st.pb[bank][:, :],
                                    wslot[slot][:, o2 * 1024 + c * 128:
                                                o2 * 1024 + (c + 1) * 128],
                                    st.xln[:, c * T + c0: c * T + c0 + 512],
                                    start=(c == 0), stop=(c == CH - 1),
                                    skip_group_check=True)
                    return r
                hme = mk.op("tensor", hmm, waits=w, inc=sp)
                for t_ in toks:
                    ws.consumed(t_[0], hme)

                def hevac(e, og=og, e4=e4):
                    for b8 in range(8):
                        ob = og * 8 + b8
                        e.activation(h_sb[:, ob * 512:(ob + 1) * 512],
                                     st.pb[b8][:, :], AF.Gelu,
                                     bias=b1c[:, e4 * 32 + ob: e4 * 32 + ob + 1])
                        e.drain()
                    return None
                hev = mk.op("scalar", hevac, waits=[hme, cst_ev], inc=sa)

                def hgate(e, og=og, e4=e4):
                    for b8 in range(8):
                        ob = og * 8 + b8
                        e.tensor_tensor(h_sb[:, ob * 512:(ob + 1) * 512],
                                        h_sb[:, ob * 512:(ob + 1) * 512],
                                        gbs[:, e4 * 512:(e4 + 1) * 512], ALU.mult)
                        e.drain()
                    return None
                hev = mk.op("vector", hgate, waits=[hev], inc=sv)
                h_evs.append(hev)
            h_done = h_evs[-1]
            w2e = None
            for obq in range(4):
                toks2 = []
                for o2 in range(2):
                    ob = obq * 2 + o2
                    r0 = (e4 * 8 + ob) * 128
                    for half in range(2):
                        toks2.append(ws.load(lambda s_, r0=r0, half=half: (
                            s_, wd["w2t"][r0:r0 + 128, half * 2048:(half + 1) * 2048]),
                            extra_waits=[h_done]))

                def w2mm(e, toks2=toks2, e4=e4, nt=nt, obq=obq):
                    r = None
                    for o2 in range(2):
                        ob = obq * 2 + o2
                        for half in range(2):
                            slot, sem, val = toks2[o2 * 2 + half]
                            e.wait_ge(sem.h, val)
                            for fc in range(16):
                                fcg = half * 16 + fc
                                last = (half == 1 and fc == 15 and e4 != 0)
                                r = e.matmul(st.pb[ob][:, :],
                                             wslot[slot][:, fc * 128:(fc + 1) * 128],
                                             h_sb[:, fcg * 512:(fcg + 1) * 512],
                                             start=(half == 0 and fc == 0), stop=last,
                                             skip_group_check=True)
                        if e4 == 0:
                            r = e.matmul(st.pb[ob][:, :],
                                         b2r_sb[0:4, ob * 128:(ob + 1) * 128],
                                         gates[0:4, nt * 512:(nt + 1) * 512],
                                         start=False, stop=True,
                                         skip_group_check=True)
                    return r
                w = [h_done]
                if w2e:
                    w.append(w2e)
                w2e = mk.op("tensor", w2mm, waits=w, inc=sp)
                for t_ in toks2:
                    ws.consumed(t_[0], w2e)
            prev_w2 = w2e

            def accup(e, e4=e4):
                for ob in range(8):
                    if e4 == 0:
                        e.tensor_copy(acc[:, ob * 512:(ob + 1) * 512], st.pb[ob][:, :])
                    else:
                        e.tensor_tensor(acc[:, ob * 512:(ob + 1) * 512],
                                        acc[:, ob * 512:(ob + 1) * 512],
                                        st.pb[ob][:, :], ALU.add)
                    e.drain()
                return None
            acc_ev = mk.op("vector", accup, waits=[w2e] + ([acc_ev] if acc_ev else []),
                           inc=sv)
            prev_w2 = acc_ev

        sqr = st.B.t[:, 0:4096]

        def unsq(e, c0=c0):
            for c in range(CH):
                r = e.activation(sqr[:, c * 512:(c + 1) * 512],
                                 st.xln[:, c * T + c0: c * T + c0 + 512], AF.Square)
            return r
        uev = mk.op("scalar", unsq, waits=[acc_ev], inc=sa)

        def unstat(e, c0=c0):
            for c in range(CH):
                e.matmul(st.pb[4][0:1, :], st.ones_b,
                         st.xln[:, c * T + c0: c * T + c0 + 512],
                         start=(c == 0), stop=(c == CH - 1), skip_group_check=True)
            for c in range(CH):
                r = e.matmul(st.pb[5][0:1, :], st.ones_b, sqr[:, c * 512:(c + 1) * 512],
                             start=(c == 0), stop=(c == CH - 1), skip_group_check=True)
            return r
        uev = mk.op("tensor", unstat, waits=[uev, acc_ev], inc=sp)

        def unsmath(e):
            e.tensor_scalar(mrow, st.pb[4][0:1, :], 1.0 / D, None, ALU.mult)
            e.drain()
            e.tensor_tensor(rrow, mrow, mrow, ALU.mult)
            e.drain()
            e.scalar_tensor_tensor(rrow, st.pb[5][0:1, :], 1.0 / D, rrow,
                                   ALU.mult, ALU.subtract)
            e.drain()
            e.tensor_scalar(rrow, rrow, LN_EPS, None, ALU.add)
            e.drain()
            return None
        uev = mk.op("vector", unsmath, waits=[uev], inc=sv)
        uev = mk.op("scalar", lambda e: e.activation(rrow, rrow, AF.Sqrt),
                    waits=[uev], inc=sa)
        uev = mk.op("vector", lambda e: e.reciprocal(rrow, rrow), waits=[uev], inc=sv)

        def unbc(e):
            e.matmul(st.pb[0][:, :], st.ones_f, mrow, start=True, stop=True,
                     skip_group_check=True)
            return e.matmul(st.pb[1][:, :], st.ones_f, rrow, start=True, stop=True,
                            skip_group_check=True)
        uev = mk.op("tensor", unbc, waits=[uev], inc=sp)

        def unapply(e, c0=c0):
            for c in range(CH):
                uc = un[:, c * 512:(c + 1) * 512]
                e.tensor_tensor(uc, st.xln[:, c * T + c0: c * T + c0 + 512],
                                st.pb[0][:, :], ALU.subtract)
                e.drain()
                e.tensor_tensor(uc, uc, st.pb[1][:, :], ALU.mult)
                e.drain()
            return None
        un_ev = mk.op("vector", unapply, waits=[uev], inc=sv)

        for e2 in range(4):
            def mkbias(e, e2=e2):
                for c in range(CH):
                    e.tensor_scalar(bc0[:, c * 8:(c + 1) * 8], gneg,
                                    b0h[:, e2 * 8 + c: e2 * 8 + c + 1], None, ALU.add)
                    e.drain()
                for fc in range(16):
                    e.tensor_scalar(bc1[:, fc * 8:(fc + 1) * 8], gneg,
                                    b1h[:, e2 * 16 + fc: e2 * 16 + fc + 1], None,
                                    ALU.add)
                    e.drain()
                return None
            bias_ev = mk.op("vector", mkbias, waits=[un_ev], inc=sv)

            sqslot = [st.B.t[:, 0:4096], st.B.t[:, 4096:8192]]
            zn = st.B.t[:, 8192:16384]
            sq_free = [None, None]
            z_evs = []
            for mg in range(2):
                mm_last = None
                for g in range(8):
                    sl = sqslot[g % 2]
                    w = [bias_ev]
                    if sq_free[g % 2]:
                        w.append(sq_free[g % 2])

                    def mktanh(e, sl=sl, g=g, e2=e2):
                        for c in range(CH):
                            e.activation(sl[:, c * 512:(c + 1) * 512],
                                         un[:, c * 512:(c + 1) * 512], AF.Tanh,
                                         bias=bc0[:, c * 8 + g: c * 8 + g + 1],
                                         scale=sc0[:, e2 * 8 + c: e2 * 8 + c + 1])
                            e.drain()
                        return None
                    tev = mk.op("scalar", mktanh, waits=w, inc=sa)

                    def mkbasis(e, sl=sl):
                        for c in range(CH):
                            s_ = sl[:, c * 512:(c + 1) * 512]
                            e.tensor_tensor(s_, s_, s_, ALU.mult)
                            e.drain()
                            e.tensor_scalar(s_, s_, -1.0, 1.0, ALU.mult, ALU.add)
                            e.drain()
                        return None
                    bev = mk.op("vector", mkbasis, waits=[tev], inc=sv)
                    toks = []
                    for obh in range(4):
                        r0 = ((e2 * 8 + g) * 16 + mg * 8 + obh * 2) * 128
                        toks.append(ws.load(lambda s_, r0=r0: (
                            s_.rearrange("p (b n) -> p b n", n=1024),
                            wd["w0t"][r0:r0 + 256, :].rearrange("(b p) n -> p b n",
                                                               p=128)),
                            extra_waits=[bev]))

                    def zmm(e, toks=toks, sl=sl, g=g):
                        r = None
                        for obh in range(4):
                            slot, sem, val = toks[obh]
                            e.wait_ge(sem.h, val)
                            for o2 in range(2):
                                bank = obh * 2 + o2
                                for hc in range(CH):
                                    r = e.matmul(
                                        st.pb[bank][:, :],
                                        wslot[slot][:, o2 * 1024 + hc * 128:
                                                    o2 * 1024 + (hc + 1) * 128],
                                        sl[:, hc * 512:(hc + 1) * 512],
                                        start=(g == 0 and hc == 0),
                                        stop=(g == 7 and hc == CH - 1),
                                        skip_group_check=True)
                        return r
                    w = [bev]
                    if mm_last:
                        w.append(mm_last)
                    if mg == 1 and g == 0:
                        w.append(z_evs[0])
                    mm_last = mk.op("tensor", zmm, waits=w, inc=sp)
                    for t_ in toks:
                        ws.consumed(t_[0], mm_last)
                    sq_free[g % 2] = mm_last

                def zevac(e, mg=mg, e2=e2):
                    for b8 in range(8):
                        ob = mg * 8 + b8
                        e.tensor_scalar(zn[:, ob * 512:(ob + 1) * 512],
                                        st.pb[b8][:, :],
                                        b0kc[:, e2 * 16 + ob: e2 * 16 + ob + 1],
                                        None, ALU.add)
                        e.drain()
                    return None
                zev = mk.op("vector", zevac, waits=[mm_last], inc=sv)
                z_evs.append(zev)
            z_done = z_evs[-1]

            def zsq(e):
                for zc in range(16):
                    r = e.activation(sqslot[zc % 2][:, (zc // 2 % 4) * 512:
                                                    (zc // 2 % 4) * 512 + 512],
                                     zn[:, zc * 512:(zc + 1) * 512], AF.Square) \
                        if False else None
                for zc in range(16):
                    dstc = st.B.t[:, (zc % 8) * 512:(zc % 8) * 512 + 512]
                    r = e.activation(dstc, zn[:, zc * 512:(zc + 1) * 512], AF.Square)
                    if zc == 7:
                        e.drain()
                return r
            zstat_ev = None

            def zhalf(e, h0):
                for zc in range(8):
                    dstc = st.B.t[:, zc * 512:(zc + 1) * 512]
                    e.activation(dstc, zn[:, (h0 + zc) * 512:(h0 + zc + 1) * 512],
                                 AF.Square)
                    e.drain()
                return None
            sev = mk.op("scalar", lambda e: zhalf(e, 0), waits=[z_done], inc=sa)

            def zst1(e):
                for zc in range(8):
                    e.matmul(st.pb[4][0:1, :], st.ones_b,
                             zn[:, zc * 512:(zc + 1) * 512],
                             start=(zc == 0), stop=False, skip_group_check=True)
                for zc in range(8):
                    r = e.matmul(st.pb[5][0:1, :], st.ones_b,
                                 st.B.t[:, zc * 512:(zc + 1) * 512],
                                 start=(zc == 0), stop=False, skip_group_check=True)
                return r
            sev = mk.op("tensor", zst1, waits=[sev], inc=sp)
            sev2 = mk.op("scalar", lambda e: zhalf(e, 8), waits=[sev], inc=sa)

            def zst2(e):
                for zc in range(8):
                    e.matmul(st.pb[4][0:1, :], st.ones_b,
                             zn[:, (8 + zc) * 512:(9 + zc) * 512],
                             start=False, stop=(zc == 7), skip_group_check=True)
                for zc in range(8):
                    r = e.matmul(st.pb[5][0:1, :], st.ones_b,
                                 st.B.t[:, zc * 512:(zc + 1) * 512],
                                 start=False, stop=(zc == 7), skip_group_check=True)
                return r
            sev = mk.op("tensor", zst2, waits=[sev2], inc=sp)

            def zsmath(e):
                e.tensor_scalar(mrow, st.pb[4][0:1, :], 1.0 / 2048.0, None, ALU.mult)
                e.drain()
                e.tensor_tensor(rrow, mrow, mrow, ALU.mult)
                e.drain()
                e.scalar_tensor_tensor(rrow, st.pb[5][0:1, :], 1.0 / 2048.0, rrow,
                                       ALU.mult, ALU.subtract)
                e.drain()
                e.tensor_scalar(rrow, rrow, LN_EPS, None, ALU.add)
                e.drain()
                return None
            sev = mk.op("vector", zsmath, waits=[sev], inc=sv)
            sev = mk.op("scalar", lambda e: e.activation(rrow, rrow, AF.Sqrt),
                        waits=[sev], inc=sa)
            sev = mk.op("vector", lambda e: e.reciprocal(rrow, rrow),
                        waits=[sev], inc=sv)

            def zbc(e):
                e.matmul(st.pb[0][:, :], st.ones_f, mrow, start=True, stop=True,
                         skip_group_check=True)
                return e.matmul(st.pb[1][:, :], st.ones_f, rrow, start=True,
                                stop=True, skip_group_check=True)
            sev = mk.op("tensor", zbc, waits=[sev], inc=sp)

            def znorm(e):
                for zc in range(16):
                    zc_ = zn[:, zc * 512:(zc + 1) * 512]
                    e.tensor_tensor(zc_, zc_, st.pb[0][:, :], ALU.subtract)
                    e.drain()
                    e.tensor_tensor(zc_, zc_, st.pb[1][:, :], ALU.mult)
                    e.drain()
                return None
            zn_ev = mk.op("vector", znorm, waits=[sev], inc=sv)

            mm_last = None
            sqh_free = [None, None]
            for g in range(8):
                half_evs = []
                for hh in range(2):
                    sl = sqslot[hh]
                    w = [zn_ev]
                    if sqh_free[hh]:
                        w.append(sqh_free[hh])

                    def ktanh(e, sl=sl, g=g, e2=e2, hh=hh):
                        for fc8 in range(8):
                            fc = hh * 8 + fc8
                            e.activation(sl[:, fc8 * 512:(fc8 + 1) * 512],
                                         zn[:, fc * 512:(fc + 1) * 512], AF.Tanh,
                                         bias=bc1[:, fc * 8 + g: fc * 8 + g + 1],
                                         scale=sc1[:, e2 * 16 + fc: e2 * 16 + fc + 1])
                            e.drain()
                        return None
                    tev = mk.op("scalar", ktanh, waits=w, inc=sa)

                    def kbasis(e, sl=sl):
                        for fc8 in range(8):
                            s_ = sl[:, fc8 * 512:(fc8 + 1) * 512]
                            e.tensor_tensor(s_, s_, s_, ALU.mult)
                            e.drain()
                            e.tensor_scalar(s_, s_, -1.0, 1.0, ALU.mult, ALU.add)
                            e.drain()
                        return None
                    half_evs.append(mk.op("vector", kbasis, waits=[tev], inc=sv))
                for ob in range(8):
                    r0 = ((e2 * 8 + g) * 8 + ob) * 128
                    tok = ws.load(lambda s_, r0=r0: (
                        s_, wd["w1kt"][r0:r0 + 128, :]),
                        extra_waits=[half_evs[1]])

                    def kmm(e, tok=tok, ob=ob, g=g):
                        slot, sem, val = tok
                        e.wait_ge(sem.h, val)
                        r = None
                        for ic in range(16):
                            sl = sqslot[ic // 8]
                            r = e.matmul(st.pb[ob][:, :],
                                         wslot[slot][:, ic * 128:(ic + 1) * 128],
                                         sl[:, (ic % 8) * 512:(ic % 8 + 1) * 512],
                                         start=(g == 0 and ic == 0),
                                         stop=(g == 7 and ic == 15),
                                         skip_group_check=True)
                        return r
                    w = [half_evs[1]]
                    if mm_last:
                        w.append(mm_last)
                    mm_last = mk.op("tensor", kmm, waits=w, inc=sp)
                    ws.consumed(tok[0], mm_last)
                sqh_free[0] = mm_last
                sqh_free[1] = mm_last

            def kevac(e, e2=e2):
                for ob in range(8):
                    e.tensor_scalar(tmpk, st.pb[ob][:, :],
                                    b1kc[:, e2 * 8 + ob: e2 * 8 + ob + 1], None,
                                    ALU.add)
                    e.drain()
                    e.tensor_tensor(tmpk, tmpk, gbs[:, (4 + e2) * 512:(5 + e2) * 512],
                                    ALU.mult)
                    e.drain()
                    e.tensor_tensor(acc[:, ob * 512:(ob + 1) * 512],
                                    acc[:, ob * 512:(ob + 1) * 512], tmpk, ALU.add)
                    e.drain()
                return None
            acc_ev = mk.op("vector", kevac, waits=[mm_last], inc=sv)
            un_ev = acc_ev

        def xup(e, c0=c0):
            for ob in range(8):
                xc = st.x[:, ob * T + c0: ob * T + c0 + 512]
                e.scalar_tensor_tensor(xc, acc[:, ob * 512:(ob + 1) * 512], 0.5,
                                       xc, ALU.mult, ALU.add)
                e.drain()
            return None
        done = mk.op("vector", xup, waits=[acc_ev], inc=sv)
        banks_free = done

    st.A.pop(); st.B.pop(); st.Cf.pop(); st.W.pop()
    return done




WIN = 512
NW = 4


def emit_f3(mk, st, xln_ev, wd, sems, phases=4):
    sv, sa, sp = sems["v"], sems["a"], sems["p"]
    st.A.push(); st.B.push(); st.Cf.push(); st.W.push()
    y = st.A.alloc(16384)
    qw = st.B.t[:, 0:2048]
    kw = [st.B.t[:, 2048:4096], st.B.t[:, 4096:6144]]
    vT = [st.B.t[:, 6144:8192], st.B.t[:, 8192:10240]]
    tabs = st.B.t[:, 10240:12288]
    Ow = st.B.t[:, 12288:14336]
    hshw = st.B.t[:, 14336:16384]
    P_sb = st.W.alloc(1024)
    PT = st.W.alloc(1024)
    sqt = st.W.alloc(512)
    wslot = [st.W.alloc(1408) for _ in range(4)]
    ws = WStream(mk, wslot, 1408)
    Mb = st.f2s[:, 0:1408]
    Tm = st.f2s[:, 1408:2304]
    hsel = st.Cf.alloc(2)
    s2 = st.Cf.alloc(128)
    Rm = st.cbf[:, 320:448]
    ybnd = st.f2s[:, 2434:2446]
    lnb_c = st.Cf.alloc(1)
    r2 = st.Cf.alloc(512)
    ptmp = st.Cf.alloc(512)
    ptm2 = st.Cf.alloc(512)
    qs_c = st.Cf.alloc(2)
    ks_c = st.Cf.alloc(2)
    lng_c = st.Cf.alloc(16)
    lnb2_c = st.Cf.alloc(16)
    b1a_c = st.Cf.alloc(44)
    b1g_c = st.Cf.alloc(44)
    b2_c = st.Cf.alloc(16)
    stm = st.lnscr[:, 1088:1104]
    m1 = stm[:, 0:1]
    m1s = stm[:, 1:2]
    den = stm[:, 2:3]
    rec = stm[:, 3:4]
    smu = st.stat[0:1, 0:512]
    srs = st.stat[0:1, 512:1024]

    ld0 = mk.sem()
    for (dst, key) in ((qs_c, "qs"), (ks_c, "ks"), (lng_c, "lng"), (lnb2_c, "lnb"),
                       (b1a_c, "b1a"), (b1g_c, "b1g"), (b2_c, "b2c3")):
        mk.op("gpsimd", lambda e, dst=dst, key=key: e.dma_start(
            out=dst[:, :], in_=wd[key][:, :]), waits=[xln_ev], inc=ld0, dma=True)
    gpp = mk.sem()

    def mkmasks(e):
        e.memset(Mb, 0.0)
        e.drain()
        e.affine_select(Mb, Mb, [[1, 1408]], ALU.is_ge, -1e30, base=-384,
                        channel_multiplier=-1)
        e.drain()
        e.affine_select(Mb, Mb, [[-1, 1408]], ALU.is_ge, -1e30, base=896,
                        channel_multiplier=1)
        e.drain()
        e.memset(Tm, 0.0)
        e.drain()
        e.affine_select(Tm, Tm, [[-1, 896]], ALU.is_ge, -1e30, base=384,
                        channel_multiplier=1)
        e.drain()
        e.memset(hsel, 1.0)
        e.drain()
        e.affine_select(hsel, hsel, [[-128, 2]], ALU.is_ge, 0.0, base=64,
                        channel_multiplier=1)
        e.drain()
        e.affine_select(hsel, hsel, [[128, 2]], ALU.is_ge, 0.0, base=63,
                        channel_multiplier=-1)
        e.drain()
        e.memset(s2, 1.0)
        e.drain()
        e.affine_select(s2, s2, [[1, 128]], ALU.is_ge, 0.0, base=64,
                        channel_multiplier=-128)
        e.drain()
        e.affine_select(s2, s2, [[-1, 128]], ALU.is_ge, 0.0, base=63,
                        channel_multiplier=128)
        e.drain()
        return e.iota(lnb_c, [[0, 1]], channel_multiplier=1,
                      allow_small_or_imprecise_dtypes=True)
    gev = mk.op("gpsimd", mkmasks, waits=[st.init_ev, xln_ev], inc=gpp)

    def mklnb(e):
        e.tensor_scalar(ptmp[:, 0:1], lnb_c, 64.0, None, ALU.is_ge)
        e.drain()
        e.scalar_tensor_tensor(lnb_c, ptmp[:, 0:1], -64.0, lnb_c, ALU.mult, ALU.add)
        e.drain()
        e.tensor_scalar(ptmp[:, 0:1], lnb_c, 32.0, None, ALU.is_ge)
        e.drain()
        e.scalar_tensor_tensor(lnb_c, ptmp[:, 0:1], -32.0, lnb_c, ALU.mult, ALU.add)
        e.drain()
        e.tensor_scalar(lnb_c, lnb_c, 2.0 / 89.6, 25.6 / 89.6, ALU.mult, ALU.add)
        e.drain()
        return None
    lev = mk.op("vector", mklnb, waits=[gev], inc=sv)
    lev = mk.op("scalar", lambda e: e.activation(lnb_c, lnb_c, AF.Ln),
                waits=[lev], inc=sa)
    pa = Arena(st.Cf.t, 2048)
    pa.pos = 513
    perm_ev = emit_rot_perm(mk, st, Rm, "half64", sems, pa, ready=lev)
    consts_ev = perm_ev

    invf = st.lnscr[:, 1105:1106]
    pmr = st.lnscr[:, 1106:1107]
    gev2 = mk.op("gpsimd", lambda e: e.iota(invf, [[0, 1]], channel_multiplier=1,
                                            allow_small_or_imprecise_dtypes=True),
                 waits=[gev], inc=gpp)

    def mkinvf(e):
        e.tensor_scalar(pmr, invf, 64.0, None, ALU.is_ge)
        e.drain()
        e.scalar_tensor_tensor(pmr, pmr, -64.0, invf, ALU.mult, ALU.add)
        e.drain()
        e.tensor_scalar(ptm2[:, 0:1], pmr, 32.0, None, ALU.is_ge)
        e.drain()
        e.scalar_tensor_tensor(pmr, ptm2[:, 0:1], -32.0, pmr, ALU.mult, ALU.add)
        e.drain()
        return None
    iev = mk.op("vector", mkinvf, waits=[gev2], inc=sv)
    iev = mk.op("scalar", lambda e: e.activation(
        invf, pmr, AF.Exp, scale=-2.0 * float(np.log(10000.0)) / 64.0),
        waits=[iev], inc=sa)
    consts2_ev = iev

    def ycopy(e):
        for c in range(CH):
            e.tensor_copy(y[:, c * T:(c + 1) * T], st.xln[:, c * T:(c + 1) * T])
            e.drain()
        return None
    y_ev = mk.op("vector", ycopy, waits=[xln_ev, ld0.ev], inc=sv)

    done = None
    phase_n = 0
    for l in range(2):
        if phase_n >= phases:
            break
        def savebnd(e):
            for c4 in range(4):
                c = 4 + c4
                for wq in range(3):
                    e.tensor_copy(ybnd[:, c4 * 3 + wq: c4 * 3 + wq + 1],
                                  y[:, c * T + (wq + 1) * 512 - 1:
                                    c * T + (wq + 1) * 512])
                    e.drain()
            return None
        bnd_ev = mk.op("vector", savebnd, waits=[y_ev], inc=sv)
        prev_qkv_banks = None
        o_done = None
        for win in range(NW):
            t0 = win * WIN
            def mksh(e, t0=t0, win=win):
                for c4 in range(4):
                    c = 4 + c4
                    dst = hshw[:, c4 * 512:(c4 + 1) * 512]
                    if win == 0:
                        e.memset(dst[:, 0:1], 0.0)
                        e.drain()
                    else:
                        e.tensor_copy(dst[:, 0:1],
                                      ybnd[:, c4 * 3 + win - 1: c4 * 3 + win])
                        e.drain()
                    e.tensor_copy(dst[:, 1:512],
                                  y[:, c * T + t0: c * T + t0 + 511])
                    e.drain()
                return None
            w = [y_ev, bnd_ev]
            if o_done:
                w.append(o_done)
            sh_ev = mk.op("vector", mksh, waits=w, inc=sv)

            gevt = mk.op("gpsimd", lambda e: e.iota(
                ptmp, [[1, 512]], channel_multiplier=0,
                allow_small_or_imprecise_dtypes=True),
                waits=[sh_ev, consts_ev], inc=gpp)

            def tb1(e, t0=t0):
                e.tensor_scalar(ptm2, ptmp, float(t0), invf, ALU.add, ALU.mult)
                e.drain()
                e.tensor_scalar(ptmp, ptm2, 1.0 / (2 * PI), M23, ALU.mult, ALU.add)
                e.drain()
                e.tensor_scalar(ptmp, ptmp, M23, None, ALU.subtract)
                e.drain()
                e.scalar_tensor_tensor(ptmp, ptmp, -2 * PI, ptm2, ALU.mult, ALU.add)
                e.drain()
                e.tensor_scalar(ptmp, ptmp, PI, -PI, ALU.min, ALU.max)
                e.drain()
                return None
            ev = mk.op("vector", tb1, waits=[gevt, consts2_ev], inc=sv)
            ev = mk.op("scalar", lambda e: e.activation(
                tabs[:, 512:1024], ptmp, AF.Sin), waits=[ev], inc=sa)

            def tb2(e, t0=t0):
                e.tensor_scalar(ptmp, ptm2, 1.0 / (2 * PI), 0.25, ALU.mult, ALU.add)
                e.drain()
                e.tensor_scalar(ptmp, ptmp, M23, None, ALU.add)
                e.drain()
                e.tensor_scalar(ptmp, ptmp, M23, None, ALU.subtract)
                e.drain()
                e.scalar_tensor_tensor(ptmp, ptmp, -2 * PI, ptm2, ALU.mult, ALU.add)
                e.drain()
                e.tensor_scalar(ptmp, ptmp, PI / 2, None, ALU.add)
                e.drain()
                e.tensor_scalar(ptmp, ptmp, PI, -PI, ALU.min, ALU.max)
                e.drain()
                return None
            ev = mk.op("vector", tb2, waits=[ev], inc=sv)
            ev = mk.op("scalar", lambda e: e.activation(
                tabs[:, 0:512], ptmp, AF.Sin), waits=[ev], inc=sa)

            gevt2 = mk.op("gpsimd", lambda e: e.iota(
                ptmp, [[1, 512]], channel_multiplier=0,
                allow_small_or_imprecise_dtypes=True), waits=[ev], inc=gpp)

            def tb3(e, t0=t0):
                e.tensor_scalar(ptmp, ptmp, 1.0 / 256.0, (t0 - 1024.0) / 256.0,
                                ALU.mult, ALU.add)
                e.drain()
                e.tensor_scalar(ptm2, ptmp, lnb_c, None, ALU.mult)
                e.drain()
                return None
            ev = mk.op("vector", tb3, waits=[gevt2], inc=sv)
            ev = mk.op("scalar", lambda e: e.activation(ptmp, ptm2, AF.Exp),
                       waits=[ev], inc=sa)

            def tb4(e):
                e.tensor_tensor(tabs[:, 1024:1536], tabs[:, 0:512], ptmp, ALU.mult)
                e.drain()
                e.tensor_tensor(tabs[:, 1536:2048], tabs[:, 512:1024], ptmp, ALU.mult)
                e.drain()
                return None
            ev = mk.op("vector", tb4, waits=[ev], inc=sv)
            ev = mk.op("scalar", lambda e: e.activation(
                ptmp, ptm2, AF.Exp, scale=-1.0), waits=[ev], inc=sa)

            def tb5(e):
                e.tensor_tensor(tabs[:, 0:512], tabs[:, 0:512], ptmp, ALU.mult)
                e.drain()
                e.tensor_tensor(tabs[:, 512:1024], tabs[:, 512:1024], ptmp, ALU.mult)
                e.drain()
                return None
            tab_ev = mk.op("vector", tb5, waits=[ev], inc=sv)

            kcur = kw[win % 2]
            vTcur = vT[win % 2]
            proj_ev = {}
            for gi, blk0 in enumerate((0, 4, 8)):
                dsts = [qw, kcur, None][gi]
                evp = None
                for b4 in range(4):
                    blk = l * 12 + blk0 + b4
                    tok = ws.load(lambda s_, blk=blk: (
                        s_[:, 0:1024], wd["wqkvt"][blk * 128:(blk + 1) * 128, :]),
                        extra_waits=[sh_ev])
                    bank = st.pb[6 + (b4 % 2)]
                    w = [sh_ev]
                    if prev_qkv_banks:
                        w.append(prev_qkv_banks)

                    def pmm(e, tok=tok, t0=t0, bank=bank):
                        slot, sem, val = tok
                        e.wait_ge(sem.h, val)
                        for c in range(CH):
                            rhs = (y[:, c * T + t0: c * T + t0 + 512] if c < 4
                                   else hshw[:, (c - 4) * 512:(c - 3) * 512])
                            r = e.matmul(bank[:, :],
                                         wslot[slot][:, c * 128:(c + 1) * 128],
                                         rhs, start=(c == 0), stop=(c == CH - 1),
                                         skip_group_check=True)
                        return r
                    mev = mk.op("tensor", pmm, waits=w, inc=sp)
                    ws.consumed(tok[0], mev)
                    if gi < 2:
                        evp = mk.op("vector", lambda e, dsts=dsts, b4=b4, bank=bank:
                                    e.tensor_copy(dsts[:, b4 * 512:(b4 + 1) * 512],
                                                  bank[:, :]), waits=[mev], inc=sv)
                    else:
                        evp = mk.op("vector", lambda e, b4=b4, bank=bank:
                                    e.tensor_copy(P_sb[:, 0:512], bank[:, :]),
                                    waits=[mev] + ([evp] if evp else []), inc=sv)

                        def vtr(e, b4=b4):
                            pbf = st.pb[5][:, :].bitcast(BF16)
                            for j in range(4):
                                r = e.transpose(pbf[:, j * 128:(j + 1) * 128],
                                                P_sb[:, j * 128:(j + 1) * 128],
                                                st.ident_b)
                            return r
                        tev2 = mk.op("tensor", vtr, waits=[evp], inc=sp)

                        def vte(e, b4=b4, vTcur=vTcur):
                            pbf = st.pb[5][:, :].bitcast(BF16)
                            for j in range(4):
                                e.tensor_copy(
                                    vTcur[:, (2 * b4) * 256 + j * 64:
                                          (2 * b4) * 256 + (j + 1) * 64],
                                    pbf[0:128, j * 128: j * 128 + 64])
                                e.drain()
                                e.tensor_copy(
                                    vTcur[:, (2 * b4 + 1) * 256 + j * 64:
                                          (2 * b4 + 1) * 256 + (j + 1) * 64],
                                    pbf[0:128, j * 128 + 64: (j + 1) * 128])
                                e.drain()
                            return None
                        evp = mk.op("vector", vte, waits=[tev2], inc=sv)
                    prev_qkv_banks = evp
                proj_ev[gi] = evp

            def lnorm_rope(src, scol, tq, ev_in):
                evs = ev_in
                first = True
                for c4 in range(4):
                    sl = src[:, c4 * 512:(c4 + 1) * 512]

                    def sq(e, sl=sl):
                        return e.activation(ptm2, sl, AF.Square)
                    wsq = [evs] + ([tab_ev] if first else [])
                    first = False
                    ev1 = mk.op("scalar", sq, waits=wsq, inc=sa)

                    def nmm(e):
                        return e.matmul(st.pb[4][0:2, :], hsel[:, 0:2], ptm2,
                                        start=True, stop=True, skip_group_check=True)
                    ev1 = mk.op("tensor", nmm, waits=[ev1], inc=sp)

                    def nmath(e):
                        e.tensor_scalar(r2[0:2, :], st.pb[4][0:2, :], 1e-24, None,
                                        ALU.max)
                        e.drain()
                        return None
                    ev1 = mk.op("vector", nmath, waits=[ev1], inc=sv)
                    ev1 = mk.op("scalar", lambda e: e.activation(
                        r2[0:2, :], r2[0:2, :], AF.Sqrt), waits=[ev1], inc=sa)
                    ev1 = mk.op("vector", lambda e: e.reciprocal(
                        r2[0:2, :], r2[0:2, :]), waits=[ev1], inc=sv)

                    def nbc(e):
                        return e.matmul(st.pb[4][:, :], s2[0:2, :], r2[0:2, :],
                                        start=True, stop=True, skip_group_check=True)
                    ev1 = mk.op("tensor", nbc, waits=[ev1], inc=sp)

                    def napp(e, sl=sl, scol=scol):
                        e.scalar_tensor_tensor(sl, sl, scol, st.pb[4][:, :],
                                               ALU.mult, ALU.mult)
                        e.drain()
                        return None
                    ev1 = mk.op("vector", napp, waits=[ev1], inc=sv)

                    def rotmm(e, sl=sl):
                        return e.matmul(st.pb[6][:, :], Rm, sl, start=True,
                                        stop=True, skip_group_check=True)
                    ev1 = mk.op("tensor", rotmm, waits=[ev1, perm_ev], inc=sp)

                    def rope(e, sl=sl, tq=tq):
                        cc = tabs[:, 1024:1536] if tq else tabs[:, 0:512]
                        ss = tabs[:, 1536:2048] if tq else tabs[:, 512:1024]
                        e.tensor_tensor(sqt, sl, cc, ALU.mult)
                        e.drain()
                        e.tensor_tensor(P_sb[:, 0:512], st.pb[6][:, :], ss, ALU.mult)
                        e.drain()
                        e.tensor_tensor(sl, sqt, P_sb[:, 0:512], ALU.add)
                        e.drain()
                        return None
                    evs = mk.op("vector", rope, waits=[ev1, tab_ev], inc=sv)
                return evs
            q_ev = lnorm_rope(qw, qs_c[:, l:l + 1], True, proj_ev[0])
            k_ev = lnorm_rope(kcur, ks_c[:, l:l + 1], False, proj_ev[1])
            prev_qkv_banks = k_ev

            kprev = kw[(win + 1) % 2]
            vTprev = vT[(win + 1) % 2]
            exp_prev = None
            for h8 in range(8):
                c = h8 // 2
                hb = (h8 % 2) * 64
                for qt in range(4):
                    nk = 512 if win == 0 else 1024

                    def smm(e, c=c, hb=hb, qt=qt, win=win, kprev=kprev, kcur=kcur):
                        r = None
                        if win > 0:
                            r = e.matmul(st.pb[0][:, :],
                                         qw[c * 128 + hb if False else 0:0, 0:0]
                                         if False else
                                         qw[hb:hb + 64, c * 512 + qt * 128:
                                            c * 512 + (qt + 1) * 128],
                                         kprev[hb:hb + 64, c * 512:(c + 1) * 512],
                                         start=True, stop=True,
                                         skip_group_check=True)
                        r = e.matmul(st.pb[1][:, :],
                                     qw[hb:hb + 64, c * 512 + qt * 128:
                                        c * 512 + (qt + 1) * 128],
                                     kcur[hb:hb + 64, c * 512:(c + 1) * 512],
                                     start=True, stop=True, skip_group_check=True)
                        return r
                    w = [q_ev, k_ev]
                    if exp_prev:
                        w.append(exp_prev)
                    sev = mk.op("tensor", smm, waits=w, inc=sp)

                    def msk_max(e, qt=qt, win=win):
                        off = (3 - qt) * 128
                        if win == 0:
                            e.tensor_tensor(ptmp, st.pb[1][:, :],
                                            Tm[:, off:off + 512], ALU.add)
                            e.drain()
                            e.tensor_reduce(m1, ptmp, AX.X, ALU.max)
                            e.drain()
                        else:
                            e.tensor_tensor(ptmp, st.pb[0][:, :],
                                            Mb[:, off:off + 512], ALU.add)
                            e.drain()
                            e.tensor_tensor(ptm2, st.pb[1][:, :],
                                            Mb[:, off + 512:off + 1024], ALU.add)
                            e.drain()
                            e.tensor_reduce(m1, ptmp, AX.X, ALU.max)
                            e.drain()
                            e.tensor_reduce(m1s, ptm2, AX.X, ALU.max)
                            e.drain()
                            e.tensor_tensor(m1, m1, m1s, ALU.max)
                            e.drain()
                        e.tensor_scalar(m1s, m1, -8.0, None, ALU.mult)
                        e.drain()
                        return None
                    mev = mk.op("vector", msk_max, waits=[sev], inc=sv)

                    def expp(e, win=win):
                        if win == 0:
                            e.activation(P_sb[:, 512:1024], ptmp, AF.Exp, bias=m1s,
                                         scale=8.0, accum_out=den)
                            e.drain()
                        else:
                            e.activation(P_sb[:, 0:512], ptmp, AF.Exp, bias=m1s,
                                         scale=8.0, accum_out=den)
                            e.drain()
                            e.activation(P_sb[:, 512:1024], ptm2, AF.Exp, bias=m1s,
                                         scale=8.0, accum_out=rec)
                            e.drain()
                        return None
                    exp_prev = mk.op("scalar", expp, waits=[mev], inc=sa)

                    def norm(e, win=win):
                        if win > 0:
                            e.tensor_tensor(den, den, rec, ALU.add)
                            e.drain()
                        e.reciprocal(rec, den)
                        e.drain()
                        if win == 0:
                            e.tensor_scalar(P_sb[:, 512:1024], P_sb[:, 512:1024],
                                            rec, None, ALU.mult)
                        else:
                            e.tensor_scalar(P_sb[:, :], P_sb[:, :], rec, None,
                                            ALU.mult)
                        e.drain()
                        return None
                    nev = mk.op("vector", norm, waits=[exp_prev], inc=sv)

                    nch = 4 if win == 0 else 8
                    ch0 = 4 if win == 0 else 0

                    def ptr(e, nch=nch, ch0=ch0):
                        pbf = st.pb[5][:, :].bitcast(BF16)
                        for j in range(nch):
                            r = e.transpose(pbf[:, j * 128:(j + 1) * 128],
                                            P_sb[:, (ch0 + j) * 128:
                                                 (ch0 + j + 1) * 128],
                                            st.ident_b)
                        return r
                    tev3 = mk.op("tensor", ptr, waits=[nev], inc=sp)

                    def pte(e, nch=nch):
                        pbf = st.pb[5][:, :].bitcast(BF16)
                        for j in range(nch):
                            e.tensor_copy(PT[:, j * 128:(j + 1) * 128],
                                          pbf[:, j * 128:(j + 1) * 128])
                            e.drain()
                        return None
                    ptev = mk.op("vector", pte, waits=[tev3], inc=sv)

                    def pvm(e, h8=h8, nch=nch, win=win, vTprev=vTprev, vTcur=vTcur):
                        r = None
                        for j in range(nch):
                            if win == 0:
                                vt_ = vTcur
                                kc = j
                            else:
                                vt_ = vTprev if j < 4 else vTcur
                                kc = j % 4
                            r = e.matmul(st.pb[2][0:64, 0:128],
                                         vt_[:, h8 * 256 + kc * 64:
                                             h8 * 256 + (kc + 1) * 64],
                                         PT[:, j * 128:(j + 1) * 128],
                                         start=(j == 0), stop=(j == nch - 1),
                                         skip_group_check=True)
                        return r
                    w = [ptev, proj_ev[2]]
                    if o_done:
                        w.append(o_done)
                    pvev = mk.op("tensor", pvm, waits=w, inc=sp)
                    o_done = mk.op("vector", lambda e, hb=hb, c=c, qt=qt:
                                   e.tensor_copy(
                                       Ow[hb:hb + 64, c * 512 + qt * 128:
                                          c * 512 + (qt + 1) * 128],
                                       st.pb[2][0:64, 0:128]),
                                   waits=[pvev], inc=sv)

            for o in range(CH):
                tok = ws.load(lambda s_, o=o, l=l: (
                    s_[:, 0:512], wd["wot"][(l * 8 + o) * 128:(l * 8 + o + 1) * 128, :]),
                    extra_waits=[o_done])

                def wmm(e, tok=tok, o=o):
                    slot, sem, val = tok
                    e.wait_ge(sem.h, val)
                    for c4 in range(4):
                        r = e.matmul(st.pb[3][:, :],
                                     wslot[slot][:, c4 * 128:(c4 + 1) * 128],
                                     Ow[:, c4 * 512:(c4 + 1) * 512],
                                     start=(c4 == 0), stop=(c4 == 3),
                                     skip_group_check=True)
                    return r
                mev = mk.op("tensor", wmm, waits=[o_done], inc=sp)
                ws.consumed(tok[0], mev)
                o_done = mk.op("vector", lambda e, o=o, t0=t0: e.tensor_tensor(
                    y[:, o * T + t0: o * T + t0 + 512],
                    y[:, o * T + t0: o * T + t0 + 512],
                    st.pb[3][:, :], ALU.add), waits=[mev], inc=sv)
        attn_done = o_done
        phase_n += 1
        if phase_n >= phases:
            y_ev = attn_done
            break

        hn = st.B.t[:, 0:4096]
        ag = st.B.t[:, 4096:4096 + 11264]

        def savebnd2(e):
            for c4 in range(4):
                c = 4 + c4
                for wq in range(3):
                    e.tensor_copy(ybnd[:, c4 * 3 + wq: c4 * 3 + wq + 1],
                                  y[:, c * T + (wq + 1) * 512 - 1:
                                    c * T + (wq + 1) * 512])
                    e.drain()
            return None
        ffn_done = mk.op("vector", savebnd2, waits=[attn_done], inc=sv)
        for nt in range(NT):
            t0 = nt * 512
            def fsh2(e, t0=t0, nt=nt):
                for c4 in range(4):
                    c = 4 + c4
                    dst = hn[:, c * 512:(c + 1) * 512]
                    if nt == 0:
                        e.memset(dst[:, 0:1], 0.0)
                        e.drain()
                    else:
                        e.tensor_copy(dst[:, 0:1],
                                      ybnd[:, c4 * 3 + nt - 1: c4 * 3 + nt])
                        e.drain()
                    e.tensor_copy(dst[:, 1:512],
                                  y[:, c * T + t0: c * T + t0 + 511])
                    e.drain()
                return None
            sh_ev = mk.op("vector", fsh2, waits=[ffn_done], inc=sv)

            stat_ev = sh_ev
            for c in range(CH):
                src_fn = (lambda c=c, t0=t0: y[:, c * T + t0: c * T + t0 + 512]
                          if c < 4 else hn[:, c * 512:(c + 1) * 512])
                src = src_fn()
                sqev = mk.op("scalar", lambda e, src=src: e.activation(
                    sqt, src, AF.Square), waits=[stat_ev], inc=sa)

                def stmm(e, src=src, c=c):
                    e.matmul(st.pb[4][0:1, :], st.ones_b, src,
                             start=(c == 0), stop=(c == CH - 1),
                             skip_group_check=True)
                    return e.matmul(st.pb[5][0:1, :], st.ones_b, sqt[:, 0:512],
                                    start=(c == 0), stop=(c == CH - 1),
                                    skip_group_check=True)
                stat_ev = mk.op("tensor", stmm, waits=[sqev], inc=sp)

            def fsmath(e):
                e.tensor_scalar(smu, st.pb[4][0:1, :], 1.0 / D, None,
                                ALU.mult)
                e.drain()
                e.tensor_tensor(srs, smu, smu, ALU.mult)
                e.drain()
                e.scalar_tensor_tensor(srs, st.pb[5][0:1, :], 1.0 / D,
                                       srs, ALU.mult, ALU.subtract)
                e.drain()
                e.tensor_scalar(srs, srs, LN_EPS, None, ALU.add)
                e.drain()
                return None
            ev = mk.op("vector", fsmath, waits=[stat_ev], inc=sv)
            ev = mk.op("scalar", lambda e: e.activation(
                srs, srs, AF.Sqrt), waits=[ev], inc=sa)
            ev = mk.op("vector", lambda e: e.reciprocal(srs, srs),
                       waits=[ev], inc=sv)

            def fbc(e):
                e.matmul(st.pb[4][:, :], st.ones_f, smu, start=True,
                         stop=True, skip_group_check=True)
                return e.matmul(st.pb[5][:, :], st.ones_f, srs[0:1, :], start=True,
                                stop=True, skip_group_check=True)
            ev = mk.op("tensor", fbc, waits=[ev], inc=sp)

            def fapply(e, t0=t0, l=l):
                for c in range(CH):
                    src = (y[:, c * T + t0: c * T + t0 + 512] if c < 4
                           else hn[:, c * 512:(c + 1) * 512])
                    dst = hn[:, c * 512:(c + 1) * 512]
                    e.tensor_tensor(ptmp, src, st.pb[4][:, :], ALU.subtract)
                    e.drain()
                    e.tensor_tensor(ptmp, ptmp, st.pb[5][:, :], ALU.mult)
                    e.drain()
                    e.tensor_scalar(dst, ptmp, lng_c[:, l * 8 + c: l * 8 + c + 1],
                                    lnb2_c[:, l * 8 + c: l * 8 + c + 1],
                                    ALU.mult, ALU.add)
                    e.drain()
                return None
            hn_ev = mk.op("vector", fapply, waits=[ev], inc=sv)

            agev = None
            for j in range(22):
                tokg = ws.load(lambda s_, j=j, l=l: (
                    s_[:, 0:1024], wd["w1gt"][(l * 22 + j) * 128:
                                              (l * 22 + j + 1) * 128, :]),
                    extra_waits=[hn_ev])
                toka = ws.load(lambda s_, j=j, l=l: (
                    s_[:, 0:1024], wd["w1at"][(l * 22 + j) * 128:
                                              (l * 22 + j + 1) * 128, :]),
                    extra_waits=[hn_ev])
                bg = st.pb[(j % 3) * 2]
                ba = st.pb[(j % 3) * 2 + 1]

                def agmm(e, tokg=tokg, toka=toka, bg=bg, ba=ba):
                    r = None
                    for (tok, bank) in ((tokg, bg), (toka, ba)):
                        slot, sem, val = tok
                        e.wait_ge(sem.h, val)
                        for c in range(CH):
                            r = e.matmul(bank[:, :],
                                         wslot[slot][:, c * 128:(c + 1) * 128],
                                         hn[:, c * 512:(c + 1) * 512],
                                         start=(c == 0), stop=(c == CH - 1),
                                         skip_group_check=True)
                    return r
                w = [hn_ev]
                if agev:
                    w.append(agev)
                mev = mk.op("tensor", agmm, waits=w, inc=sp)
                ws.consumed(tokg[0], mev)
                ws.consumed(toka[0], mev)
                gelev = mk.op("scalar", lambda e, j=j, l=l, bg=bg: e.activation(
                    ag[:, j * 512:(j + 1) * 512], bg[:, :], AF.Gelu,
                    bias=b1g_c[:, l * 22 + j: l * 22 + j + 1]),
                    waits=[mev], inc=sa)
                agev = mk.op("vector", lambda e, j=j, l=l, ba=ba:
                             e.scalar_tensor_tensor(
                                 ag[:, j * 512:(j + 1) * 512], ba[:, :],
                                 b1a_c[:, l * 22 + j: l * 22 + j + 1],
                                 ag[:, j * 512:(j + 1) * 512],
                                 ALU.add, ALU.mult), waits=[gelev], inc=sv)
            ag_done = agev

            ffn_done = None
            for o in range(CH):
                toks = [ws.load(lambda s_, o=o, l=l, hh=hh: (
                    s_, wd["w2t3"][(l * 8 + o) * 128:(l * 8 + o + 1) * 128,
                                   hh * 1408:(hh + 1) * 1408]),
                    extra_waits=[ag_done]) for hh in range(2)]

                def w2mm(e, toks=toks, o=o):
                    r = None
                    for hh in range(2):
                        slot, sem, val = toks[hh]
                        e.wait_ge(sem.h, val)
                        for kc in range(11):
                            kcg = hh * 11 + kc
                            r = e.matmul(st.pb[o][:, :],
                                         wslot[slot][:, kc * 128:(kc + 1) * 128],
                                         ag[:, kcg * 512:(kcg + 1) * 512],
                                         start=(hh == 0 and kc == 0),
                                         stop=(hh == 1 and kc == 10),
                                         skip_group_check=True)
                    return r
                w = [ag_done]
                if ffn_done:
                    w.append(ffn_done)
                mev = mk.op("tensor", w2mm, waits=w, inc=sp)
                for t_ in toks:
                    ws.consumed(t_[0], mev)
                ffn_done = mk.op("vector", lambda e, o=o, t0=t0, l=l:
                                 e.scalar_tensor_tensor(
                                     y[:, o * T + t0: o * T + t0 + 512],
                                     st.pb[o][:, :],
                                     b2_c[:, l * 8 + o: l * 8 + o + 1],
                                     y[:, o * T + t0: o * T + t0 + 512],
                                     ALU.add, ALU.add), waits=[mev], inc=sv)
        y_ev = ffn_done
        phase_n += 1
        if phase_n >= phases:
            break

    def xup(e):
        for c in range(CH):
            for nt in range(NT):
                xc = st.x[:, c * T + nt * 512: c * T + nt * 512 + 512]
                e.scalar_tensor_tensor(xc, y[:, c * T + nt * 512:
                                             c * T + nt * 512 + 512], 0.5,
                                       xc, ALU.mult, ALU.add)
                e.drain()
        return None
    done = mk.op("vector", xup, waits=[y_ev], inc=sv)
    st.A.pop(); st.B.pop(); st.Cf.pop(); st.W.pop()
    return done




def wtile(w):
    i, o = w.shape
    return np.ascontiguousarray(
        w.reshape(i // 128, 128, o // 128, 128).transpose(2, 1, 0, 3)
         .reshape(o, i))


def host_cols(a, nblk):
    E = a.shape[0]
    return np.ascontiguousarray(a.reshape(E, nblk, 128).transpose(2, 0, 1)
                                 .reshape(128, E * nblk))


HID3 = 2730
HIDP3 = 2816


def _prep_weights(I):
    f32 = lambda a: np.ascontiguousarray(np.asarray(a), dtype=np.float32)
    w = {}
    w["f0_w0t"] = f32(np.asarray(I["L0_ffn_weight"]).T)
    w["a0_hf"] = f32(np.asarray(I["L0_attn_h_filter"]).T)
    w["a0_gf"] = f32(np.asarray(I["L0_attn_g_filter"]).T)
    w["a0_hs"] = f32(np.asarray(I["L0_attn_h_synth"]).T)
    w["a0_gs"] = f32(np.asarray(I["L0_attn_g_synth"]).T)
    w["a0_gamma"] = f32(np.asarray(I["L0_attn_norm_gamma"]).reshape(8, 128).T)
    w["a0_beta"] = f32(np.asarray(I["L0_attn_norm_beta"]).reshape(8, 128).T)
    w["f1_rwt"] = f32(np.asarray(I["L1_ffn_router_w"]).T)
    w["f1_rb"] = f32(np.asarray(I["L1_ffn_router_b"]).reshape(8, 1))
    w["f1_ewt"] = f32(np.asarray(I["L1_ffn_expert_w"]).transpose(0, 2, 1)
                      .reshape(8 * 1024, 1024))
    w["f1_eb"] = f32(I["L1_ffn_expert_b"])
    w["a1_wqkv"] = wtile(f32(I["L1_attn_w_qkv"]))
    w["a1_wproj"] = wtile(f32(I["L1_attn_w_proj"]))
    w["a2_wq"] = wtile(f32(I["L2_attn_Wq"]))
    w["a2_wkv"] = wtile(f32(I["L2_attn_Wkv"]))
    w["a2_wout"] = wtile(f32(I["L2_attn_Wout"]))
    w["a2_bout"] = f32(np.asarray(I["L2_attn_bout"]).reshape(8, 128).T)
    W1 = f32(I["L2_ffn_mlp_W1"])
    W2 = f32(I["L2_ffn_mlp_W2"])
    W0 = f32(I["L2_ffn_kan_sl_W0"])
    W1k = f32(I["L2_ffn_kan_sl_W1"])
    w["f2_gw"] = f32(I["L2_ffn_gate_W"])
    w["f2_w1t"] = np.concatenate([wtile(W1[e]) for e in range(4)], 0)
    w["f2_w2t"] = np.concatenate([wtile(W2[e]) for e in range(4)], 0)
    W0r = W0.reshape(4, 1024, 8, 2048).transpose(0, 2, 1, 3)
    w["f2_w0t"] = np.concatenate([wtile(np.ascontiguousarray(W0r[e, g]))
                                  for e in range(4) for g in range(8)], 0)
    W1r = W1k.reshape(4, 2048, 8, 1024).transpose(0, 2, 1, 3)
    w["f2_w1kt"] = np.concatenate([wtile(np.ascontiguousarray(W1r[e, g]))
                                   for e in range(4) for g in range(8)], 0)
    w["f2_b1"] = host_cols(f32(I["L2_ffn_mlp_b1"]), 32)
    w["f2_b2r"] = f32(I["L2_ffn_mlp_b2"])
    w["f2_g0c"] = host_cols(f32(I["L2_ffn_kan_ln_g0"]), 8)
    w["f2_b0c"] = host_cols(f32(I["L2_ffn_kan_ln_b0"]), 8)
    w["f2_g1c"] = host_cols(f32(I["L2_ffn_kan_ln_g1"]), 16)
    w["f2_b1cc"] = host_cols(f32(I["L2_ffn_kan_ln_b1"]), 16)
    w["f2_b0k"] = host_cols(f32(I["L2_ffn_kan_sl_b0"]), 16)
    w["f2_b1k"] = host_cols(f32(I["L2_ffn_kan_sl_b1"]), 8)
    w["a3_wqkv"] = wtile(f32(I["L3_attn_Wqkv"]))
    w["a3_wout"] = wtile(f32(I["L3_attn_Wout"]))
    w["a3_bout"] = f32(np.asarray(I["L3_attn_bout"]).reshape(8, 128).T)
    w["a3_bqkv"] = f32(np.asarray(I["L3_attn_bqkv"]).reshape(24, 128).T)
    wqkv3 = f32(I["L3_ffn_wqkv"])
    wo3 = f32(I["L3_ffn_wo"])
    w13 = f32(I["L3_ffn_w1"])
    w23 = f32(I["L3_ffn_w2"])
    qs3 = f32(I["L3_ffn_q_scale"])
    ks3 = f32(I["L3_ffn_k_scale"])
    lg3 = f32(I["L3_ffn_ln_g"])
    lb3 = f32(I["L3_ffn_ln_b"])
    b13 = f32(I["L3_ffn_b1"])
    b23 = f32(I["L3_ffn_b2"])

    def padw(a, n):
        out = np.zeros((a.shape[0], n), np.float32)
        out[:, :a.shape[1]] = a
        return out

    def padc(vec, n):
        out = np.zeros(n, np.float32)
        out[:len(vec)] = vec
        return out
    w["f3_wqkvt"] = np.concatenate([wtile(wqkv3[l]) for l in range(2)], 0)
    w["f3_wot"] = np.concatenate([wtile(wo3[l]) for l in range(2)], 0)
    w["f3_w1at"] = np.concatenate(
        [wtile(padw(w13[l][:, :HID3], HIDP3)) for l in range(2)], 0)
    w["f3_w1gt"] = np.concatenate(
        [wtile(padw(w13[l][:, HID3:], HIDP3)) for l in range(2)], 0)
    w2p = []
    for l in range(2):
        t_ = np.zeros((HIDP3, 1024), np.float32)
        t_[:HID3] = w23[l]
        w2p.append(wtile(t_))
    w["f3_w2t3"] = np.concatenate(w2p, 0)
    w["f3_qs"] = np.ascontiguousarray(
        np.stack([np.tile(qs3[l], 2) for l in range(2)], 1))
    w["f3_ks"] = np.ascontiguousarray(
        np.stack([np.tile(ks3[l], 2) for l in range(2)], 1))
    w["f3_lng"] = np.ascontiguousarray(
        np.concatenate([lg3[l].reshape(8, 128).T for l in range(2)], 1))
    w["f3_lnb"] = np.ascontiguousarray(
        np.concatenate([lb3[l].reshape(8, 128).T for l in range(2)], 1))
    w["f3_b1a"] = np.ascontiguousarray(np.concatenate(
        [padc(b13[l][:HID3], HIDP3).reshape(22, 128).T for l in range(2)], 1))
    w["f3_b1g"] = np.ascontiguousarray(np.concatenate(
        [padc(b13[l][HID3:], HIDP3).reshape(22, 128).T for l in range(2)], 1))
    w["f3_b2c3"] = np.ascontiguousarray(np.concatenate(
        [b23[l].reshape(8, 128).T for l in range(2)], 1))
    return w


def build_full(wshapes):
    nc = bass.Bass("TRN2", target_bir_lowering=False, debug=False)
    x_t = nc.dram_tensor("x_t", [D, T], F32, kind="ExternalInput").ap()
    y_t = nc.dram_tensor("y_t", [D, T], F32, kind="ExternalOutput").ap()
    dr = {}
    for k, shp in wshapes.items():
        dr[k] = nc.dram_tensor(k, list(shp), F32, kind="ExternalInput").ap()

    mk = MK(nc)
    st = State(mk)
    sems = {k: mk.sem(k) for k in ["v", "a", "p", "g"]}
    ldx = mk.sem("ldx")
    for c in range(CH):
        mk.op("sync", lambda e, c=c: e.dma_start(
            out=st.x[:, c * T:(c + 1) * T],
            in_=x_t[c * 128:(c + 1) * 128, :]), inc=ldx, dma=True)
    d = ldx.ev

    sub = lambda pre: {k[len(pre):]: v for k, v in dr.items() if k.startswith(pre)}

    st.W.push()
    wsig = st.W.alloc(8192)
    f0p = emit_f0_prep(mk, st, dr["f0_w0t"], sems, wsig, gate=d)
    xln = emit_ln(mk, st, d, sems)
    d = emit_f0(mk, st, xln, f0p, wsig, sems)
    xln = emit_ln(mk, st, d, sems)
    d = emit_a0(mk, st, xln, sub("a0_"), sems)
    xln = emit_ln(mk, st, d, sems)
    d = emit_f0(mk, st, xln, f0p, wsig, sems)
    st.W.pop()
    xln = emit_ln(mk, st, d, sems)
    d = emit_f1(mk, st, d, xln, sub("f1_"), sems)
    xln = emit_ln(mk, st, d, sems)
    d = emit_attn_l1(mk, st, xln,
                     {"wqkv_t": dr["a1_wqkv"], "wproj_t": dr["a1_wproj"]}, sems)
    xln = emit_ln(mk, st, d, sems)
    d = emit_f1(mk, st, d, xln, sub("f1_"), sems)
    xln = emit_ln(mk, st, d, sems)
    d = emit_f2(mk, st, xln, sub("f2_"), sems)
    xln = emit_ln(mk, st, d, sems)
    d = emit_attn_l23(mk, st, xln,
                      {"wq_t": dr["a2_wq"], "wkv_t": dr["a2_wkv"],
                       "wout_t": dr["a2_wout"], "bout": dr["a2_bout"]}, sems, 2)
    xln = emit_ln(mk, st, d, sems)
    d = emit_f2(mk, st, xln, sub("f2_"), sems)
    xln = emit_ln(mk, st, d, sems)
    d = emit_f3(mk, st, xln, sub("f3_"), sems)
    xln = emit_ln(mk, st, d, sems)
    d = emit_attn_l23(mk, st, xln,
                      {"wqkv_t": dr["a3_wqkv"], "wout_t": dr["a3_wout"],
                       "bout": dr["a3_bout"], "bqkv": dr["a3_bqkv"]}, sems, 3)
    xln = emit_ln(mk, st, d, sems)
    d = emit_f3(mk, st, xln, sub("f3_"), sems)

    stc = mk.sem("stc")
    for c in range(CH):
        mk.op("gpsimd", lambda e, c=c: e.dma_start(
            out=y_t[c * 128:(c + 1) * 128, :],
            in_=st.x[:, c * T:(c + 1) * T]), waits=[d], inc=stc, dma=True)
    mk.op("gpsimd", lambda e: None, waits=[stc.ev])
    mk.finish()
    return nc


_BUILT = {}


def kernel(**inputs):
    w = _prep_weights(inputs)
    x = np.ascontiguousarray(np.asarray(inputs["L0_attn_x"]), dtype=np.float32)
    key = "full"
    if key not in _BUILT:
        _BUILT[key] = build_full({k: v.shape for k, v in w.items()})
    nc = _BUILT[key]
    in_maps = []
    for b in range(8):
        m = {"x_t": np.ascontiguousarray(x[b].T)}
        m.update(w)
        in_maps.append(m)
    res = run_bass_kernel_spmd(nc, in_maps, core_ids=list(range(8)))
    out = np.stack([np.asarray(res.results[b]["y_t"]).T for b in range(8)], 0)
    return np.ascontiguousarray(out).astype(np.float32)


# version 04
# speedup vs baseline: 1.0769x; 1.0769x over previous
import numpy as np
import ml_dtypes
import concourse.bass as bass
import concourse.mybir as mybir
from concourse.bass_utils import run_bass_kernel_spmd
from contextlib import ExitStack

F32 = mybir.dt.float32
BF16 = mybir.dt.bfloat16
AF = mybir.ActivationFunctionType
ALU = mybir.AluOpType
AX = mybir.AxisListType

D = 1024
T = 2048
CH = D // 128
NT = T // 512
PI = float(np.pi)
M23 = 8388608.0
M15 = 12582912.0
LN_EPS = 1e-5


class Sem:
    __slots__ = ("h", "n")
    def __init__(self, h):
        self.h = h
        self.n = 0
    @property
    def ev(self):
        return (self, self.n)


class MK:
    ENGS = ("tensor", "vector", "scalar", "gpsimd", "sync")

    def __init__(self, nc):
        self.nc = nc
        self.ctx = ExitStack()
        self.q = {e: [] for e in self.ENGS}
        self._semn = 0

    def sem(self, name=None):
        self._semn += 1
        name = name or f"s{self._semn}"
        return Sem(self.ctx.enter_context(self.nc.semaphore(name)))

    def sb(self, name, shape, dtype):
        return self.ctx.enter_context(self.nc.sbuf_tensor(name, shape, dtype))

    def psum(self, name, shape, dtype):
        return self.ctx.enter_context(self.nc.psum_tensor(name, shape, dtype))

    def op(self, eng, fn, waits=(), inc=None, dma=False):
        k = 16 if dma else 1
        tgt = None
        if inc is not None:
            inc.n += k
            tgt = (inc, k)
        waits = [(s, v) for (s, v) in waits if s is not None]
        tail_drain = (not dma) and eng in ("vector", "scalar", "gpsimd")

        def run(e):
            for (s, v) in waits:
                e.wait_ge(s.h, v)
            r = fn(e)
            if tail_drain:
                r = e.drain()
            if tgt is not None:
                (r if r is not None else e.nop()).then_inc(tgt[0].h, tgt[1])
        self.q[eng].append(run)
        return None if inc is None else inc.ev

    def finish(self):
        with self.nc.Block() as block:
            for eng in self.ENGS:
                if not self.q[eng]:
                    continue
                def make(lst):
                    def body(e):
                        for fn in lst:
                            fn(e)
                    return body
                getattr(block, eng)(make(self.q[eng]))
        self.ctx.close()


class Arena:
    def __init__(self, t, cols):
        self.t = t
        self.cols = cols
        self.pos = 0
        self.marks = []

    def alloc(self, n):
        assert self.pos + n <= self.cols, f"arena overflow {self.pos}+{n}>{self.cols}"
        s = self.t[:, self.pos:self.pos + n]
        self.pos += n
        return s

    def push(self):
        self.marks.append(self.pos)

    def pop(self):
        self.pos = self.marks.pop()


class WStream:
    def __init__(self, mk, slots, cols):
        self.mk = mk
        self.slots = slots
        self.sems = [mk.sem() for _ in slots]
        self.consume_ev = [None] * len(slots)
        self.rr = 0

    def load(self, mkaps, extra_waits=()):
        slot = self.rr % len(self.slots)
        self.rr += 1
        w = list(extra_waits)
        if self.consume_ev[slot] is not None:
            w.append(self.consume_ev[slot])
        sem = self.sems[slot]

        def fn(e, slot=slot):
            out_ap, in_ap = mkaps(self.slots[slot])
            return e.dma_start(out=out_ap, in_=in_ap)
        self.mk.op("gpsimd", fn, waits=w, inc=sem, dma=True)
        return (slot, sem, sem.n)

    def consumed(self, slot, ev):
        self.consume_ev[slot] = ev


class State:
    def __init__(self, mk: MK):
        self.mk = mk
        self.x = mk.sb("x_arena", [128, CH * T], F32)
        self.xln = mk.sb("xln_arena", [128, CH * T], BF16)
        self.A = Arena(mk.sb("arena_a", [128, 16384], BF16), 16384)
        self.B = Arena(mk.sb("arena_b", [128, 16384], BF16), 16384)
        self.Cf = Arena(mk.sb("arena_c", [128, 2048], F32), 2048)
        self.W = Arena(mk.sb("warena", [128, 8192], BF16), 8192)
        self.S4 = mk.psum("s4", [128, 2048], F32)
        self.pb = [self.S4[:, i * 512:(i + 1) * 512] for i in range(4)] + \
                  [mk.psum(f"pb{i}", [128, 512], F32) for i in range(4, 8)]
        self.cons = mk.sb("cons", [128, 256], F32)
        self.cbf = mk.sb("cbf", [128, 448], BF16)
        self.stat = mk.sb("stat", [128, 2048], F32)
        self.f2s = mk.sb("f2s", [128, 2816], BF16)
        self.lnscr = mk.sb("lnscr", [128, 1472], F32)
        self.ident_f = self.cons[:, 0:128]
        self.ones_f = self.cons[0:1, 128:256]
        self.ones_f32 = self.cons[32:33, 128:256]
        self.ident_b = self.cbf[:, 0:128]
        self.ones_b = self.cbf[:, 128:129]
        self.ones_brow = self.cbf[0:1, 192:320]
        g = mk.sem("init")

        def init(e):
            e.memset(self.ident_f, 1.0)
            e.drain()
            e.affine_select(self.ident_f, self.ident_f, [[-1, 128]], ALU.is_equal,
                            0.0, channel_multiplier=1)
            e.drain()
            e.memset(self.ones_f, 1.0)
            e.drain()
            e.memset(self.ones_f32, 1.0)
            e.drain()
            e.memset(self.ident_b, 1.0)
            e.drain()
            e.affine_select(self.ident_b, self.ident_b, [[-1, 128]], ALU.is_equal,
                            0.0, channel_multiplier=1)
            e.drain()
            e.memset(self.ones_b, 1.0)
            e.drain()
            return e.memset(self.ones_brow, 1.0)
        self.init_ev = mk.op("gpsimd", init, inc=g)
        for eng in ("tensor", "vector", "scalar", "sync"):
            mk.op(eng, lambda e: None, waits=[self.init_ev])

    def mean_nt(self, nt):
        return self.stat[0:1, nt * 512:(nt + 1) * 512]

    def rstd_nt(self, nt):
        return self.stat[32:33, nt * 512:(nt + 1) * 512]



def emit_ln(mk: MK, st: State, ready, sems):
    sv, sa, sp = sems["v"], sems["a"], sems["p"]
    st.B.push()
    xb = st.B.alloc(4096)
    sqb = st.B.alloc(4096)
    tf = st.lnscr[:, 0:512]
    tf2 = st.lnscr[:, 512:1024]
    var = st.stat[64:65, 0:512]

    prev_stats_done = None
    prev_apply_done = None
    prev_mm_done = None
    xln_ev = None
    for nt in range(NT):
        c0 = nt * 512
        mean, rstd = st.mean_nt(nt), st.rstd_nt(nt)
        w = [x for x in (ready, prev_mm_done) if x]

        def conv(e, c0=c0):
            for c in range(CH):
                xc = st.x[:, c * T + c0: c * T + c0 + 512]
                e.activation(xb[:, c * 512:(c + 1) * 512], xc, AF.Copy)
                r = e.activation(sqb[:, c * 512:(c + 1) * 512], xc, AF.Square)
            return r
        conv_ev = mk.op("scalar", conv, waits=w, inc=sa)
        w = [x for x in (conv_ev, prev_stats_done) if x]

        def mmstat(e):
            for c in range(CH):
                e.matmul(st.pb[4][0:1, :], st.ones_b, xb[:, c * 512:(c + 1) * 512],
                         start=(c == 0), stop=(c == CH - 1), skip_group_check=True)
            for c in range(CH):
                r = e.matmul(st.pb[5][0:1, :], st.ones_b, sqb[:, c * 512:(c + 1) * 512],
                             start=(c == 0), stop=(c == CH - 1), skip_group_check=True)
            return r
        prev_mm_done = mk.op("tensor", mmstat, waits=w, inc=sp)

        def smath(e, mean=mean):
            e.tensor_scalar(mean, st.pb[4][0:1, :], 1.0 / D, None, ALU.mult)
            e.drain()
            e.tensor_tensor(var, mean, mean, ALU.mult)
            e.drain()
            e.scalar_tensor_tensor(var, st.pb[5][0:1, :], 1.0 / D, var,
                                   ALU.mult, ALU.subtract)
            e.drain()
            return e.tensor_scalar(var, var, LN_EPS, None, ALU.add)
        ev = mk.op("vector", smath, waits=[prev_mm_done], inc=sv)
        prev_stats_done = ev
        ev = mk.op("scalar", lambda e, rstd=rstd: e.activation(rstd, var, AF.Sqrt),
                   waits=[ev], inc=sa)
        ev = mk.op("vector", lambda e, rstd=rstd: e.reciprocal(rstd, rstd),
                   waits=[ev], inc=sv)
        w = [x for x in (ev, prev_apply_done) if x]

        def bcast(e, mean=mean, rstd=rstd):
            e.matmul(st.pb[0][:, :], st.ones_f, mean, start=True, stop=True,
                     skip_group_check=True)
            return e.matmul(st.pb[1][:, :], st.ones_f32, rstd, start=True, stop=True,
                            skip_group_check=True)
        ev = mk.op("tensor", bcast, waits=w, inc=sp)

        def apply(e, c0=c0):
            for cp in range(0, CH, 2):
                for c, t_ in ((cp, tf), (cp + 1, tf2)):
                    xc = st.x[:, c * T + c0: c * T + c0 + 512]
                    e.tensor_tensor(t_, xc, st.pb[0][:, :], ALU.subtract)
                e.drain()
                for c, t_ in ((cp, tf), (cp + 1, tf2)):
                    xlc = st.xln[:, c * T + c0: c * T + c0 + 512]
                    e.tensor_tensor(xlc, t_, st.pb[1][:, :], ALU.mult)
                e.drain()
            return None
        xln_ev = mk.op("vector", apply, waits=[ev], inc=sv)
        prev_apply_done = xln_ev
    st.B.pop()
    return xln_ev



def emit_f0_prep(mk: MK, st: State, w0t, sems, wsig, gate=None):
    sa = sems["a"]
    st.A.push()
    tmp = st.A.alloc(8192)
    ld = mk.sem()
    gw_ = [gate] if gate else []
    for k in range(CH):
        mk.op("gpsimd", lambda e, k=k: e.dma_start(
            out=tmp[:, k * 1024:(k + 1) * 1024],
            in_=w0t[k * 128:(k + 1) * 128, :]), waits=gw_, inc=ld, dma=True)

    def sign(e):
        for k in range(CH):
            r = e.activation(wsig[:, k * 1024:(k + 1) * 1024],
                             tmp[:, k * 1024:(k + 1) * 1024], AF.Sign)
        return r
    ev = mk.op("scalar", sign, waits=[ld.ev], inc=sa)
    st.A.pop()
    return ev


def emit_f0(mk: MK, st: State, xln_ev, wsig_ev, wsig, sems):
    sv, sp = sems["v"], sems["p"]
    ev_prev = {}
    done = None
    for o in range(CH):
        for nt in range(NT):
            i = o * NT + nt
            bank = st.pb[i % 4]
            w = [xln_ev, wsig_ev]
            if i >= 4:
                w.append(ev_prev[i % 4])

            def mms(e, o=o, nt=nt, bank=bank):
                for k in range(CH):
                    r = e.matmul(bank[:, :],
                                 wsig[:, k * 1024 + o * 128: k * 1024 + (o + 1) * 128],
                                 st.xln[:, k * T + nt * 512: k * T + nt * 512 + 512],
                                 start=(k == 0), stop=(k == CH - 1),
                                 skip_group_check=True)
                return r
            mev = mk.op("tensor", mms, waits=w, inc=sp)

            def evac(e, o=o, nt=nt, bank=bank):
                xc = st.x[:, o * T + nt * 512: o * T + nt * 512 + 512]
                return e.scalar_tensor_tensor(xc, bank[:, :], 0.5, xc,
                                              ALU.mult, ALU.add)
            done = mk.op("vector", evac, waits=[mev], inc=sv)
            ev_prev[i % 4] = done
    return done



def emit_f1(mk: MK, st: State, x_ready, xln_ev, wd, sems):
    sv, sa, sp = sems["v"], sems["a"], sems["p"]
    st.A.push(); st.B.push(); st.Cf.push(); st.W.push()
    xh = st.A.alloc(4096)
    xl = st.A.alloc(4096)
    gates = st.A.alloc(2048)
    sel = st.A.alloc(1024)
    ebsb = st.A.alloc(1024)
    wh = st.A.alloc(64)
    wl = st.A.alloc(64)
    wslot = [st.W.alloc(2048) for _ in range(4)]
    rw_f = st.Cf.alloc(64)
    swn = st.Cf.alloc(1)
    rbc = st.Cf.alloc(1)
    lgsb = st.Cf.alloc(512)
    scr = st.Cf.alloc(16)
    m1, m1n, ssum, srec = scr[:, 0:1], scr[:, 1:2], scr[:, 2:3], scr[:, 3:4]
    ohm = scr[:, 4:12]
    gtm = st.B.alloc(8)

    ld0 = mk.sem()
    gw_ = [x_ready] if x_ready else []
    mk.op("gpsimd", lambda e: e.dma_start(
        out=rw_f.rearrange("p (c e) -> p c e", e=8),
        in_=wd["rwt"].rearrange("(c p) e -> p c e", p=128)), waits=gw_, inc=ld0, dma=True)
    mk.op("gpsimd", lambda e: e.dma_start(out=rbc[0:8, :], in_=wd["rb"][:, :]),
          waits=gw_, inc=ld0, dma=True)
    mk.op("gpsimd", lambda e: e.dma_start(out=ebsb[0:8, :], in_=wd["eb"][:, :]),
          waits=gw_, inc=ld0, dma=True)

    def mksel(e):
        e.memset(sel, 1.0)
        e.drain()
        for ex in range(8):
            e.affine_select(sel[:, ex * 128:(ex + 1) * 128],
                            sel[:, ex * 128:(ex + 1) * 128],
                            [[0, 128]], ALU.is_equal, 0.0, base=-ex,
                            channel_multiplier=1)
            e.drain()
        return None
    sel_ev = mk.op("gpsimd", mksel, waits=[st.init_ev] + gw_, inc=sems["g"])

    whl_ev = mk.op("scalar", lambda e: e.activation(wh, rw_f, AF.Copy),
                   waits=[ld0.ev], inc=sa)
    whl_ev = mk.op("vector", lambda e: e.tensor_tensor(wl, rw_f, wh, ALU.subtract),
                   waits=[whl_ev], inc=sv)

    def swmm(e):
        for c in range(CH):
            e.matmul(st.pb[3][0:8, 0:1], wh[:, c * 8:(c + 1) * 8], st.ones_b,
                     start=(c == 0), stop=False, skip_group_check=True)
        for c in range(CH):
            r = e.matmul(st.pb[3][0:8, 0:1], wl[:, c * 8:(c + 1) * 8], st.ones_b,
                         start=False, stop=(c == CH - 1), skip_group_check=True)
        return r
    ev = mk.op("tensor", swmm, waits=[whl_ev, st.init_ev], inc=sp)
    swn_ev = mk.op("vector", lambda e: e.tensor_scalar(
        swn[0:8, :], st.pb[3][0:8, 0:1], -1.0, None, ALU.mult), waits=[ev], inc=sv)

    gates_ev = None
    prev = None
    for nt in range(NT):
        c0 = nt * 512

        def split(e, c0=c0):
            for c in range(CH):
                r = e.activation(xh[:, c * 512:(c + 1) * 512],
                                 st.x[:, c * T + c0:c * T + c0 + 512], AF.Copy)
            return r
        w = [x_ready]
        if prev:
            w.append(prev)
        ev_h = mk.op("scalar", split, waits=w, inc=sa)

        def split2(e, c0=c0):
            for c in range(CH):
                e.tensor_tensor(xl[:, c * 512:(c + 1) * 512],
                                st.x[:, c * T + c0:c * T + c0 + 512],
                                xh[:, c * 512:(c + 1) * 512], ALU.subtract)
            return None
        ev_l = mk.op("vector", split2, waits=[ev_h], inc=sv)

        def lgmm(e):
            for (wop, xop) in ((wh, xh), (wl, xh), (wh, xl)):
                for c in range(CH):
                    r = e.matmul(st.pb[2][0:8, :], wop[:, c * 8:(c + 1) * 8],
                                 xop[:, c * 512:(c + 1) * 512],
                                 start=(wop is wh and xop is xh and c == 0),
                                 stop=(wop is wh and xop is xl and c == CH - 1),
                                 skip_group_check=True)
            return r
        prev = mk.op("tensor", lgmm, waits=[ev_l, swn_ev], inc=sp)
        ev = mk.op("vector", lambda e: e.tensor_copy(lgsb[0:8, :], st.pb[2][0:8, :]),
                   waits=[prev, xln_ev], inc=sv)
        def bc2(e, nt=nt):
            e.matmul(st.pb[0][0:8, :], st.ones_f[0:1, 0:8], st.mean_nt(nt),
                     start=True, stop=True, skip_group_check=True)
            return e.matmul(st.pb[1][0:8, :], st.ones_f32[0:1, 0:8], st.rstd_nt(nt),
                            start=True, stop=True, skip_group_check=True)
        ev = mk.op("tensor", bc2, waits=[ev], inc=sp)

        def corr(e):
            e.scalar_tensor_tensor(lgsb[0:8, :], st.pb[0][0:8, :], swn[0:8, :],
                                   lgsb[0:8, :], ALU.mult, ALU.add)
            e.drain()
            e.tensor_tensor(lgsb[0:8, :], lgsb[0:8, :], st.pb[1][0:8, :], ALU.mult)
            e.drain()
            return e.tensor_scalar(lgsb[0:8, :], lgsb[0:8, :], rbc[0:8, :], None,
                                   ALU.add)
        ev = mk.op("vector", corr, waits=[ev, swn_ev], inc=sv)

        def ltr(e):
            for gq in range(4):
                r = e.transpose(st.pb[2][:, gq * 8:(gq + 1) * 8],
                                lgsb[0:8, gq * 128:(gq + 1) * 128],
                                st.ident_f[0:8, 0:8])
            return r
        ev = mk.op("tensor", ltr, waits=[ev], inc=sp)
        for gq in range(4):
            tile = st.pb[2][:, gq * 8:(gq + 1) * 8]

            def sm1(e, tile=tile):
                e.tensor_reduce(m1, tile, AX.X, ALU.max)
                e.drain()
                return e.tensor_scalar(m1n, m1, -1.0, None, ALU.mult)
            ev = mk.op("vector", sm1, waits=[ev], inc=sv)
            ev = mk.op("scalar", lambda e, tile=tile: e.activation(
                ohm, tile, AF.Exp, bias=m1n, accum_out=ssum), waits=[ev], inc=sa)

            def sm2(e, tile=tile):
                e.reciprocal(srec, ssum)
                e.drain()
                e.tensor_scalar(ohm, tile, m1, None, ALU.is_equal)
                e.drain()
                return e.tensor_scalar(gtm, ohm, srec, None, ALU.mult)
            ev = mk.op("vector", sm2, waits=[ev], inc=sv)
            ev = mk.op("tensor", lambda e, gq=gq: e.transpose(
                st.pb[3][0:8, 0:256].bitcast(BF16)[:, gq * 64:gq * 64 + 64].rearrange(
                    "p (a b) -> p (a b)") if False else
                st.pb[3][:, 0:256].bitcast(BF16)[0:8, gq * 128:(gq + 1) * 128],
                gtm, st.ident_b), waits=[ev], inc=sp)
            ev = mk.op("vector", lambda e, gq=gq, nt=nt: e.tensor_copy(
                gates[0:8, nt * 512 + gq * 128: nt * 512 + (gq + 1) * 128],
                st.pb[3][:, 0:256].bitcast(BF16)[0:8, gq * 128:(gq + 1) * 128]),
                waits=[ev], inc=sv)
        gates_ev = ev

    gbs = st.B.alloc(4096)
    ue = [st.B.alloc(4096), st.B.alloc(4096)]
    ws = WStream(mk, wslot, 1024)
    ue_consumed = [None, None]
    done = None
    pb2_free_for_gb = gates_ev
    for nt in range(NT):
        c0 = nt * 512
        ev_gb = gates_ev
        for ex in range(8):
            w = [ev_gb, sel_ev]
            if done is not None:
                w.append(done)
            ev_mm = mk.op("tensor", lambda e, ex=ex, nt=nt: e.matmul(
                st.pb[6][:, :], sel[0:8, ex * 128:(ex + 1) * 128],
                gates[0:8, nt * 512:(nt + 1) * 512],
                start=True, stop=True, skip_group_check=True), waits=w, inc=sp)
            ev_gb = mk.op("vector", lambda e, ex=ex: e.tensor_copy(
                gbs[:, ex * 512:(ex + 1) * 512], st.pb[6][:, :]),
                waits=[ev_mm], inc=sv)
        gbs_ev = ev_gb
        mm_prev = None
        for ex in range(8):
            u = ue[ex % 2]
            w = [gbs_ev]
            if ue_consumed[ex % 2]:
                w.append(ue_consumed[ex % 2])

            def mkue(e, u=u, ex=ex, c0=c0):
                for c in range(CH):
                    e.tensor_tensor(u[:, c * 512:(c + 1) * 512],
                                    st.xln[:, c * T + c0:c * T + c0 + 512],
                                    gbs[:, ex * 512:(ex + 1) * 512], ALU.mult)
                return None
            uev = mk.op("vector", mkue, waits=w, inc=sv)
            toks = []
            for h in range(4):
                r0 = (ex * 8 + h * 2) * 128
                toks.append(ws.load(lambda slot_t, r0=r0: (
                    slot_t.rearrange("p (c n) -> p c n", n=1024),
                    wd["ewt"][r0:r0 + 2 * 128, :].rearrange("(c p) n -> p c n", p=128))))

            def mme(e, ex=ex, nt=nt, uu=u, toks=toks):
                r = None
                for h in range(4):
                    slot, sem, val = toks[h]
                    e.wait_ge(sem.h, val)
                    for c2 in range(2):
                        c = h * 2 + c2
                        for o in range(8):
                            r = e.matmul(
                                st.pb[o][:, :],
                                wslot[slot][:, c2 * 1024 + o * 128:
                                            c2 * 1024 + (o + 1) * 128],
                                uu[:, c * 512:(c + 1) * 512],
                                start=(ex == 0 and c == 0), stop=False,
                                skip_group_check=True)
                return r
            mm_prev = mk.op("tensor", mme, waits=[uev], inc=sp)
            ue_consumed[ex % 2] = mm_prev
            for h in range(4):
                ws.consumed(toks[h][0], mm_prev)

        def mmb(e, nt=nt):
            for o in range(8):
                r = e.matmul(st.pb[o][:, :], ebsb[0:8, o * 128:(o + 1) * 128],
                             gates[0:8, nt * 512:(nt + 1) * 512],
                             start=False, stop=True, skip_group_check=True)
            return r
        mm_prev = mk.op("tensor", mmb, waits=[mm_prev], inc=sp)

        def evac(e, nt=nt):
            for o in range(8):
                xc = st.x[:, o * T + nt * 512: o * T + nt * 512 + 512]
                e.scalar_tensor_tensor(xc, st.pb[o][:, :], 0.5, xc,
                                       ALU.mult, ALU.add)
            return None
        done = mk.op("vector", evac, waits=[mm_prev], inc=sv)

    st.A.pop(); st.B.pop(); st.Cf.pop(); st.W.pop()
    return done



def emit_rot_perm(mk: MK, st: State, dst, kind, sems, scr, ready=None):
    sv = sems["v"]
    scr.push()
    jv = scr.alloc(128)
    pv = scr.alloc(1)
    t1 = scr.alloc(128)
    par = scr.alloc(128)
    mm_ = scr.alloc(128)
    mp_ = scr.alloc(128)
    gp = mk.sem()
    w0 = [st.init_ev] + ([ready] if ready else [])
    ev = mk.op("gpsimd", lambda e: (
        e.iota(jv, [[1, 128]], channel_multiplier=0,
               allow_small_or_imprecise_dtypes=True),
        e.drain(),
        e.iota(pv, [[0, 1]], channel_multiplier=1,
               allow_small_or_imprecise_dtypes=True))[-1], waits=w0, inc=gp)

    def build(e):
        e.tensor_scalar(t1, jv, -1.0, pv, ALU.mult, ALU.add)
        e.drain()
        if kind == "il64":
            e.tensor_scalar(par, jv, 0.5, -0.25, ALU.mult, ALU.add)
            e.drain()
            e.tensor_scalar(par, par, M15, None, ALU.add)
            e.drain()
            e.tensor_scalar(par, par, M15, None, ALU.subtract)
            e.drain()
            e.scalar_tensor_tensor(par, par, -2.0, jv, ALU.mult, ALU.add)
            e.drain()
            e.tensor_scalar(mm_, t1, 1.0, None, ALU.is_equal)
            e.drain()
            e.tensor_scalar(par, par, 0.5, None, ALU.is_lt)
            e.drain()
            e.tensor_tensor(mm_, mm_, par, ALU.mult)
            e.drain()
            e.tensor_scalar(mp_, t1, -1.0, None, ALU.is_equal)
            e.drain()
            e.tensor_scalar(par, par, 0.5, None, ALU.is_lt)
            e.drain()
            e.tensor_tensor(mp_, mp_, par, ALU.mult)
            e.drain()
            e.scalar_tensor_tensor(dst, mm_, -1.0, mp_, ALU.mult, ALU.add)
            e.drain()
        elif kind == "half64":
            jb = par
            e.tensor_scalar(jb, jv, 64.0, None, ALU.is_ge)
            e.drain()
            e.scalar_tensor_tensor(jb, jb, -64.0, jv, ALU.mult, ALU.add)
            e.drain()
            e.tensor_scalar(mm_, t1, 32.0, None, ALU.is_equal)
            e.drain()
            e.tensor_scalar(mp_, jb, 32.0, None, ALU.is_lt)
            e.drain()
            e.tensor_tensor(mm_, mm_, mp_, ALU.mult)
            e.drain()
            e.tensor_scalar(mp_, t1, -32.0, None, ALU.is_equal)
            e.drain()
            e.tensor_scalar(jb, jb, 32.0, None, ALU.is_ge)
            e.drain()
            e.tensor_tensor(mp_, mp_, jb, ALU.mult)
            e.drain()
            e.scalar_tensor_tensor(dst, mm_, -1.0, mp_, ALU.mult, ALU.add)
            e.drain()
        else:
            raise ValueError(kind)
        return None
    ev = mk.op("vector", build, waits=[ev], inc=sv)
    scr.pop()
    return ev


def emit_rope_tables(mk: MK, st: State, Ct, St_, kind, sems, ready=None):
    sv, sa = sems["v"], sems["a"]
    st.Cf.push()
    pos = st.Cf.alloc(1024)
    frq = st.Cf.alloc(512)
    rr = st.Cf.alloc(512)
    invf = st.lnscr[:, 1040:1041]
    pmr = st.lnscr[:, 1041:1042]
    tmp32 = st.lnscr[:, 1042:1043]
    gp = mk.sem()
    w0 = [st.init_ev] + ([ready] if ready else [])
    ev0 = mk.op("gpsimd", lambda e: (
        e.iota(pos, [[1, 1024]], channel_multiplier=0,
               allow_small_or_imprecise_dtypes=True),
        e.drain(),
        e.iota(invf, [[0, 1]], channel_multiplier=1,
               allow_small_or_imprecise_dtypes=True))[-1], waits=w0, inc=gp)

    def mkinv(e):
        e.tensor_scalar(pmr, invf, 64.0, None, ALU.is_ge)
        e.drain()
        e.scalar_tensor_tensor(pmr, pmr, -64.0, invf, ALU.mult, ALU.add)
        e.drain()
        if kind == "il64":
            e.tensor_scalar(pmr, pmr, 0.5, -0.25, ALU.mult, ALU.add)
            e.drain()
            e.tensor_scalar(pmr, pmr, M15, None, ALU.add)
            e.drain()
            e.tensor_scalar(pmr, pmr, M15, None, ALU.subtract)
            e.drain()
        else:
            e.tensor_scalar(tmp32, pmr, 32.0, None, ALU.is_ge)
            e.drain()
            e.scalar_tensor_tensor(pmr, tmp32, -32.0, pmr, ALU.mult, ALU.add)
            e.drain()
        return None
    ev = mk.op("vector", mkinv, waits=[ev0], inc=sv)
    ev = mk.op("scalar", lambda e: e.activation(
        invf, pmr, AF.Exp, scale=-2.0 * float(np.log(10000.0)) / 64.0),
        waits=[ev], inc=sa)
    evs = []
    for half in range(2):
        off = half * 1024
        for piece in range(2):
            po = piece * 512

            def mkfr(e, off=off, po=po):
                e.scalar_tensor_tensor(frq, pos[:, po:po + 512],
                                       0.0 if off == 0 else 0.0, pos[:, po:po + 512],
                                       ALU.mult, ALU.bypass) if False else None
                e.tensor_scalar(frq, pos[:, po:po + 512], float(off), invf,
                                ALU.add, ALU.mult)
                e.drain()
                return None
            ev = mk.op("vector", mkfr, waits=[ev], inc=sv)

            def redsin(e):
                e.tensor_scalar(rr, frq, 1.0 / (2 * PI), M23, ALU.mult, ALU.add)
                e.drain()
                e.tensor_scalar(rr, rr, M23, None, ALU.subtract)
                e.drain()
                e.scalar_tensor_tensor(rr, rr, -2 * PI, frq, ALU.mult, ALU.add)
                e.drain()
                e.tensor_scalar(rr, rr, PI, -PI, ALU.min, ALU.max)
                e.drain()
                return None
            ev = mk.op("vector", redsin, waits=[ev], inc=sv)
            ev = mk.op("scalar", lambda e, off=off, po=po: e.activation(
                St_[:, off + po: off + po + 512], rr, AF.Sin), waits=[ev], inc=sa)

            def redcos(e):
                e.tensor_scalar(rr, frq, 1.0 / (2 * PI), 0.25, ALU.mult, ALU.add)
                e.drain()
                e.tensor_scalar(rr, rr, M23, None, ALU.add)
                e.drain()
                e.tensor_scalar(rr, rr, M23, None, ALU.subtract)
                e.drain()
                e.scalar_tensor_tensor(rr, rr, -2 * PI, frq, ALU.mult, ALU.add)
                e.drain()
                e.tensor_scalar(rr, rr, PI / 2, None, ALU.add)
                e.drain()
                e.tensor_scalar(rr, rr, PI, -PI, ALU.min, ALU.max)
                e.drain()
                return None
            ev = mk.op("vector", redcos, waits=[ev], inc=sv)
            ev = mk.op("scalar", lambda e, off=off, po=po: e.activation(
                Ct[:, off + po: off + po + 512], rr, AF.Sin), waits=[ev], inc=sa)
    st.Cf.pop()
    return ev



def emit_attn_l1(mk: MK, st: State, xln_ev, wd, sems):
    sv, sa, sp = sems["v"], sems["a"], sems["p"]
    st.A.push(); st.B.push(); st.W.push()
    q_sb = st.A.alloc(2048)
    k_sb = st.A.alloc(2048)
    v_sb = st.A.alloc(2048)
    vT = st.A.alloc(2048)
    Ct = st.A.alloc(2048)
    St_ = st.A.alloc(2048)
    P_sb = st.A.alloc(2048)
    PT = st.A.alloc(2048)
    Rm = st.cbf[:, 320:448]
    O_all = st.B.alloc(16384)
    wslot = [st.W.alloc(1024) for _ in range(4)]
    ws = WStream(mk, wslot, 1024)
    mx4 = st.lnscr[:, 1024:1028]
    m1 = st.lnscr[:, 1028:1029]
    m1s = st.lnscr[:, 1029:1030]
    dnc = st.lnscr[:, 1032:1036]
    den = st.lnscr[:, 1036:1037]
    rec = st.lnscr[:, 1037:1038]

    tab_ev = emit_rope_tables(mk, st, Ct, St_, "il64", sems, ready=xln_ev)
    perm_ev = emit_rot_perm(mk, st, Rm, "il64", sems, st.Cf, ready=tab_ev)

    pb67_free = [None, None]
    attn_done = None
    exp_done_prev = None
    pt_bank_free = None
    pv_bank_free = None
    for hg in range(CH):
        qkv_ev = {}
        for gi, (blk, dst) in enumerate(((hg, q_sb), (8 + hg, k_sb), (16 + hg, v_sb))):
            tok = ws.load(lambda slot_t, blk=blk: (
                slot_t, wd["wqkv_t"][blk * 128:(blk + 1) * 128, :]),
                extra_waits=[xln_ev])
            ev_pr = None
            for nt in range(NT):
                bank = st.pb[6 + nt % 2]
                w = [xln_ev]
                if pb67_free[nt % 2]:
                    w.append(pb67_free[nt % 2])

                def mmp(e, tok=tok, nt=nt, bank=bank):
                    slot, sem, val = tok
                    e.wait_ge(sem.h, val)
                    for c in range(CH):
                        r = e.matmul(bank[:, :],
                                     wslot[slot][:, c * 128:(c + 1) * 128],
                                     st.xln[:, c * T + nt * 512: c * T + nt * 512 + 512],
                                     start=(c == 0), stop=(c == CH - 1),
                                     skip_group_check=True)
                    return r
                mev = mk.op("tensor", mmp, waits=w, inc=sp)
                ev_pr = mk.op("vector", lambda e, dst=dst, nt=nt, bank=bank:
                              e.tensor_copy(dst[:, nt * 512:(nt + 1) * 512], bank[:, :]),
                              waits=[mev], inc=sv)
                pb67_free[nt % 2] = ev_pr
            qkv_ev[gi] = ev_pr
            ws.consumed(tok[0], mev)
        rope_evs = []
        for (src, ev_src) in ((q_sb, qkv_ev[0]), (k_sb, qkv_ev[1])):
            for nt in range(NT):
                w = [ev_src, tab_ev, perm_ev]
                if attn_done:
                    w.append(attn_done)
                if pb67_free[nt % 2]:
                    w.append(pb67_free[nt % 2])
                bank = st.pb[6 + nt % 2]
                rev = mk.op("tensor", lambda e, src=src, nt=nt, bank=bank: e.matmul(
                    bank[:, :], Rm, src[:, nt * 512:(nt + 1) * 512],
                    start=True, stop=True, skip_group_check=True), waits=w, inc=sp)

                def rope(e, src=src, nt=nt, bank=bank):
                    s_ = slice(nt * 512, (nt + 1) * 512)
                    e.tensor_tensor(P_sb[:, s_], src[:, s_], Ct[:, s_], ALU.mult)
                    e.tensor_tensor(PT[:, s_], bank[:, :], St_[:, s_], ALU.mult)
                    e.drain()
                    e.tensor_tensor(src[:, s_], P_sb[:, s_], PT[:, s_], ALU.add)
                    e.drain()
                    return None
                rev = mk.op("vector", rope, waits=[rev], inc=sv)
                pb67_free[nt % 2] = rev
                rope_evs.append(rev)
        rope_done = rope_evs[-1]
        vt_evs = []
        for kc2 in range(2):
            w = [qkv_ev[2]]
            if pt_bank_free:
                w.append(pt_bank_free)
            if attn_done:
                w.append(attn_done)

            def vtr(e, kc2=kc2):
                pbf = st.pb[4][:, :].bitcast(BF16)
                for j in range(8):
                    kc = kc2 * 8 + j
                    r = e.transpose(pbf[:, j * 128:(j + 1) * 128],
                                    v_sb[:, kc * 128:(kc + 1) * 128], st.ident_b)
                return r
            tev = mk.op("tensor", vtr, waits=w, inc=sp)

            def vte(e, kc2=kc2):
                pbf = st.pb[4][:, :].bitcast(BF16)
                for j in range(8):
                    kc = kc2 * 8 + j
                    e.tensor_copy(vT[:, kc * 128:(kc + 1) * 128],
                                  pbf[:, j * 128:(j + 1) * 128])
                return None
            pt_bank_free = mk.op("vector", vte, waits=[tev], inc=sv)
            vt_evs.append(pt_bank_free)
        vt_done = pt_bank_free
        for h2 in range(2):
            hb = h2 * 64
            for qt in range(16):
                w = [rope_done]
                if exp_done_prev:
                    w.append(exp_done_prev)

                def smm(e, hb=hb, qt=qt):
                    for n2 in range(4):
                        r = e.matmul(st.pb[n2][:, :],
                                     q_sb[hb:hb + 64, qt * 128:(qt + 1) * 128],
                                     k_sb[hb:hb + 64, n2 * 512:(n2 + 1) * 512],
                                     start=True, stop=True, skip_group_check=True)
                    return r
                sev = mk.op("tensor", smm, waits=w, inc=sp)

                def rmax(e):
                    e.tensor_reduce(m1, st.S4[:, :], AX.X, ALU.max)
                    e.drain()
                    e.tensor_scalar(m1s, m1, -0.125, None, ALU.mult)
                    e.drain()
                    return None
                mev = mk.op("vector", rmax, waits=[sev], inc=sv)

                def expp(e):
                    for n2 in range(4):
                        e.activation(P_sb[:, n2 * 512:(n2 + 1) * 512],
                                     st.pb[n2][:, :], AF.Exp, bias=m1s, scale=0.125,
                                     accum_out=dnc[:, n2:n2 + 1])
                    return None
                exp_done_prev = mk.op("scalar", expp, waits=[mev], inc=sa)

                def norm(e):
                    e.tensor_tensor(den, dnc[:, 0:1], dnc[:, 1:2], ALU.add)
                    e.tensor_tensor(rec, dnc[:, 2:3], dnc[:, 3:4], ALU.add)
                    e.drain()
                    e.tensor_tensor(den, den, rec, ALU.add)
                    e.drain()
                    e.reciprocal(rec, den)
                    e.drain()
                    e.tensor_scalar(P_sb[:, :], P_sb[:, :], rec, None, ALU.mult)
                    e.drain()
                    return None
                nev = mk.op("vector", norm, waits=[exp_done_prev], inc=sv)
                ptev = None
                for kc2 in range(2):
                    w = [nev]
                    if pv_bank_free:
                        w.append(pv_bank_free)

                    def ptr(e, kc2=kc2):
                        pbf = st.pb[5][:, :].bitcast(BF16)
                        for j in range(8):
                            kc = kc2 * 8 + j
                            r = e.transpose(pbf[:, j * 128:(j + 1) * 128],
                                            P_sb[:, kc * 128:(kc + 1) * 128],
                                            st.ident_b)
                        return r
                    tev = mk.op("tensor", ptr, waits=w, inc=sp)
                    eng = "vector" if kc2 == 0 else "scalar"

                    def pte(e, kc2=kc2, eng=eng):
                        pbf = st.pb[5][:, :].bitcast(BF16)
                        for j in range(8):
                            kc = kc2 * 8 + j
                            if eng == "vector":
                                e.tensor_copy(PT[:, kc * 128:(kc + 1) * 128],
                                              pbf[:, j * 128:(j + 1) * 128])
                            else:
                                e.activation(PT[:, kc * 128:(kc + 1) * 128],
                                             pbf[:, j * 128:(j + 1) * 128], AF.Copy)
                        return None
                    pv_bank_free = mk.op(eng, pte, waits=[tev],
                                         inc=sv if eng == "vector" else sa)
                    ptev = pv_bank_free
                w = [ptev, vt_done]
                if attn_done:
                    w.append(attn_done)

                def pvm(e, hb=hb):
                    for kc in range(16):
                        r = e.matmul(st.pb[4][0:64, 0:128],
                                     vT[:, kc * 128 + hb: kc * 128 + hb + 64],
                                     PT[:, kc * 128:(kc + 1) * 128],
                                     start=(kc == 0), stop=(kc == 15),
                                     skip_group_check=True)
                    return r
                pvev = mk.op("tensor", pvm, waits=w, inc=sp)
                attn_done = mk.op("vector", lambda e, hg=hg, hb=hb, qt=qt:
                                  e.tensor_copy(
                                      O_all[hb:hb + 64,
                                            hg * 2048 + qt * 128: hg * 2048 + (qt + 1) * 128],
                                      st.pb[4][0:64, 0:128]),
                                  waits=[pvev], inc=sv)
    done = None
    ev_prev = {}
    for o in range(CH):
        tok = ws.load(lambda slot_t, o=o: (
            slot_t, wd["wproj_t"][o * 128:(o + 1) * 128, :]), extra_waits=[attn_done])
        for nt in range(NT):
            i = o * NT + nt
            bank = st.pb[i % 4]
            w = [attn_done]
            if i >= 4:
                w.append(ev_prev[i % 4])

            def mmo(e, tok=tok, nt=nt, bank=bank):
                slot, sem, val = tok
                e.wait_ge(sem.h, val)
                for c in range(CH):
                    r = e.matmul(bank[:, :],
                                 wslot[slot][:, c * 128:(c + 1) * 128],
                                 O_all[:, c * 2048 + nt * 512: c * 2048 + nt * 512 + 512],
                                 start=(c == 0), stop=(c == CH - 1),
                                 skip_group_check=True)
                return r
            mev = mk.op("tensor", mmo, waits=w, inc=sp)
            if nt == NT - 1:
                ws.consumed(tok[0], mev)

            def evac(e, o=o, nt=nt, bank=bank):
                xc = st.x[:, o * T + nt * 512: o * T + nt * 512 + 512]
                return e.tensor_tensor(xc, xc, bank[:, :], ALU.add)
            done = mk.op("vector", evac, waits=[mev], inc=sv)
            ev_prev[i % 4] = done
    st.A.pop(); st.B.pop(); st.W.pop()
    return done


def _mk_l3_perm(mk: MK, st: State, dst, sems, scr, ready=None):
    sv = sems["v"]
    scr.push()
    jv = scr.alloc(128)
    pv = scr.alloc(1)
    t1 = scr.alloc(128)
    jb = scr.alloc(128)
    mm_ = scr.alloc(128)
    mp_ = scr.alloc(128)
    gp = mk.sem()
    w0 = [st.init_ev] + ([ready] if ready else [])
    ev = mk.op("gpsimd", lambda e: (
        e.iota(jv, [[1, 128]], channel_multiplier=0,
               allow_small_or_imprecise_dtypes=True),
        e.drain(),
        e.iota(pv, [[0, 1]], channel_multiplier=1,
               allow_small_or_imprecise_dtypes=True))[-1], waits=w0, inc=gp)

    def build(e):
        e.tensor_scalar(jb, jv, 64.0, None, ALU.is_ge)
        e.drain()
        e.scalar_tensor_tensor(jb, jb, -64.0, jv, ALU.mult, ALU.add)
        e.drain()
        e.tensor_scalar(t1, jv, -1.0, pv, ALU.mult, ALU.add)
        e.drain()
        e.tensor_tensor(t1, t1, jb, ALU.add)
        e.drain()
        e.tensor_scalar(mm_, jb, 2.0, 1.0, ALU.mult, ALU.add)
        e.drain()
        e.tensor_tensor(mm_, t1, mm_, ALU.is_equal)
        e.drain()
        e.tensor_scalar(mp_, jb, 32.0, None, ALU.is_lt)
        e.drain()
        e.tensor_tensor(mm_, mm_, mp_, ALU.mult)
        e.drain()
        e.tensor_scalar(mp_, jb, 2.0, -64.0, ALU.mult, ALU.add)
        e.drain()
        e.tensor_tensor(mp_, t1, mp_, ALU.is_equal)
        e.drain()
        e.tensor_scalar(jb, jb, 32.0, None, ALU.is_ge)
        e.drain()
        e.tensor_tensor(mp_, mp_, jb, ALU.mult)
        e.drain()
        e.tensor_tensor(jb, t1, jb if False else jb, ALU.is_equal) if False else None
        e.scalar_tensor_tensor(mp_, mm_, -1.0, mp_, ALU.mult, ALU.add)
        e.drain()
        e.tensor_scalar(t1, jv, -1.0, pv, ALU.mult, ALU.add)
        e.drain()
        e.tensor_scalar(t1, t1, 0.0, None, ALU.is_equal)
        e.drain()
        e.tensor_tensor(dst, mp_, t1, ALU.add)
        e.drain()
        return None
    ev = mk.op("vector", build, waits=[ev], inc=sv)
    scr.pop()
    return ev


def emit_attn_l23(mk: MK, st: State, xln_ev, wd, sems, layer):
    sv, sa, sp = sems["v"], sems["a"], sems["p"]
    st.A.push(); st.B.push(); st.W.push(); st.Cf.push()
    q_sb = st.A.alloc(2048)
    k_sb = st.A.alloc(2048)
    v_sb = k_sb if layer == 2 else st.A.alloc(2048)
    vT = st.A.alloc(2048)
    Ct = st.A.alloc(2048)
    St_ = st.A.alloc(2048)
    P_sb = st.A.alloc(2048)
    PT = st.A.alloc(2048)
    Rm = st.cbf[:, 320:448]
    O_all = st.B.alloc(16384)
    wslot = [st.W.alloc(1024) for _ in range(4)]
    ws = WStream(mk, wslot, 1024)
    bias_c = st.lnscr[:, 1056:1088]
    mx4 = st.lnscr[:, 1024:1028]
    m1 = st.lnscr[:, 1028:1029]
    m1s = st.lnscr[:, 1029:1030]
    dnc = st.lnscr[:, 1032:1036]
    den = st.lnscr[:, 1036:1037]
    rec = st.lnscr[:, 1037:1038]

    tab_ev = emit_rope_tables(mk, st, Ct, St_, "half32", sems, ready=xln_ev)
    if layer == 2:
        perm_ev = emit_rot_perm(mk, st, Rm, "half64", sems, st.Cf, ready=tab_ev)
    else:
        perm_ev = _mk_l3_perm(mk, st, Rm, sems, st.Cf, ready=tab_ev)
        mlo = st.lnscr[:, 1043:1044]
        mhi = st.lnscr[:, 1044:1045]

        def mkmask(e):
            e.tensor_scalar(mhi, mlo, 64.0, None, ALU.is_ge)
            e.drain()
            e.scalar_tensor_tensor(mhi, mhi, -64.0, mlo, ALU.mult, ALU.add)
            e.drain()
            e.tensor_scalar(mhi, mhi, 32.0, None, ALU.is_ge)
            e.drain()
            e.tensor_scalar(mlo, mhi, -1.0, 1.0, ALU.mult, ALU.add)
            e.drain()
            return None
        gp2 = mk.sem()
        evm = mk.op("gpsimd", lambda e: e.iota(mlo, [[0, 1]], channel_multiplier=1,
                                               allow_small_or_imprecise_dtypes=True),
                    waits=[tab_ev], inc=gp2)
        evm = mk.op("vector", mkmask, waits=[evm], inc=sv)

        def mkE(e):
            e.tensor_scalar(St_[:, :], St_[:, :], mlo, None, ALU.mult)
            e.drain()
            e.scalar_tensor_tensor(Ct[:, :], Ct[:, :], mhi, St_[:, :],
                                   ALU.mult, ALU.add)
            e.drain()
            return None
        tab_ev = mk.op("vector", mkE, waits=[evm, tab_ev], inc=sv)

    ldb = mk.sem()
    if layer == 2:
        mk.op("gpsimd", lambda e: e.dma_start(out=bias_c[:, 0:8],
                                              in_=wd["bout"][:, :]),
              waits=[xln_ev], inc=ldb, dma=True)
    else:
        mk.op("gpsimd", lambda e: e.dma_start(out=bias_c[:, 0:8],
                                              in_=wd["bout"][:, :]),
              waits=[xln_ev], inc=ldb, dma=True)
        mk.op("gpsimd", lambda e: e.dma_start(out=bias_c[:, 8:32],
                                              in_=wd["bqkv"][:, :]),
              waits=[xln_ev], inc=ldb, dma=True)
    bias_ev = ldb.ev

    pb67_free = [None, None]
    attn_done = None
    exp_done_prev = None
    pt_bank_free = None
    pv_bank_free = None
    scale = 0.125
    for hg in range(CH):
        if layer == 2:
            projs = [(q_sb, "wq_t", hg, None), (k_sb, "wkv_t", hg, None)]
        else:
            projs = [(q_sb, "wqkv_t", hg, 8 + hg), (k_sb, "wqkv_t", 8 + hg, 16 + hg),
                     (v_sb, "wqkv_t", 16 + hg, None)]
        qkv_ev = {}
        for gi, spec in enumerate(projs):
            dst, wkey, blk = spec[0], spec[1], spec[2]
            tok = ws.load(lambda slot_t, wkey=wkey, blk=blk: (
                slot_t, wd[wkey][blk * 128:(blk + 1) * 128, :]), extra_waits=[xln_ev])
            ev_pr = None
            for nt in range(NT):
                bank = st.pb[6 + nt % 2]
                w = [xln_ev]
                if pb67_free[nt % 2]:
                    w.append(pb67_free[nt % 2])

                def mmp(e, tok=tok, nt=nt, bank=bank):
                    slot, sem, val = tok
                    e.wait_ge(sem.h, val)
                    for c in range(CH):
                        r = e.matmul(bank[:, :],
                                     wslot[slot][:, c * 128:(c + 1) * 128],
                                     st.xln[:, c * T + nt * 512: c * T + nt * 512 + 512],
                                     start=(c == 0), stop=(c == CH - 1),
                                     skip_group_check=True)
                    return r
                mev = mk.op("tensor", mmp, waits=w, inc=sp)
                if layer == 3:
                    blkq = blk

                    def evp(e, dst=dst, nt=nt, bank=bank, blkq=blkq):
                        return e.tensor_scalar(dst[:, nt * 512:(nt + 1) * 512],
                                               bank[:, :],
                                               bias_c[:, 8 + blkq:9 + blkq], None,
                                               ALU.add)
                    ev_pr = mk.op("vector", evp, waits=[mev, bias_ev], inc=sv)
                else:
                    ev_pr = mk.op("vector", lambda e, dst=dst, nt=nt, bank=bank:
                                  e.tensor_copy(dst[:, nt * 512:(nt + 1) * 512],
                                                bank[:, :]), waits=[mev], inc=sv)
                pb67_free[nt % 2] = ev_pr
            qkv_ev[gi] = ev_pr
            ws.consumed(tok[0], mev)
        rope_evs = []
        if layer == 2:
            rope_srcs = ((q_sb, qkv_ev[0]), (k_sb, qkv_ev[1]))
        else:
            rope_srcs = ((q_sb, qkv_ev[0]), (k_sb, qkv_ev[1]))
        for (srcb, ev_src) in rope_srcs:
            for nt in range(NT):
                w = [ev_src, tab_ev, perm_ev]
                if attn_done:
                    w.append(attn_done)
                if pb67_free[nt % 2]:
                    w.append(pb67_free[nt % 2])
                bank = st.pb[6 + nt % 2]
                rev = mk.op("tensor", lambda e, srcb=srcb, nt=nt, bank=bank: e.matmul(
                    bank[:, :], Rm, srcb[:, nt * 512:(nt + 1) * 512],
                    start=True, stop=True, skip_group_check=True), waits=w, inc=sp)
                if layer == 2:
                    def rope(e, srcb=srcb, nt=nt, bank=bank):
                        s_ = slice(nt * 512, (nt + 1) * 512)
                        e.tensor_tensor(P_sb[:, s_], srcb[:, s_], Ct[:, s_], ALU.mult)
                        e.tensor_tensor(PT[:, s_], bank[:, :], St_[:, s_], ALU.mult)
                        e.drain()
                        e.tensor_tensor(srcb[:, s_], P_sb[:, s_], PT[:, s_], ALU.add)
                        e.drain()
                        return None
                else:
                    def rope(e, srcb=srcb, nt=nt, bank=bank):
                        s_ = slice(nt * 512, (nt + 1) * 512)
                        e.tensor_tensor(srcb[:, s_], bank[:, :], Ct[:, s_], ALU.mult)
                        e.drain()
                        return None
                rev = mk.op("vector", rope, waits=[rev], inc=sv)
                pb67_free[nt % 2] = rev
                rope_evs.append(rev)
        rope_done = rope_evs[-1]
        v_ready = rope_done if layer == 2 else qkv_ev[2]
        for kc2 in range(2):
            w = [v_ready]
            if pt_bank_free:
                w.append(pt_bank_free)
            if attn_done:
                w.append(attn_done)

            def vtr(e, kc2=kc2):
                pbf = st.pb[4][:, :].bitcast(BF16)
                for j in range(8):
                    kc = kc2 * 8 + j
                    r = e.transpose(pbf[:, j * 128:(j + 1) * 128],
                                    v_sb[:, kc * 128:(kc + 1) * 128], st.ident_b)
                return r
            tev = mk.op("tensor", vtr, waits=w, inc=sp)

            def vte(e, kc2=kc2):
                pbf = st.pb[4][:, :].bitcast(BF16)
                for j in range(8):
                    kc = kc2 * 8 + j
                    e.tensor_copy(vT[:, kc * 128:(kc + 1) * 128],
                                  pbf[:, j * 128:(j + 1) * 128])
                return None
            pt_bank_free = mk.op("vector", vte, waits=[tev], inc=sv)
        vt_done = pt_bank_free
        for h2 in range(2):
            hb = h2 * 64
            for qt in range(16):
                w = [rope_done, vt_done]
                if exp_done_prev:
                    w.append(exp_done_prev)

                def smm(e, hb=hb, qt=qt):
                    for n2 in range(4):
                        r = e.matmul(st.pb[n2][:, :],
                                     q_sb[hb:hb + 64, qt * 128:(qt + 1) * 128],
                                     k_sb[hb:hb + 64, n2 * 512:(n2 + 1) * 512],
                                     start=True, stop=True, skip_group_check=True)
                    return r
                sev = mk.op("tensor", smm, waits=w, inc=sp)

                def rmax(e):
                    e.tensor_reduce(m1, st.S4[:, :], AX.X, ALU.max)
                    e.drain()
                    e.tensor_scalar(m1s, m1, -scale, None, ALU.mult)
                    e.drain()
                    return None
                mev = mk.op("vector", rmax, waits=[sev], inc=sv)

                def expp(e):
                    for n2 in range(4):
                        e.activation(P_sb[:, n2 * 512:(n2 + 1) * 512],
                                     st.pb[n2][:, :], AF.Exp, bias=m1s, scale=scale,
                                     accum_out=dnc[:, n2:n2 + 1])
                    return None
                exp_done_prev = mk.op("scalar", expp, waits=[mev], inc=sa)

                def norm(e):
                    e.tensor_tensor(den, dnc[:, 0:1], dnc[:, 1:2], ALU.add)
                    e.tensor_tensor(rec, dnc[:, 2:3], dnc[:, 3:4], ALU.add)
                    e.drain()
                    e.tensor_tensor(den, den, rec, ALU.add)
                    e.drain()
                    e.reciprocal(rec, den)
                    e.drain()
                    e.tensor_scalar(P_sb[:, :], P_sb[:, :], rec, None, ALU.mult)
                    e.drain()
                    return None
                nev = mk.op("vector", norm, waits=[exp_done_prev], inc=sv)
                ptev = None
                for kc2 in range(2):
                    w = [nev]
                    if pv_bank_free:
                        w.append(pv_bank_free)

                    def ptr(e, kc2=kc2):
                        pbf = st.pb[5][:, :].bitcast(BF16)
                        for j in range(8):
                            kc = kc2 * 8 + j
                            r = e.transpose(pbf[:, j * 128:(j + 1) * 128],
                                            P_sb[:, kc * 128:(kc + 1) * 128],
                                            st.ident_b)
                        return r
                    tev = mk.op("tensor", ptr, waits=w, inc=sp)
                    eng = "vector" if kc2 == 0 else "scalar"

                    def pte(e, kc2=kc2, eng=eng):
                        pbf = st.pb[5][:, :].bitcast(BF16)
                        for j in range(8):
                            kc = kc2 * 8 + j
                            if eng == "vector":
                                e.tensor_copy(PT[:, kc * 128:(kc + 1) * 128],
                                              pbf[:, j * 128:(j + 1) * 128])
                            else:
                                e.activation(PT[:, kc * 128:(kc + 1) * 128],
                                             pbf[:, j * 128:(j + 1) * 128], AF.Copy)
                        return None
                    pv_bank_free = mk.op(eng, pte, waits=[tev],
                                         inc=sv if eng == "vector" else sa)
                    ptev = pv_bank_free
                w = [ptev, vt_done]
                if attn_done:
                    w.append(attn_done)

                def pvm(e, hb=hb):
                    for kc in range(16):
                        r = e.matmul(st.pb[4][0:64, 0:128],
                                     vT[:, kc * 128 + hb: kc * 128 + hb + 64],
                                     PT[:, kc * 128:(kc + 1) * 128],
                                     start=(kc == 0), stop=(kc == 15),
                                     skip_group_check=True)
                    return r
                pvev = mk.op("tensor", pvm, waits=w, inc=sp)
                attn_done = mk.op("vector", lambda e, hg=hg, hb=hb, qt=qt:
                                  e.tensor_copy(
                                      O_all[hb:hb + 64,
                                            hg * 2048 + qt * 128: hg * 2048 + (qt + 1) * 128],
                                      st.pb[4][0:64, 0:128]),
                                  waits=[pvev], inc=sv)
    done = None
    ev_prev = {}
    for o in range(CH):
        tok = ws.load(lambda slot_t, o=o: (
            slot_t, wd["wout_t"][o * 128:(o + 1) * 128, :]), extra_waits=[attn_done])
        for nt in range(NT):
            i = o * NT + nt
            bank = st.pb[i % 4]
            w = [attn_done, bias_ev]
            if i >= 4:
                w.append(ev_prev[i % 4])

            def mmo(e, tok=tok, nt=nt, bank=bank):
                slot, sem, val = tok
                e.wait_ge(sem.h, val)
                for c in range(CH):
                    r = e.matmul(bank[:, :],
                                 wslot[slot][:, c * 128:(c + 1) * 128],
                                 O_all[:, c * 2048 + nt * 512: c * 2048 + nt * 512 + 512],
                                 start=(c == 0), stop=(c == CH - 1),
                                 skip_group_check=True)
                return r
            mev = mk.op("tensor", mmo, waits=w, inc=sp)
            if nt == NT - 1:
                ws.consumed(tok[0], mev)

            def evac(e, o=o, nt=nt, bank=bank):
                xc = st.x[:, o * T + nt * 512: o * T + nt * 512 + 512]
                e.scalar_tensor_tensor(xc, bank[:, :], bias_c[:, o:o + 1], xc,
                                       ALU.add, ALU.add)
                e.drain()
                return None
            done = mk.op("vector", evac, waits=[mev], inc=sv)
            ev_prev[i % 4] = done
    st.A.pop(); st.B.pop(); st.W.pop(); st.Cf.pop()
    return done



def emit_a0(mk: MK, st: State, xln_ev, wd, sems):
    sv, sa, sp = sems["v"], sems["a"], sems["p"]
    st.A.push(); st.B.push(); st.Cf.push()
    l1 = st.A.alloc(8192)
    h1 = st.A.alloc(8192)
    l2 = st.B.alloc(4096)
    h2 = st.B.alloc(4096)
    q_sb = st.B.alloc(4096)
    att = st.B.alloc(4096)
    filt = st.Cf.alloc(128)
    fcomb = st.Cf.alloc(64)
    denr = st.Cf.alloc(512)
    gb = st.Cf.alloc(16)
    ln2scr = st.Cf.alloc(512)
    var2 = st.Cf.alloc(512)

    ldf = mk.sem()
    for fi, key in enumerate(["hf", "gf", "hs", "gs"]):
        mk.op("gpsimd", lambda e, fi=fi, key=key: e.dma_start(
            out=filt[:, fi * 32:(fi + 1) * 32].rearrange("p (c k) -> p c k", k=4),
            in_=wd[key].rearrange("(c p) k -> p c k", p=128)), waits=[xln_ev],
            inc=ldf, dma=True)
    mk.op("gpsimd", lambda e: e.dma_start(out=gb[:, 0:8], in_=wd["gamma"][:, :]),
          waits=[xln_ev], inc=ldf, dma=True)
    mk.op("gpsimd", lambda e: e.dma_start(out=gb[:, 8:16], in_=wd["beta"][:, :]),
          waits=[xln_ev], inc=ldf, dma=True)
    fev = ldf.ev

    def fc(fi, c, k):
        return filt[:, fi * 32 + c * 4 + k: fi * 32 + c * 4 + k + 1]

    def mkcomb(e):
        for fi2, fi in enumerate((2, 3)):
            for c in range(CH):
                base = fi2 * 24 + c * 3
                e.tensor_tensor(fcomb[:, base:base + 1], fc(fi, c, 1), fc(fi, c, 2), ALU.add)
                e.drain()
                e.tensor_tensor(fcomb[:, base + 1:base + 2], fc(fi, c, 0), fc(fi, c, 1), ALU.add)
                e.drain()
                e.tensor_tensor(fcomb[:, base + 2:base + 3], fc(fi, c, 2), fc(fi, c, 3), ALU.add)
                e.drain()
        return None
    comb_ev = mk.op("vector", mkcomb, waits=[fev], inc=sv)

    def down(e, src_get, dst, dstw, N, fi_l, fi_h):
        Nh = N // 2
        for c in range(CH):
            s = src_get(c)
            for (fi, d) in ((fi_l, dst), (fi_h, dstw)):
                dc = d[:, c * Nh:(c + 1) * Nh]
                e.tensor_scalar(dc, s[:, 0:N:2], fc(fi, c, 1), None, ALU.mult)
                e.drain()
                e.scalar_tensor_tensor(dc, s[:, 1:N:2], fc(fi, c, 2), dc,
                                       ALU.mult, ALU.add)
                e.drain()
                e.scalar_tensor_tensor(dc[:, 1:Nh], s[:, 1:N - 2:2], fc(fi, c, 0),
                                       dc[:, 1:Nh], ALU.mult, ALU.add)
                e.drain()
                e.scalar_tensor_tensor(dc[:, 0:Nh - 1], s[:, 2:N:2], fc(fi, c, 3),
                                       dc[:, 0:Nh - 1], ALU.mult, ALU.add)
                e.drain()
        return None

    ev = mk.op("vector", lambda e: down(
        e, lambda c: st.xln[:, c * T:(c + 1) * T], l1, h1, 2048, 0, 1),
        waits=[xln_ev, fev], inc=sv)
    ev = mk.op("vector", lambda e: down(
        e, lambda c: l1[:, c * 1024:(c + 1) * 1024], l2, h2, 1024, 0, 1),
        waits=[ev], inc=sv)
    lev_ev = ev

    def mkq(e):
        for c in range(CH):
            lc = l2[:, c * 512:(c + 1) * 512]
            e.tensor_scalar(q_sb[:, c * 512:(c + 1) * 512], lc, 0.0, None, ALU.min)
        return None
    ev = mk.op("vector", mkq, waits=[lev_ev], inc=sv)

    def mkq2(e):
        for c in range(CH):
            qc = q_sb[:, c * 512:(c + 1) * 512]
            e.activation(qc, qc, AF.Exp)
        return None
    ev = mk.op("scalar", mkq2, waits=[ev], inc=sa)

    def mkq3(e):
        for c in range(CH):
            lc = l2[:, c * 512:(c + 1) * 512]
            qc = q_sb[:, c * 512:(c + 1) * 512]
            e.tensor_scalar(ln2scr, lc, 0.0, None, ALU.max)
            e.drain()
            e.tensor_tensor(qc, qc, ln2scr, ALU.add)
            e.drain()
        return None
    q_ev = mk.op("vector", mkq3, waits=[ev], inc=sv)

    qTvT = l1[:, 0:512]
    kvs = l1[:, 512:576]
    zbf = l1[:, 576:577]
    numb = l1[:, 640:1152]
    att_done = None
    tp_free = None
    for h in range(16):
        c = h // 2
        hb = (h % 2) * 64
        w = [q_ev]
        if tp_free:
            w.append(tp_free)

        def tr(e, c=c, hb=hb):
            pbf = st.pb[4][:, :].bitcast(BF16)
            idl = st.ident_b[hb:hb + 64, hb:hb + 64]
            for j in range(4):
                e.transpose(pbf[:, j * 64:(j + 1) * 64],
                            q_sb[hb:hb + 64, c * 512 + j * 128: c * 512 + (j + 1) * 128],
                            idl)
            for j in range(4):
                r = e.transpose(pbf[:, 256 + j * 64: 256 + (j + 1) * 64],
                                l2[hb:hb + 64, c * 512 + j * 128: c * 512 + (j + 1) * 128],
                                idl)
            return r
        tev = mk.op("tensor", tr, waits=w, inc=sp)

        def tre(e):
            pbf = st.pb[4][:, :].bitcast(BF16)
            e.tensor_copy(qTvT[:, :], pbf[:, 0:512])
            e.drain()
            return None
        trev = mk.op("vector", tre, waits=[tev], inc=sv)
        tp_free = trev

        def kvm(e):
            for j in range(4):
                r = e.matmul(st.pb[3][0:64, 0:64], qTvT[:, j * 64:(j + 1) * 64],
                             qTvT[:, 256 + j * 64: 256 + (j + 1) * 64],
                             start=(j == 0), stop=(j == 3), skip_group_check=True)
            return r
        kev = mk.op("tensor", kvm, waits=[trev], inc=sp)
        zrow = st.lnscr[hb:hb + 64, 1100:1101]

        def zred(e, c=c, hb=hb):
            e.tensor_reduce(zrow, q_sb[hb:hb + 64, c * 512:(c + 1) * 512],
                            AX.X, ALU.add)
            e.drain()
            e.tensor_copy(zbf[hb:hb + 64, :], zrow)
            e.drain()
            e.tensor_copy(kvs[hb:hb + 64, :], st.pb[3][0:64, 0:64])
            e.drain()
            return None
        zev = mk.op("vector", zred, waits=[kev, q_ev], inc=sv)

        def numm(e, c=c, hb=hb):
            e.matmul(st.pb[0][0:64, :], kvs[hb:hb + 64, :],
                     q_sb[hb:hb + 64, c * 512:(c + 1) * 512],
                     start=True, stop=True, skip_group_check=True)
            return e.matmul(st.pb[1][0:1, :], zbf[hb:hb + 64, :],
                            q_sb[hb:hb + 64, c * 512:(c + 1) * 512],
                            start=True, stop=True, skip_group_check=True)
        nev = mk.op("tensor", numm, waits=[zev], inc=sp)

        def dinv(e):
            e.tensor_scalar(denr[0:1, :], st.pb[1][0:1, :], 1e-6, None, ALU.add)
            e.drain()
            e.reciprocal(denr[0:1, :], denr[0:1, :])
            e.drain()
            e.tensor_copy(numb, st.pb[0][0:128, :])
            e.drain()
            return None
        dev = mk.op("vector", dinv, waits=[nev], inc=sv)
        bev = mk.op("tensor", lambda e: e.matmul(
            st.pb[2][:, :], st.ones_f, denr[0:1, :], start=True, stop=True,
            skip_group_check=True), waits=[dev], inc=sp)
        att_done = mk.op("vector", lambda e, c=c, hb=hb: e.tensor_tensor(
            att[hb:hb + 64, c * 512:(c + 1) * 512], numb[0:64, :],
            st.pb[2][0:64, :], ALU.mult), waits=[bev], inc=sv)

    low1 = l1[:, 0:8192]

    def up(e, srcl, srch, dst, N):
        Nh = N // 2
        for c in range(CH):
            sl = srcl[:, c * Nh:(c + 1) * Nh]
            sh = srch[:, c * Nh:(c + 1) * Nh]
            dc = dst[:, c * N:(c + 1) * N]
            for fi2, s in ((0, sl), (1, sh)):
                cb = lambda j: fcomb[:, fi2 * 24 + c * 3 + j: fi2 * 24 + c * 3 + j + 1]
                f0 = fc(2 + fi2, c, 0)
                f3 = fc(2 + fi2, c, 3)
                if fi2 == 0:
                    e.tensor_scalar(dc[:, 0:N:2], s, cb(0), None, ALU.mult)
                    e.drain()
                    e.tensor_scalar(dc[:, 1:N:2], s, cb(1), None, ALU.mult)
                    e.drain()
                else:
                    e.scalar_tensor_tensor(dc[:, 0:N:2], s, cb(0), dc[:, 0:N:2],
                                           ALU.mult, ALU.add)
                    e.drain()
                    e.scalar_tensor_tensor(dc[:, 1:N:2], s, cb(1), dc[:, 1:N:2],
                                           ALU.mult, ALU.add)
                    e.drain()
                e.scalar_tensor_tensor(dc[:, 2:N:2], s[:, 0:Nh - 1], f0,
                                       dc[:, 2:N:2], ALU.mult, ALU.add)
                e.drain()
                e.scalar_tensor_tensor(dc[:, 0:N - 2:2], s[:, 1:Nh], f3,
                                       dc[:, 0:N - 2:2], ALU.mult, ALU.add)
                e.drain()
                e.scalar_tensor_tensor(dc[:, 1:N - 2:2], s[:, 1:Nh], cb(2),
                                       dc[:, 1:N - 2:2], ALU.mult, ALU.add)
                e.drain()
        return None
    ev = mk.op("vector", lambda e: up(e, att, h2, low1, 1024),
               waits=[att_done, comb_ev], inc=sv)
    low0 = st.B.t[:, 0:16384]
    ev = mk.op("vector", lambda e: up(e, low1, h1, low0, 2048), waits=[ev], inc=sv)

    def mks(e):
        for c in range(CH):
            e.tensor_tensor(low0[:, c * T:(c + 1) * T], low0[:, c * T:(c + 1) * T],
                            st.xln[:, c * T:(c + 1) * T], ALU.add)
            e.drain()
        return None
    s_ev = mk.op("vector", mks, waits=[ev], inc=sv)

    sqb = h1
    prev_stats_done = None
    prev_apply_done = None
    prev_mm_done = None
    done = None
    mean = st.lnscr[0:1, 1104:1616] if False else None
    for nt in range(NT):
        c0 = nt * 512
        mean = st.lnscr[0:1, 512:1024]
        mean = ln2scr[0:1, :]
        rstd = var2[0:1, :]
        w = [s_ev]
        if prev_mm_done:
            w.append(prev_mm_done)

        def conv(e, c0=c0):
            for c in range(CH):
                sc = low0[:, c * T + c0: c * T + c0 + 512]
                r = e.activation(sqb[:, c * 512:(c + 1) * 512], sc, AF.Square)
            return r
        conv_ev = mk.op("scalar", conv, waits=w, inc=sa)
        w = [conv_ev]
        if prev_stats_done:
            w.append(prev_stats_done)

        def mmstat(e, c0=c0):
            for c in range(CH):
                e.matmul(st.pb[4][0:1, :], st.ones_b,
                         low0[:, c * T + c0: c * T + c0 + 512],
                         start=(c == 0), stop=(c == CH - 1), skip_group_check=True)
            for c in range(CH):
                r = e.matmul(st.pb[5][0:1, :], st.ones_b, sqb[:, c * 512:(c + 1) * 512],
                             start=(c == 0), stop=(c == CH - 1), skip_group_check=True)
            return r
        prev_mm_done = mk.op("tensor", mmstat, waits=w, inc=sp)

        def smath(e, mean=mean, rstd=rstd):
            e.tensor_scalar(mean, st.pb[4][0:1, :], 1.0 / D, None, ALU.mult)
            e.drain()
            e.tensor_tensor(rstd, mean, mean, ALU.mult)
            e.drain()
            e.scalar_tensor_tensor(rstd, st.pb[5][0:1, :], 1.0 / D, rstd,
                                   ALU.mult, ALU.subtract)
            e.drain()
            return e.tensor_scalar(rstd, rstd, LN_EPS, None, ALU.add)
        ev = mk.op("vector", smath, waits=[prev_mm_done], inc=sv)
        prev_stats_done = ev
        ev = mk.op("scalar", lambda e, rstd=rstd: e.activation(rstd, rstd, AF.Sqrt),
                   waits=[ev], inc=sa)
        ev = mk.op("vector", lambda e, rstd=rstd: e.reciprocal(rstd, rstd),
                   waits=[ev], inc=sv)
        w = [ev]
        if prev_apply_done:
            w.append(prev_apply_done)

        def bcast(e, mean=mean, rstd=rstd):
            e.matmul(st.pb[0][:, :], st.ones_f, mean, start=True, stop=True,
                     skip_group_check=True)
            return e.matmul(st.pb[1][:, :], st.ones_f, rstd, start=True, stop=True,
                            skip_group_check=True)
        ev = mk.op("tensor", bcast, waits=w, inc=sp)

        def apply(e, c0=c0):
            tfa = st.lnscr[:, 0:512]
            tfb = st.lnscr[:, 512:1024]
            for cp in range(0, CH, 2):
                pair = ((cp, tfa), (cp + 1, tfb))
                for c, t_ in pair:
                    e.tensor_tensor(t_, low0[:, c * T + c0: c * T + c0 + 512],
                                    st.pb[0][:, :], ALU.subtract)
                e.drain()
                for c, t_ in pair:
                    e.tensor_tensor(t_, t_, st.pb[1][:, :], ALU.mult)
                e.drain()
                for c, t_ in pair:
                    e.tensor_scalar(t_, t_, gb[:, c:c + 1], gb[:, 8 + c:9 + c],
                                    ALU.mult, ALU.add)
                e.drain()
                for c, t_ in pair:
                    xc = st.x[:, c * T + c0: c * T + c0 + 512]
                    e.tensor_tensor(xc, xc, t_, ALU.add)
                e.drain()
            return None
        done = mk.op("vector", apply, waits=[ev], inc=sv)
        prev_apply_done = done
    st.A.pop(); st.B.pop(); st.Cf.pop()
    return done






GRID0 = -1.2
GRIDS = 1.4 / 7.0


def emit_f2(mk, st, xln_ev, wd, sems):
    sv, sa, sp = sems["v"], sems["a"], sems["p"]
    st.A.push(); st.B.push(); st.Cf.push(); st.W.push()
    gates = st.A.alloc(2048)
    gbs = st.A.alloc(4096)
    acc = st.A.alloc(4096)
    un = st.A.alloc(4096)
    wslot = [st.W.alloc(2048) for _ in range(4)]
    ws = WStream(mk, wslot, 2048)
    sc0 = st.Cf.alloc(32)
    sc1 = st.Cf.alloc(64)
    b0h = st.Cf.alloc(32)
    b1h = st.Cf.alloc(64)
    b0kc = st.Cf.alloc(64)
    b1kc = st.Cf.alloc(32)
    b1c = st.Cf.alloc(128)
    g0cc = st.Cf.alloc(32)
    g1cc = st.Cf.alloc(64)
    gneg = st.Cf.alloc(8)
    bc0 = st.Cf.alloc(64)
    bc1 = st.Cf.alloc(128)
    r_mu = st.Cf.alloc(512)
    r_rs = st.Cf.alloc(512)
    mrow = r_mu[0:1, :]
    rrow = r_rs[0:1, :]
    m1 = st.lnscr[:, 1088:1089]
    m2 = st.lnscr[:, 1089:1090]
    dn = st.lnscr[:, 1090:1091]
    p8 = st.lnscr[:, 1096:1104]
    msk = st.lnscr[:, 1104:1112]
    gtm = st.lnscr[:, 1112:1120]
    swn = st.Cf.alloc(1)
    gwh = st.f2s[:, 2624:2688]
    gwl = st.f2s[:, 2688:2752]
    xh = un
    xl = acc
    sel4 = st.f2s[:, 64:576]
    sel4k = st.f2s[:, 576:1088]
    b2r_sb = st.f2s[:, 1088:2112]
    tmpk = st.f2s[:, 2112:2624]

    ld0 = mk.sem()
    gw_ = [xln_ev]
    for (dst, key) in ((b1c, "b1"), (g0cc, "g0c"), (b0h, "b0c"),
                       (g1cc, "g1c"), (b1h, "b1cc"), (b0kc, "b0k"),
                       (b1kc, "b1k")):
        mk.op("gpsimd", lambda e, dst=dst, key=key: e.dma_start(
            out=dst[:, :], in_=wd[key][:, :]), waits=gw_, inc=ld0, dma=True)
    mk.op("gpsimd", lambda e: e.dma_start(out=b2r_sb[0:4, :], in_=wd["b2r"][:, :]),
          waits=gw_, inc=ld0, dma=True)
    gw_f = bc1[:, 0:64]
    mk.op("gpsimd", lambda e: e.dma_start(
        out=gw_f.rearrange("p (c e) -> p c e", e=8),
        in_=wd["gw"].rearrange("(c p) e -> p c e", p=128)), waits=gw_, inc=ld0,
        dma=True)
    gp2 = mk.sem()
    ev_g = mk.op("gpsimd", lambda e: e.iota(gneg, [[1, 8]], channel_multiplier=0,
                                            allow_small_or_imprecise_dtypes=True),
                 waits=[st.init_ev, xln_ev], inc=gp2)

    def mkconsts(e):
        e.tensor_scalar(gneg, gneg, -0.5 * GRIDS, -0.5 * GRID0, ALU.mult, ALU.add)
        e.drain()
        e.tensor_scalar(sc0, g0cc, 0.5, None, ALU.mult)
        e.drain()
        e.tensor_scalar(sc1, g1cc, 0.5, None, ALU.mult)
        e.drain()
        e.tensor_scalar(b0h, b0h, 0.5, None, ALU.mult)
        e.drain()
        e.tensor_scalar(b1h, b1h, 0.5, None, ALU.mult)
        e.drain()
        return None
    cst_ev = mk.op("vector", mkconsts, waits=[ld0.ev, ev_g], inc=sv)
    whl_ev = mk.op("scalar", lambda e: e.activation(gwh, gw_f, AF.Copy),
                   waits=[ld0.ev], inc=sa)
    whl_ev = mk.op("vector", lambda e: e.tensor_tensor(gwl, gw_f, gwh, ALU.subtract),
                   waits=[whl_ev], inc=sv)

    def swmm(e):
        for c in range(CH):
            e.matmul(st.pb[3][0:8, 0:1], gwh[:, c * 8:(c + 1) * 8], st.ones_b,
                     start=(c == 0), stop=False, skip_group_check=True)
        for c in range(CH):
            r = e.matmul(st.pb[3][0:8, 0:1], gwl[:, c * 8:(c + 1) * 8], st.ones_b,
                         start=False, stop=(c == CH - 1), skip_group_check=True)
        return r
    swev = mk.op("tensor", swmm, waits=[whl_ev, st.init_ev], inc=sp)
    swn_ev = mk.op("vector", lambda e: e.tensor_scalar(
        swn[0:8, :], st.pb[3][0:8, 0:1], -1.0, None, ALU.mult), waits=[swev], inc=sv)

    def mksel(e):
        e.memset(sel4, 1.0)
        e.drain()
        for ex in range(4):
            e.affine_select(sel4[:, ex * 128:(ex + 1) * 128],
                            sel4[:, ex * 128:(ex + 1) * 128],
                            [[0, 128]], ALU.is_equal, 0.0, base=-ex,
                            channel_multiplier=1)
            e.drain()
        e.memset(sel4k, 1.0)
        e.drain()
        for ex in range(4):
            e.affine_select(sel4k[:, ex * 128:(ex + 1) * 128],
                            sel4k[:, ex * 128:(ex + 1) * 128],
                            [[0, 128]], ALU.is_equal, 0.0, base=-(32 + ex),
                            channel_multiplier=1)
            e.drain()
        return None
    sel_ev = mk.op("gpsimd", mksel, waits=[st.init_ev] + gw_, inc=gp2)

    gates_ev = None
    prev_pb3 = None
    prev_split = None
    for nt in range(NT):
        c0r = nt * 512

        def xsplit(e, c0r=c0r):
            for c in range(CH):
                r = e.activation(xh[:, c * 512:(c + 1) * 512],
                                 st.x[:, c * T + c0r: c * T + c0r + 512], AF.Copy)
            return r
        w = [xln_ev]
        if prev_split:
            w.append(prev_split)
        ev_h = mk.op("scalar", xsplit, waits=w, inc=sa)

        def xsplit2(e, c0r=c0r):
            for c in range(CH):
                e.tensor_tensor(xl[:, c * 512:(c + 1) * 512],
                                st.x[:, c * T + c0r: c * T + c0r + 512],
                                xh[:, c * 512:(c + 1) * 512], ALU.subtract)
                e.drain()
            return None
        ev_l = mk.op("vector", xsplit2, waits=[ev_h], inc=sv)

        def lgmm(e):
            first = True
            for (wop, xop) in ((gwh, xh), (gwl, xh), (gwh, xl)):
                for c in range(CH):
                    r = e.matmul(st.pb[2][0:8, :], wop[:, c * 8:(c + 1) * 8],
                                 xop[:, c * 512:(c + 1) * 512],
                                 start=first,
                                 stop=(wop is gwh and xop is xl and c == CH - 1),
                                 skip_group_check=True)
                    first = False
            return r
        lev = mk.op("tensor", lgmm, waits=[ev_l, swn_ev], inc=sp)
        prev_split = lev
        lg_sb = r_mu
        eev = mk.op("vector", lambda e: e.tensor_copy(lg_sb[0:8, :], st.pb[2][0:8, :]),
                    waits=[lev, cst_ev], inc=sv)

        def lbc(e, nt=nt):
            e.matmul(st.pb[0][0:8, :], st.ones_f[0:1, 0:8], st.mean_nt(nt),
                     start=True, stop=True, skip_group_check=True)
            return e.matmul(st.pb[1][0:8, :], st.ones_f32[0:1, 0:8], st.rstd_nt(nt),
                            start=True, stop=True, skip_group_check=True)
        bev2 = mk.op("tensor", lbc, waits=[eev], inc=sp)

        def lcorr(e):
            e.scalar_tensor_tensor(lg_sb[0:8, :], st.pb[0][0:8, :], swn[0:8, :],
                                   lg_sb[0:8, :], ALU.mult, ALU.add)
            e.drain()
            e.tensor_tensor(lg_sb[0:8, :], lg_sb[0:8, :], st.pb[1][0:8, :], ALU.mult)
            e.drain()
            return None
        eev = mk.op("vector", lcorr, waits=[bev2, swn_ev], inc=sv)
        w = [eev]
        if prev_pb3:
            w.append(prev_pb3)

        def ltr2(e):
            for gq in range(4):
                r = e.transpose(st.pb[3][:, gq * 8:(gq + 1) * 8],
                                lg_sb[0:8, gq * 128:(gq + 1) * 128],
                                st.ident_f[0:8, 0:8])
            return r
        ev = mk.op("tensor", ltr2, waits=w, inc=sp)
        for gq in range(4):
            tile = st.pb[3][:, gq * 8:(gq + 1) * 8]

            def gtop2(e, tile=tile):
                e.tensor_reduce(m1, tile, AX.X, ALU.max)
                e.drain()
                e.tensor_scalar(msk, tile, m1, None, ALU.is_equal)
                e.drain()
                e.scalar_tensor_tensor(msk, msk, -1e30, tile, ALU.mult, ALU.add)
                e.drain()
                e.tensor_reduce(m2, msk, AX.X, ALU.max)
                e.drain()
                e.tensor_scalar(dn, m2, m1, None, ALU.subtract)
                e.drain()
                return None
            ev = mk.op("vector", gtop2, waits=[ev], inc=sv)
            ev = mk.op("scalar", lambda e: e.activation(dn, dn, AF.Exp),
                       waits=[ev], inc=sa)

            def gm1n(e):
                e.tensor_scalar(dn, dn, 1.0, None, ALU.add)
                e.drain()
                e.reciprocal(dn, dn)
                e.drain()
                e.tensor_scalar(m1, m1, -1.0, None, ALU.mult)
                e.drain()
                return None
            ev = mk.op("vector", gm1n, waits=[ev], inc=sv)
            ev = mk.op("scalar", lambda e, tile=tile: e.activation(
                p8, tile, AF.Exp, bias=m1), waits=[ev], inc=sa)

            def ggate(e, tile=tile):
                e.tensor_scalar(msk, tile, m2, None, ALU.is_ge)
                e.drain()
                e.tensor_tensor(msk, msk, p8, ALU.mult)
                e.drain()
                e.tensor_scalar(gtm, msk, dn, None, ALU.mult)
                e.drain()
                return None
            ev = mk.op("vector", ggate, waits=[ev], inc=sv)

            def gtr(e):
                e.transpose(st.pb[4][0:4, 0:128], gtm[:, 0:4], st.ident_f)
                return e.transpose(st.pb[5][0:4, 0:128], gtm[:, 4:8], st.ident_f)
            ev = mk.op("tensor", gtr, waits=[ev], inc=sp)

            def gev(e, gq=gq, nt=nt):
                e.tensor_copy(gates[0:4, nt * 512 + gq * 128: nt * 512 + (gq + 1) * 128],
                              st.pb[4][0:4, 0:4].rearrange("p n -> p n") if False
                              else st.pb[4][0:4, 0:4])
                e.drain()
                return None
            def gev2(e, gq=gq, nt=nt):
                e.tensor_copy(gates[0:4, nt * 512 + gq * 128: nt * 512 + (gq + 1) * 128],
                              st.pb[4][0:4, 0:128])
                e.drain()
                e.tensor_copy(gates[32:36, nt * 512 + gq * 128: nt * 512 + (gq + 1) * 128],
                              st.pb[5][0:4, 0:128])
                e.drain()
                return None
            ev = mk.op("vector", gev2, waits=[ev], inc=sv)
        prev_pb3 = ev
        gates_ev = ev

    done = None
    banks_free = None
    for nt in range(NT):
        c0 = nt * 512
        ev_gb = gates_ev
        for ex in range(8):
            base = 0 if ex < 4 else 32
            e2 = ex % 4
            w = [ev_gb, sel_ev]
            if banks_free:
                w.append(banks_free)
            ev_mm = mk.op("tensor", lambda e, e2=e2, base=base, nt=nt: e.matmul(
                st.pb[6][:, :],
                (sel4[0:4, e2 * 128:(e2 + 1) * 128] if base == 0
                 else sel4k[32:36, e2 * 128:(e2 + 1) * 128]),
                gates[base:base + 4, nt * 512:(nt + 1) * 512],
                start=True, stop=True, skip_group_check=True), waits=w, inc=sp)
            ev_gb = mk.op("vector", lambda e, ex=ex: e.tensor_copy(
                gbs[:, ex * 512:(ex + 1) * 512], st.pb[6][:, :]),
                waits=[ev_mm], inc=sv)
        gbs_ev = ev_gb

        h_sb = st.B.t[:, 0:16384]
        acc_ev = None
        prev_w2 = None
        for e4 in range(4):
            h_evs = []
            for og in range(4):
                toks = []
                for obh in range(4):
                    r0 = (e4 * 32 + og * 8 + obh * 2) * 128
                    toks.append(ws.load(lambda s_, r0=r0: (
                        s_.rearrange("p (b n) -> p b n", n=1024),
                        wd["w1t"][r0:r0 + 256, :].rearrange("(b p) n -> p b n", p=128)),
                        extra_waits=[xln_ev]))
                w = [gbs_ev]
                if og == 0 and prev_w2:
                    w.append(prev_w2)
                if h_evs:
                    w.append(h_evs[-1])

                def hmm(e, toks=toks, c0=c0):
                    r = None
                    for obh in range(4):
                        slot, sem, val = toks[obh]
                        e.wait_ge(sem.h, val)
                        for o2 in range(2):
                            bank = obh * 2 + o2
                            for c in range(CH):
                                r = e.matmul(
                                    st.pb[bank][:, :],
                                    wslot[slot][:, o2 * 1024 + c * 128:
                                                o2 * 1024 + (c + 1) * 128],
                                    st.xln[:, c * T + c0: c * T + c0 + 512],
                                    start=(c == 0), stop=(c == CH - 1),
                                    skip_group_check=True)
                    return r
                hme = mk.op("tensor", hmm, waits=w, inc=sp)
                for t_ in toks:
                    ws.consumed(t_[0], hme)

                def hevac(e, og=og, e4=e4):
                    for b8 in range(8):
                        ob = og * 8 + b8
                        e.activation(h_sb[:, ob * 512:(ob + 1) * 512],
                                     st.pb[b8][:, :], AF.Gelu,
                                     bias=b1c[:, e4 * 32 + ob: e4 * 32 + ob + 1])
                    return None
                hev = mk.op("scalar", hevac, waits=[hme, cst_ev], inc=sa)

                def hgate(e, og=og, e4=e4):
                    for b8 in range(8):
                        ob = og * 8 + b8
                        e.tensor_tensor(h_sb[:, ob * 512:(ob + 1) * 512],
                                        h_sb[:, ob * 512:(ob + 1) * 512],
                                        gbs[:, e4 * 512:(e4 + 1) * 512], ALU.mult)
                    return None
                hev = mk.op("vector", hgate, waits=[hev], inc=sv)
                h_evs.append(hev)
            h_done = h_evs[-1]
            w2e = None
            for obq in range(4):
                toks2 = []
                for o2 in range(2):
                    ob = obq * 2 + o2
                    r0 = (e4 * 8 + ob) * 128
                    for half in range(2):
                        toks2.append(ws.load(lambda s_, r0=r0, half=half: (
                            s_, wd["w2t"][r0:r0 + 128, half * 2048:(half + 1) * 2048]),
                            extra_waits=[h_done]))

                def w2mm(e, toks2=toks2, e4=e4, nt=nt, obq=obq):
                    r = None
                    for o2 in range(2):
                        ob = obq * 2 + o2
                        for half in range(2):
                            slot, sem, val = toks2[o2 * 2 + half]
                            e.wait_ge(sem.h, val)
                            for fc in range(16):
                                fcg = half * 16 + fc
                                last = (half == 1 and fc == 15 and e4 != 0)
                                r = e.matmul(st.pb[ob][:, :],
                                             wslot[slot][:, fc * 128:(fc + 1) * 128],
                                             h_sb[:, fcg * 512:(fcg + 1) * 512],
                                             start=(half == 0 and fc == 0), stop=last,
                                             skip_group_check=True)
                        if e4 == 0:
                            r = e.matmul(st.pb[ob][:, :],
                                         b2r_sb[0:4, ob * 128:(ob + 1) * 128],
                                         gates[0:4, nt * 512:(nt + 1) * 512],
                                         start=False, stop=True,
                                         skip_group_check=True)
                    return r
                w = [h_done]
                if w2e:
                    w.append(w2e)
                w2e = mk.op("tensor", w2mm, waits=w, inc=sp)
                for t_ in toks2:
                    ws.consumed(t_[0], w2e)
            prev_w2 = w2e

            def accup(e, e4=e4):
                for ob in range(8):
                    if e4 == 0:
                        e.tensor_copy(acc[:, ob * 512:(ob + 1) * 512], st.pb[ob][:, :])
                    else:
                        e.tensor_tensor(acc[:, ob * 512:(ob + 1) * 512],
                                        acc[:, ob * 512:(ob + 1) * 512],
                                        st.pb[ob][:, :], ALU.add)
                return None
            acc_ev = mk.op("vector", accup, waits=[w2e] + ([acc_ev] if acc_ev else []),
                           inc=sv)
            prev_w2 = acc_ev

        sqr = st.B.t[:, 0:4096]

        def unsq(e, c0=c0):
            for c in range(CH):
                r = e.activation(sqr[:, c * 512:(c + 1) * 512],
                                 st.xln[:, c * T + c0: c * T + c0 + 512], AF.Square)
            return r
        uev = mk.op("scalar", unsq, waits=[acc_ev], inc=sa)

        def unstat(e, c0=c0):
            for c in range(CH):
                e.matmul(st.pb[4][0:1, :], st.ones_b,
                         st.xln[:, c * T + c0: c * T + c0 + 512],
                         start=(c == 0), stop=(c == CH - 1), skip_group_check=True)
            for c in range(CH):
                r = e.matmul(st.pb[5][0:1, :], st.ones_b, sqr[:, c * 512:(c + 1) * 512],
                             start=(c == 0), stop=(c == CH - 1), skip_group_check=True)
            return r
        uev = mk.op("tensor", unstat, waits=[uev, acc_ev], inc=sp)

        def unsmath(e):
            e.tensor_scalar(mrow, st.pb[4][0:1, :], 1.0 / D, None, ALU.mult)
            e.drain()
            e.tensor_tensor(rrow, mrow, mrow, ALU.mult)
            e.drain()
            e.scalar_tensor_tensor(rrow, st.pb[5][0:1, :], 1.0 / D, rrow,
                                   ALU.mult, ALU.subtract)
            e.drain()
            e.tensor_scalar(rrow, rrow, LN_EPS, None, ALU.add)
            e.drain()
            return None
        uev = mk.op("vector", unsmath, waits=[uev], inc=sv)
        uev = mk.op("scalar", lambda e: e.activation(rrow, rrow, AF.Sqrt),
                    waits=[uev], inc=sa)
        uev = mk.op("vector", lambda e: e.reciprocal(rrow, rrow), waits=[uev], inc=sv)

        def unbc(e):
            e.matmul(st.pb[0][:, :], st.ones_f, mrow, start=True, stop=True,
                     skip_group_check=True)
            return e.matmul(st.pb[1][:, :], st.ones_f, rrow, start=True, stop=True,
                            skip_group_check=True)
        uev = mk.op("tensor", unbc, waits=[uev], inc=sp)

        def unapply(e, c0=c0):
            for c in range(CH):
                uc = un[:, c * 512:(c + 1) * 512]
                e.tensor_tensor(uc, st.xln[:, c * T + c0: c * T + c0 + 512],
                                st.pb[0][:, :], ALU.subtract)
            e.drain()
            for c in range(CH):
                uc = un[:, c * 512:(c + 1) * 512]
                e.tensor_tensor(uc, uc, st.pb[1][:, :], ALU.mult)
            return None
        un_ev = mk.op("vector", unapply, waits=[uev], inc=sv)

        for e2 in range(4):
            def mkbias(e, e2=e2):
                for c in range(CH):
                    e.tensor_scalar(bc0[:, c * 8:(c + 1) * 8], gneg,
                                    b0h[:, e2 * 8 + c: e2 * 8 + c + 1], None, ALU.add)
                    e.drain()
                for fc in range(16):
                    e.tensor_scalar(bc1[:, fc * 8:(fc + 1) * 8], gneg,
                                    b1h[:, e2 * 16 + fc: e2 * 16 + fc + 1], None,
                                    ALU.add)
                    e.drain()
                return None
            bias_ev = mk.op("vector", mkbias, waits=[un_ev], inc=sv)

            sqslot = [st.B.t[:, 0:4096], st.B.t[:, 4096:8192]]
            zn = st.B.t[:, 8192:16384]
            sq_free = [None, None]
            z_evs = []
            for mg in range(2):
                mm_last = None
                for g in range(8):
                    sl = sqslot[g % 2]
                    w = [bias_ev]
                    if sq_free[g % 2]:
                        w.append(sq_free[g % 2])

                    def mktanh(e, sl=sl, g=g, e2=e2):
                        for c in range(CH):
                            e.activation(sl[:, c * 512:(c + 1) * 512],
                                         un[:, c * 512:(c + 1) * 512], AF.Tanh,
                                         bias=bc0[:, c * 8 + g: c * 8 + g + 1],
                                         scale=sc0[:, e2 * 8 + c: e2 * 8 + c + 1])
                        return None
                    tev = mk.op("scalar", mktanh, waits=w, inc=sa)

                    def mkbasis(e, sl=sl):
                        for c in range(CH):
                            s_ = sl[:, c * 512:(c + 1) * 512]
                            e.tensor_tensor(s_, s_, s_, ALU.mult)
                        e.drain()
                        for c in range(CH):
                            s_ = sl[:, c * 512:(c + 1) * 512]
                            e.tensor_scalar(s_, s_, -1.0, 1.0, ALU.mult, ALU.add)
                        return None
                    bev = mk.op("vector", mkbasis, waits=[tev], inc=sv)
                    toks = []
                    for obh in range(4):
                        r0 = ((e2 * 8 + g) * 16 + mg * 8 + obh * 2) * 128
                        toks.append(ws.load(lambda s_, r0=r0: (
                            s_.rearrange("p (b n) -> p b n", n=1024),
                            wd["w0t"][r0:r0 + 256, :].rearrange("(b p) n -> p b n",
                                                               p=128)),
                            extra_waits=[bev]))

                    def zmm(e, toks=toks, sl=sl, g=g):
                        r = None
                        for obh in range(4):
                            slot, sem, val = toks[obh]
                            e.wait_ge(sem.h, val)
                            for o2 in range(2):
                                bank = obh * 2 + o2
                                for hc in range(CH):
                                    r = e.matmul(
                                        st.pb[bank][:, :],
                                        wslot[slot][:, o2 * 1024 + hc * 128:
                                                    o2 * 1024 + (hc + 1) * 128],
                                        sl[:, hc * 512:(hc + 1) * 512],
                                        start=(g == 0 and hc == 0),
                                        stop=(g == 7 and hc == CH - 1),
                                        skip_group_check=True)
                        return r
                    w = [bev]
                    if mm_last:
                        w.append(mm_last)
                    if mg == 1 and g == 0:
                        w.append(z_evs[0])
                    mm_last = mk.op("tensor", zmm, waits=w, inc=sp)
                    for t_ in toks:
                        ws.consumed(t_[0], mm_last)
                    sq_free[g % 2] = mm_last

                def zevac(e, mg=mg, e2=e2):
                    for b8 in range(8):
                        ob = mg * 8 + b8
                        e.tensor_scalar(zn[:, ob * 512:(ob + 1) * 512],
                                        st.pb[b8][:, :],
                                        b0kc[:, e2 * 16 + ob: e2 * 16 + ob + 1],
                                        None, ALU.add)
                    return None
                zev = mk.op("vector", zevac, waits=[mm_last], inc=sv)
                z_evs.append(zev)
            z_done = z_evs[-1]

            def zsq(e):
                for zc in range(16):
                    r = e.activation(sqslot[zc % 2][:, (zc // 2 % 4) * 512:
                                                    (zc // 2 % 4) * 512 + 512],
                                     zn[:, zc * 512:(zc + 1) * 512], AF.Square) \
                        if False else None
                for zc in range(16):
                    dstc = st.B.t[:, (zc % 8) * 512:(zc % 8) * 512 + 512]
                    r = e.activation(dstc, zn[:, zc * 512:(zc + 1) * 512], AF.Square)
                    if zc == 7:
                        e.drain()
                return r
            zstat_ev = None

            def zhalf(e, h0):
                for zc in range(8):
                    dstc = st.B.t[:, zc * 512:(zc + 1) * 512]
                    e.activation(dstc, zn[:, (h0 + zc) * 512:(h0 + zc + 1) * 512],
                                 AF.Square)
                return None
            sev = mk.op("scalar", lambda e: zhalf(e, 0), waits=[z_done], inc=sa)

            def zst1(e):
                for zc in range(8):
                    e.matmul(st.pb[4][0:1, :], st.ones_b,
                             zn[:, zc * 512:(zc + 1) * 512],
                             start=(zc == 0), stop=False, skip_group_check=True)
                for zc in range(8):
                    r = e.matmul(st.pb[5][0:1, :], st.ones_b,
                                 st.B.t[:, zc * 512:(zc + 1) * 512],
                                 start=(zc == 0), stop=False, skip_group_check=True)
                return r
            sev = mk.op("tensor", zst1, waits=[sev], inc=sp)
            sev2 = mk.op("scalar", lambda e: zhalf(e, 8), waits=[sev], inc=sa)

            def zst2(e):
                for zc in range(8):
                    e.matmul(st.pb[4][0:1, :], st.ones_b,
                             zn[:, (8 + zc) * 512:(9 + zc) * 512],
                             start=False, stop=(zc == 7), skip_group_check=True)
                for zc in range(8):
                    r = e.matmul(st.pb[5][0:1, :], st.ones_b,
                                 st.B.t[:, zc * 512:(zc + 1) * 512],
                                 start=False, stop=(zc == 7), skip_group_check=True)
                return r
            sev = mk.op("tensor", zst2, waits=[sev2], inc=sp)

            def zsmath(e):
                e.tensor_scalar(mrow, st.pb[4][0:1, :], 1.0 / 2048.0, None, ALU.mult)
                e.drain()
                e.tensor_tensor(rrow, mrow, mrow, ALU.mult)
                e.drain()
                e.scalar_tensor_tensor(rrow, st.pb[5][0:1, :], 1.0 / 2048.0, rrow,
                                       ALU.mult, ALU.subtract)
                e.drain()
                e.tensor_scalar(rrow, rrow, LN_EPS, None, ALU.add)
                e.drain()
                return None
            sev = mk.op("vector", zsmath, waits=[sev], inc=sv)
            sev = mk.op("scalar", lambda e: e.activation(rrow, rrow, AF.Sqrt),
                        waits=[sev], inc=sa)
            sev = mk.op("vector", lambda e: e.reciprocal(rrow, rrow),
                        waits=[sev], inc=sv)

            def zbc(e):
                e.matmul(st.pb[0][:, :], st.ones_f, mrow, start=True, stop=True,
                         skip_group_check=True)
                return e.matmul(st.pb[1][:, :], st.ones_f, rrow, start=True,
                                stop=True, skip_group_check=True)
            sev = mk.op("tensor", zbc, waits=[sev], inc=sp)

            def znorm(e):
                for zc in range(16):
                    zc_ = zn[:, zc * 512:(zc + 1) * 512]
                    e.tensor_tensor(zc_, zc_, st.pb[0][:, :], ALU.subtract)
                e.drain()
                for zc in range(16):
                    zc_ = zn[:, zc * 512:(zc + 1) * 512]
                    e.tensor_tensor(zc_, zc_, st.pb[1][:, :], ALU.mult)
                return None
            zn_ev = mk.op("vector", znorm, waits=[sev], inc=sv)

            mm_last = None
            sqh_free = [None, None]
            for g in range(8):
                half_evs = []
                for hh in range(2):
                    sl = sqslot[hh]
                    w = [zn_ev]
                    if sqh_free[hh]:
                        w.append(sqh_free[hh])

                    def ktanh(e, sl=sl, g=g, e2=e2, hh=hh):
                        for fc8 in range(8):
                            fc = hh * 8 + fc8
                            e.activation(sl[:, fc8 * 512:(fc8 + 1) * 512],
                                         zn[:, fc * 512:(fc + 1) * 512], AF.Tanh,
                                         bias=bc1[:, fc * 8 + g: fc * 8 + g + 1],
                                         scale=sc1[:, e2 * 16 + fc: e2 * 16 + fc + 1])
                        return None
                    tev = mk.op("scalar", ktanh, waits=w, inc=sa)

                    def kbasis(e, sl=sl):
                        for fc8 in range(8):
                            s_ = sl[:, fc8 * 512:(fc8 + 1) * 512]
                            e.tensor_tensor(s_, s_, s_, ALU.mult)
                        e.drain()
                        for fc8 in range(8):
                            s_ = sl[:, fc8 * 512:(fc8 + 1) * 512]
                            e.tensor_scalar(s_, s_, -1.0, 1.0, ALU.mult, ALU.add)
                        return None
                    half_evs.append(mk.op("vector", kbasis, waits=[tev], inc=sv))
                for ob in range(8):
                    r0 = ((e2 * 8 + g) * 8 + ob) * 128
                    tok = ws.load(lambda s_, r0=r0: (
                        s_, wd["w1kt"][r0:r0 + 128, :]),
                        extra_waits=[half_evs[1]])

                    def kmm(e, tok=tok, ob=ob, g=g):
                        slot, sem, val = tok
                        e.wait_ge(sem.h, val)
                        r = None
                        for ic in range(16):
                            sl = sqslot[ic // 8]
                            r = e.matmul(st.pb[ob][:, :],
                                         wslot[slot][:, ic * 128:(ic + 1) * 128],
                                         sl[:, (ic % 8) * 512:(ic % 8 + 1) * 512],
                                         start=(g == 0 and ic == 0),
                                         stop=(g == 7 and ic == 15),
                                         skip_group_check=True)
                        return r
                    w = [half_evs[1]]
                    if mm_last:
                        w.append(mm_last)
                    mm_last = mk.op("tensor", kmm, waits=w, inc=sp)
                    ws.consumed(tok[0], mm_last)
                sqh_free[0] = mm_last
                sqh_free[1] = mm_last

            def kevac(e, e2=e2):
                tmpk2 = gbs[:, 0:512]
                for obp in range(0, 8, 2):
                    for ob, tk in ((obp, tmpk), (obp + 1, tmpk2)):
                        e.scalar_tensor_tensor(
                            tk, st.pb[ob][:, :],
                            b1kc[:, e2 * 8 + ob: e2 * 8 + ob + 1],
                            gbs[:, (4 + e2) * 512:(5 + e2) * 512],
                            ALU.add, ALU.mult)
                    e.drain()
                    for ob, tk in ((obp, tmpk), (obp + 1, tmpk2)):
                        e.tensor_tensor(acc[:, ob * 512:(ob + 1) * 512],
                                        acc[:, ob * 512:(ob + 1) * 512], tk, ALU.add)
                    e.drain()
                return None
            acc_ev = mk.op("vector", kevac, waits=[mm_last], inc=sv)
            un_ev = acc_ev

        def xup(e, c0=c0):
            for ob in range(8):
                xc = st.x[:, ob * T + c0: ob * T + c0 + 512]
                e.scalar_tensor_tensor(xc, acc[:, ob * 512:(ob + 1) * 512], 0.5,
                                       xc, ALU.mult, ALU.add)
            return None
        done = mk.op("vector", xup, waits=[acc_ev], inc=sv)
        banks_free = done

    st.A.pop(); st.B.pop(); st.Cf.pop(); st.W.pop()
    return done




WIN = 512
NW = 4


def emit_f3(mk, st, xln_ev, wd, sems, phases=4):
    sv, sa, sp = sems["v"], sems["a"], sems["p"]
    st.A.push(); st.B.push(); st.Cf.push(); st.W.push()
    y = st.A.alloc(16384)
    qw = st.B.t[:, 0:2048]
    kw = [st.B.t[:, 2048:4096], st.B.t[:, 4096:6144]]
    vT = [st.B.t[:, 6144:8192], st.B.t[:, 8192:10240]]
    tabs = st.B.t[:, 10240:12288]
    Ow = st.B.t[:, 12288:14336]
    hshw = st.B.t[:, 14336:16384]
    P_sb = st.W.alloc(1024)
    PT = st.W.alloc(1024)
    sqt = st.W.alloc(512)
    wslot = [st.W.alloc(1408) for _ in range(4)]
    ws = WStream(mk, wslot, 1408)
    Mb = st.f2s[:, 0:1408]
    Tm = st.f2s[:, 1408:2304]
    hsel = st.Cf.alloc(2)
    s2 = st.Cf.alloc(128)
    Rm = st.cbf[:, 320:448]
    ybnd = st.f2s[:, 2434:2446]
    lnb_c = st.Cf.alloc(1)
    r2 = st.Cf.alloc(512)
    ptmp = st.Cf.alloc(512)
    ptm2 = st.Cf.alloc(512)
    qs_c = st.Cf.alloc(2)
    ks_c = st.Cf.alloc(2)
    lng_c = st.Cf.alloc(16)
    lnb2_c = st.Cf.alloc(16)
    b1a_c = st.Cf.alloc(44)
    b1g_c = st.Cf.alloc(44)
    b2_c = st.Cf.alloc(16)
    stm = st.lnscr[:, 1088:1104]
    m1 = stm[:, 0:1]
    m1s = stm[:, 1:2]
    den = stm[:, 2:3]
    rec = stm[:, 3:4]
    smu = st.stat[0:1, 0:512]
    srs = st.stat[0:1, 512:1024]

    ld0 = mk.sem()
    for (dst, key) in ((qs_c, "qs"), (ks_c, "ks"), (lng_c, "lng"), (lnb2_c, "lnb"),
                       (b1a_c, "b1a"), (b1g_c, "b1g"), (b2_c, "b2c3")):
        mk.op("gpsimd", lambda e, dst=dst, key=key: e.dma_start(
            out=dst[:, :], in_=wd[key][:, :]), waits=[xln_ev], inc=ld0, dma=True)
    gpp = mk.sem()

    def mkmasks(e):
        e.memset(Mb, 0.0)
        e.drain()
        e.affine_select(Mb, Mb, [[1, 1408]], ALU.is_ge, -1e30, base=-384,
                        channel_multiplier=-1)
        e.drain()
        e.affine_select(Mb, Mb, [[-1, 1408]], ALU.is_ge, -1e30, base=896,
                        channel_multiplier=1)
        e.drain()
        e.memset(Tm, 0.0)
        e.drain()
        e.affine_select(Tm, Tm, [[-1, 896]], ALU.is_ge, -1e30, base=384,
                        channel_multiplier=1)
        e.drain()
        e.memset(hsel, 1.0)
        e.drain()
        e.affine_select(hsel, hsel, [[-128, 2]], ALU.is_ge, 0.0, base=64,
                        channel_multiplier=1)
        e.drain()
        e.affine_select(hsel, hsel, [[128, 2]], ALU.is_ge, 0.0, base=63,
                        channel_multiplier=-1)
        e.drain()
        e.memset(s2, 1.0)
        e.drain()
        e.affine_select(s2, s2, [[1, 128]], ALU.is_ge, 0.0, base=64,
                        channel_multiplier=-128)
        e.drain()
        e.affine_select(s2, s2, [[-1, 128]], ALU.is_ge, 0.0, base=63,
                        channel_multiplier=128)
        e.drain()
        return e.iota(lnb_c, [[0, 1]], channel_multiplier=1,
                      allow_small_or_imprecise_dtypes=True)
    gev = mk.op("gpsimd", mkmasks, waits=[st.init_ev, xln_ev], inc=gpp)

    def mklnb(e):
        e.tensor_scalar(ptmp[:, 0:1], lnb_c, 64.0, None, ALU.is_ge)
        e.drain()
        e.scalar_tensor_tensor(lnb_c, ptmp[:, 0:1], -64.0, lnb_c, ALU.mult, ALU.add)
        e.drain()
        e.tensor_scalar(ptmp[:, 0:1], lnb_c, 32.0, None, ALU.is_ge)
        e.drain()
        e.scalar_tensor_tensor(lnb_c, ptmp[:, 0:1], -32.0, lnb_c, ALU.mult, ALU.add)
        e.drain()
        e.tensor_scalar(lnb_c, lnb_c, 2.0 / 89.6, 25.6 / 89.6, ALU.mult, ALU.add)
        e.drain()
        return None
    lev = mk.op("vector", mklnb, waits=[gev], inc=sv)
    lev = mk.op("scalar", lambda e: e.activation(lnb_c, lnb_c, AF.Ln),
                waits=[lev], inc=sa)
    pa = Arena(st.Cf.t, 2048)
    pa.pos = 513
    perm_ev = emit_rot_perm(mk, st, Rm, "half64", sems, pa, ready=lev)
    consts_ev = perm_ev

    invf = st.lnscr[:, 1105:1106]
    pmr = st.lnscr[:, 1106:1107]
    gev2 = mk.op("gpsimd", lambda e: e.iota(invf, [[0, 1]], channel_multiplier=1,
                                            allow_small_or_imprecise_dtypes=True),
                 waits=[gev], inc=gpp)

    def mkinvf(e):
        e.tensor_scalar(pmr, invf, 64.0, None, ALU.is_ge)
        e.drain()
        e.scalar_tensor_tensor(pmr, pmr, -64.0, invf, ALU.mult, ALU.add)
        e.drain()
        e.tensor_scalar(ptm2[:, 0:1], pmr, 32.0, None, ALU.is_ge)
        e.drain()
        e.scalar_tensor_tensor(pmr, ptm2[:, 0:1], -32.0, pmr, ALU.mult, ALU.add)
        e.drain()
        return None
    iev = mk.op("vector", mkinvf, waits=[gev2], inc=sv)
    iev = mk.op("scalar", lambda e: e.activation(
        invf, pmr, AF.Exp, scale=-2.0 * float(np.log(10000.0)) / 64.0),
        waits=[iev], inc=sa)
    consts2_ev = iev

    def ycopy(e):
        for c in range(CH):
            e.tensor_copy(y[:, c * T:(c + 1) * T], st.xln[:, c * T:(c + 1) * T])
        return None
    y_ev = mk.op("vector", ycopy, waits=[xln_ev, ld0.ev], inc=sv)

    done = None
    phase_n = 0
    for l in range(2):
        if phase_n >= phases:
            break
        def savebnd(e):
            for c4 in range(4):
                c = 4 + c4
                for wq in range(3):
                    e.tensor_copy(ybnd[:, c4 * 3 + wq: c4 * 3 + wq + 1],
                                  y[:, c * T + (wq + 1) * 512 - 1:
                                    c * T + (wq + 1) * 512])
            return None
        bnd_ev = mk.op("vector", savebnd, waits=[y_ev], inc=sv)
        prev_qkv_banks = None
        o_done = None
        for win in range(NW):
            t0 = win * WIN
            def mksh(e, t0=t0, win=win):
                for c4 in range(4):
                    c = 4 + c4
                    dst = hshw[:, c4 * 512:(c4 + 1) * 512]
                    if win == 0:
                        e.memset(dst[:, 0:1], 0.0)
                    else:
                        e.tensor_copy(dst[:, 0:1],
                                      ybnd[:, c4 * 3 + win - 1: c4 * 3 + win])
                    e.tensor_copy(dst[:, 1:512],
                                  y[:, c * T + t0: c * T + t0 + 511])
                return None
            w = [y_ev, bnd_ev]
            if o_done:
                w.append(o_done)
            sh_ev = mk.op("vector", mksh, waits=w, inc=sv)

            gevt = mk.op("gpsimd", lambda e: e.iota(
                ptmp, [[1, 512]], channel_multiplier=0,
                allow_small_or_imprecise_dtypes=True),
                waits=[sh_ev, consts_ev], inc=gpp)

            def tb1(e, t0=t0):
                e.tensor_scalar(ptm2, ptmp, float(t0), invf, ALU.add, ALU.mult)
                e.drain()
                e.tensor_scalar(ptmp, ptm2, 1.0 / (2 * PI), M23, ALU.mult, ALU.add)
                e.drain()
                e.tensor_scalar(ptmp, ptmp, M23, None, ALU.subtract)
                e.drain()
                e.scalar_tensor_tensor(ptmp, ptmp, -2 * PI, ptm2, ALU.mult, ALU.add)
                e.drain()
                e.tensor_scalar(ptmp, ptmp, PI, -PI, ALU.min, ALU.max)
                e.drain()
                return None
            ev = mk.op("vector", tb1, waits=[gevt, consts2_ev], inc=sv)
            ev = mk.op("scalar", lambda e: e.activation(
                tabs[:, 512:1024], ptmp, AF.Sin), waits=[ev], inc=sa)

            def tb2(e, t0=t0):
                e.tensor_scalar(ptmp, ptm2, 1.0 / (2 * PI), 0.25, ALU.mult, ALU.add)
                e.drain()
                e.tensor_scalar(ptmp, ptmp, M23, None, ALU.add)
                e.drain()
                e.tensor_scalar(ptmp, ptmp, M23, None, ALU.subtract)
                e.drain()
                e.scalar_tensor_tensor(ptmp, ptmp, -2 * PI, ptm2, ALU.mult, ALU.add)
                e.drain()
                e.tensor_scalar(ptmp, ptmp, PI / 2, None, ALU.add)
                e.drain()
                e.tensor_scalar(ptmp, ptmp, PI, -PI, ALU.min, ALU.max)
                e.drain()
                return None
            ev = mk.op("vector", tb2, waits=[ev], inc=sv)
            ev = mk.op("scalar", lambda e: e.activation(
                tabs[:, 0:512], ptmp, AF.Sin), waits=[ev], inc=sa)

            gevt2 = mk.op("gpsimd", lambda e: e.iota(
                ptmp, [[1, 512]], channel_multiplier=0,
                allow_small_or_imprecise_dtypes=True), waits=[ev], inc=gpp)

            def tb3(e, t0=t0):
                e.tensor_scalar(ptmp, ptmp, 1.0 / 256.0, (t0 - 1024.0) / 256.0,
                                ALU.mult, ALU.add)
                e.drain()
                e.tensor_scalar(ptm2, ptmp, lnb_c, None, ALU.mult)
                e.drain()
                return None
            ev = mk.op("vector", tb3, waits=[gevt2], inc=sv)
            ev = mk.op("scalar", lambda e: e.activation(ptmp, ptm2, AF.Exp),
                       waits=[ev], inc=sa)

            def tb4(e):
                e.tensor_tensor(tabs[:, 1024:1536], tabs[:, 0:512], ptmp, ALU.mult)
                e.drain()
                e.tensor_tensor(tabs[:, 1536:2048], tabs[:, 512:1024], ptmp, ALU.mult)
                e.drain()
                return None
            ev = mk.op("vector", tb4, waits=[ev], inc=sv)
            ev = mk.op("scalar", lambda e: e.activation(
                ptmp, ptm2, AF.Exp, scale=-1.0), waits=[ev], inc=sa)

            def tb5(e):
                e.tensor_tensor(tabs[:, 0:512], tabs[:, 0:512], ptmp, ALU.mult)
                e.drain()
                e.tensor_tensor(tabs[:, 512:1024], tabs[:, 512:1024], ptmp, ALU.mult)
                e.drain()
                return None
            tab_ev = mk.op("vector", tb5, waits=[ev], inc=sv)

            kcur = kw[win % 2]
            vTcur = vT[win % 2]
            proj_ev = {}
            for gi, blk0 in enumerate((0, 4, 8)):
                dsts = [qw, kcur, None][gi]
                evp = None
                for b4 in range(4):
                    blk = l * 12 + blk0 + b4
                    tok = ws.load(lambda s_, blk=blk: (
                        s_[:, 0:1024], wd["wqkvt"][blk * 128:(blk + 1) * 128, :]),
                        extra_waits=[sh_ev])
                    bank = st.pb[6 + (b4 % 2)]
                    w = [sh_ev]
                    if prev_qkv_banks:
                        w.append(prev_qkv_banks)

                    def pmm(e, tok=tok, t0=t0, bank=bank):
                        slot, sem, val = tok
                        e.wait_ge(sem.h, val)
                        for c in range(CH):
                            rhs = (y[:, c * T + t0: c * T + t0 + 512] if c < 4
                                   else hshw[:, (c - 4) * 512:(c - 3) * 512])
                            r = e.matmul(bank[:, :],
                                         wslot[slot][:, c * 128:(c + 1) * 128],
                                         rhs, start=(c == 0), stop=(c == CH - 1),
                                         skip_group_check=True)
                        return r
                    mev = mk.op("tensor", pmm, waits=w, inc=sp)
                    ws.consumed(tok[0], mev)
                    if gi < 2:
                        evp = mk.op("vector", lambda e, dsts=dsts, b4=b4, bank=bank:
                                    e.tensor_copy(dsts[:, b4 * 512:(b4 + 1) * 512],
                                                  bank[:, :]), waits=[mev], inc=sv)
                    else:
                        evp = mk.op("vector", lambda e, b4=b4, bank=bank:
                                    e.tensor_copy(P_sb[:, 0:512], bank[:, :]),
                                    waits=[mev] + ([evp] if evp else []), inc=sv)

                        def vtr(e, b4=b4):
                            pbf = st.pb[5][:, :].bitcast(BF16)
                            for j in range(4):
                                r = e.transpose(pbf[:, j * 128:(j + 1) * 128],
                                                P_sb[:, j * 128:(j + 1) * 128],
                                                st.ident_b)
                            return r
                        tev2 = mk.op("tensor", vtr, waits=[evp], inc=sp)

                        def vte(e, b4=b4, vTcur=vTcur):
                            pbf = st.pb[5][:, :].bitcast(BF16)
                            for j in range(4):
                                e.tensor_copy(
                                    vTcur[:, (2 * b4) * 256 + j * 64:
                                          (2 * b4) * 256 + (j + 1) * 64],
                                    pbf[0:128, j * 128: j * 128 + 64])
                                e.tensor_copy(
                                    vTcur[:, (2 * b4 + 1) * 256 + j * 64:
                                          (2 * b4 + 1) * 256 + (j + 1) * 64],
                                    pbf[0:128, j * 128 + 64: (j + 1) * 128])
                            return None
                        evp = mk.op("vector", vte, waits=[tev2], inc=sv)
                    prev_qkv_banks = evp
                proj_ev[gi] = evp

            def lnorm_rope(src, scol, tq, ev_in):
                evs = ev_in
                first = True
                for c4 in range(4):
                    sl = src[:, c4 * 512:(c4 + 1) * 512]

                    def sq(e, sl=sl):
                        return e.activation(ptm2, sl, AF.Square)
                    wsq = [evs] + ([tab_ev] if first else [])
                    first = False
                    ev1 = mk.op("scalar", sq, waits=wsq, inc=sa)

                    def nmm(e):
                        return e.matmul(st.pb[4][0:2, :], hsel[:, 0:2], ptm2,
                                        start=True, stop=True, skip_group_check=True)
                    ev1 = mk.op("tensor", nmm, waits=[ev1], inc=sp)

                    def nmath(e):
                        e.tensor_scalar(r2[0:2, :], st.pb[4][0:2, :], 1e-24, None,
                                        ALU.max)
                        e.drain()
                        return None
                    ev1 = mk.op("vector", nmath, waits=[ev1], inc=sv)
                    ev1 = mk.op("scalar", lambda e: e.activation(
                        r2[0:2, :], r2[0:2, :], AF.Sqrt), waits=[ev1], inc=sa)
                    ev1 = mk.op("vector", lambda e: e.reciprocal(
                        r2[0:2, :], r2[0:2, :]), waits=[ev1], inc=sv)

                    def nbc(e):
                        return e.matmul(st.pb[4][:, :], s2[0:2, :], r2[0:2, :],
                                        start=True, stop=True, skip_group_check=True)
                    ev1 = mk.op("tensor", nbc, waits=[ev1], inc=sp)

                    def napp(e, sl=sl, scol=scol):
                        e.scalar_tensor_tensor(sl, sl, scol, st.pb[4][:, :],
                                               ALU.mult, ALU.mult)
                        e.drain()
                        return None
                    ev1 = mk.op("vector", napp, waits=[ev1], inc=sv)

                    def rotmm(e, sl=sl):
                        return e.matmul(st.pb[6][:, :], Rm, sl, start=True,
                                        stop=True, skip_group_check=True)
                    ev1 = mk.op("tensor", rotmm, waits=[ev1, perm_ev], inc=sp)

                    def rope(e, sl=sl, tq=tq):
                        cc = tabs[:, 1024:1536] if tq else tabs[:, 0:512]
                        ss = tabs[:, 1536:2048] if tq else tabs[:, 512:1024]
                        e.tensor_tensor(sqt, sl, cc, ALU.mult)
                        e.drain()
                        e.tensor_tensor(P_sb[:, 0:512], st.pb[6][:, :], ss, ALU.mult)
                        e.drain()
                        e.tensor_tensor(sl, sqt, P_sb[:, 0:512], ALU.add)
                        e.drain()
                        return None
                    evs = mk.op("vector", rope, waits=[ev1, tab_ev], inc=sv)
                return evs
            q_ev = lnorm_rope(qw, qs_c[:, l:l + 1], True, proj_ev[0])
            k_ev = lnorm_rope(kcur, ks_c[:, l:l + 1], False, proj_ev[1])
            prev_qkv_banks = k_ev

            kprev = kw[(win + 1) % 2]
            vTprev = vT[(win + 1) % 2]
            exp_prev = None
            for h8 in range(8):
                c = h8 // 2
                hb = (h8 % 2) * 64
                for qt in range(4):
                    nk = 512 if win == 0 else 1024

                    def smm(e, c=c, hb=hb, qt=qt, win=win, kprev=kprev, kcur=kcur):
                        r = None
                        if win > 0:
                            r = e.matmul(st.pb[0][:, :],
                                         qw[c * 128 + hb if False else 0:0, 0:0]
                                         if False else
                                         qw[hb:hb + 64, c * 512 + qt * 128:
                                            c * 512 + (qt + 1) * 128],
                                         kprev[hb:hb + 64, c * 512:(c + 1) * 512],
                                         start=True, stop=True,
                                         skip_group_check=True)
                        r = e.matmul(st.pb[1][:, :],
                                     qw[hb:hb + 64, c * 512 + qt * 128:
                                        c * 512 + (qt + 1) * 128],
                                     kcur[hb:hb + 64, c * 512:(c + 1) * 512],
                                     start=True, stop=True, skip_group_check=True)
                        return r
                    w = [q_ev, k_ev]
                    if exp_prev:
                        w.append(exp_prev)
                    sev = mk.op("tensor", smm, waits=w, inc=sp)

                    def msk_max(e, qt=qt, win=win):
                        off = (3 - qt) * 128
                        if win == 0:
                            e.tensor_tensor(ptmp, st.pb[1][:, :],
                                            Tm[:, off:off + 512], ALU.add)
                            e.drain()
                            e.tensor_reduce(m1, ptmp, AX.X, ALU.max)
                            e.drain()
                        else:
                            e.tensor_tensor(ptmp, st.pb[0][:, :],
                                            Mb[:, off:off + 512], ALU.add)
                            e.drain()
                            e.tensor_tensor(ptm2, st.pb[1][:, :],
                                            Mb[:, off + 512:off + 1024], ALU.add)
                            e.drain()
                            e.tensor_reduce(m1, ptmp, AX.X, ALU.max)
                            e.drain()
                            e.tensor_reduce(m1s, ptm2, AX.X, ALU.max)
                            e.drain()
                            e.tensor_tensor(m1, m1, m1s, ALU.max)
                            e.drain()
                        e.tensor_scalar(m1s, m1, -8.0, None, ALU.mult)
                        e.drain()
                        return None
                    mev = mk.op("vector", msk_max, waits=[sev], inc=sv)

                    def expp(e, win=win):
                        if win == 0:
                            e.activation(P_sb[:, 512:1024], ptmp, AF.Exp, bias=m1s,
                                         scale=8.0, accum_out=den)
                            e.drain()
                        else:
                            e.activation(P_sb[:, 0:512], ptmp, AF.Exp, bias=m1s,
                                         scale=8.0, accum_out=den)
                            e.drain()
                            e.activation(P_sb[:, 512:1024], ptm2, AF.Exp, bias=m1s,
                                         scale=8.0, accum_out=rec)
                            e.drain()
                        return None
                    exp_prev = mk.op("scalar", expp, waits=[mev], inc=sa)

                    def norm(e, win=win):
                        if win > 0:
                            e.tensor_tensor(den, den, rec, ALU.add)
                            e.drain()
                        e.reciprocal(rec, den)
                        e.drain()
                        if win == 0:
                            e.tensor_scalar(P_sb[:, 512:1024], P_sb[:, 512:1024],
                                            rec, None, ALU.mult)
                        else:
                            e.tensor_scalar(P_sb[:, :], P_sb[:, :], rec, None,
                                            ALU.mult)
                        e.drain()
                        return None
                    nev = mk.op("vector", norm, waits=[exp_prev], inc=sv)

                    nch = 4 if win == 0 else 8
                    ch0 = 4 if win == 0 else 0

                    def ptr(e, nch=nch, ch0=ch0):
                        pbf = st.pb[5][:, :].bitcast(BF16)
                        for j in range(nch):
                            r = e.transpose(pbf[:, j * 128:(j + 1) * 128],
                                            P_sb[:, (ch0 + j) * 128:
                                                 (ch0 + j + 1) * 128],
                                            st.ident_b)
                        return r
                    tev3 = mk.op("tensor", ptr, waits=[nev], inc=sp)

                    def pte(e, nch=nch):
                        pbf = st.pb[5][:, :].bitcast(BF16)
                        for j in range(nch):
                            e.tensor_copy(PT[:, j * 128:(j + 1) * 128],
                                          pbf[:, j * 128:(j + 1) * 128])
                        return None
                    ptev = mk.op("vector", pte, waits=[tev3], inc=sv)

                    def pvm(e, h8=h8, nch=nch, win=win, vTprev=vTprev, vTcur=vTcur):
                        r = None
                        for j in range(nch):
                            if win == 0:
                                vt_ = vTcur
                                kc = j
                            else:
                                vt_ = vTprev if j < 4 else vTcur
                                kc = j % 4
                            r = e.matmul(st.pb[2][0:64, 0:128],
                                         vt_[:, h8 * 256 + kc * 64:
                                             h8 * 256 + (kc + 1) * 64],
                                         PT[:, j * 128:(j + 1) * 128],
                                         start=(j == 0), stop=(j == nch - 1),
                                         skip_group_check=True)
                        return r
                    w = [ptev, proj_ev[2]]
                    if o_done:
                        w.append(o_done)
                    pvev = mk.op("tensor", pvm, waits=w, inc=sp)
                    o_done = mk.op("vector", lambda e, hb=hb, c=c, qt=qt:
                                   e.tensor_copy(
                                       Ow[hb:hb + 64, c * 512 + qt * 128:
                                          c * 512 + (qt + 1) * 128],
                                       st.pb[2][0:64, 0:128]),
                                   waits=[pvev], inc=sv)

            for o in range(CH):
                tok = ws.load(lambda s_, o=o, l=l: (
                    s_[:, 0:512], wd["wot"][(l * 8 + o) * 128:(l * 8 + o + 1) * 128, :]),
                    extra_waits=[o_done])

                def wmm(e, tok=tok, o=o):
                    slot, sem, val = tok
                    e.wait_ge(sem.h, val)
                    for c4 in range(4):
                        r = e.matmul(st.pb[3][:, :],
                                     wslot[slot][:, c4 * 128:(c4 + 1) * 128],
                                     Ow[:, c4 * 512:(c4 + 1) * 512],
                                     start=(c4 == 0), stop=(c4 == 3),
                                     skip_group_check=True)
                    return r
                mev = mk.op("tensor", wmm, waits=[o_done], inc=sp)
                ws.consumed(tok[0], mev)
                o_done = mk.op("vector", lambda e, o=o, t0=t0: e.tensor_tensor(
                    y[:, o * T + t0: o * T + t0 + 512],
                    y[:, o * T + t0: o * T + t0 + 512],
                    st.pb[3][:, :], ALU.add), waits=[mev], inc=sv)
        attn_done = o_done
        phase_n += 1
        if phase_n >= phases:
            y_ev = attn_done
            break

        hn = st.B.t[:, 0:4096]
        ag = st.B.t[:, 4096:4096 + 11264]

        def savebnd2(e):
            for c4 in range(4):
                c = 4 + c4
                for wq in range(3):
                    e.tensor_copy(ybnd[:, c4 * 3 + wq: c4 * 3 + wq + 1],
                                  y[:, c * T + (wq + 1) * 512 - 1:
                                    c * T + (wq + 1) * 512])
            return None
        ffn_done = mk.op("vector", savebnd2, waits=[attn_done], inc=sv)
        for nt in range(NT):
            t0 = nt * 512
            def fsh2(e, t0=t0, nt=nt):
                for c4 in range(4):
                    c = 4 + c4
                    dst = hn[:, c * 512:(c + 1) * 512]
                    if nt == 0:
                        e.memset(dst[:, 0:1], 0.0)
                    else:
                        e.tensor_copy(dst[:, 0:1],
                                      ybnd[:, c4 * 3 + nt - 1: c4 * 3 + nt])
                    e.tensor_copy(dst[:, 1:512],
                                  y[:, c * T + t0: c * T + t0 + 511])
                return None
            sh_ev = mk.op("vector", fsh2, waits=[ffn_done], inc=sv)

            stat_ev = sh_ev
            for c in range(CH):
                src_fn = (lambda c=c, t0=t0: y[:, c * T + t0: c * T + t0 + 512]
                          if c < 4 else hn[:, c * 512:(c + 1) * 512])
                src = src_fn()
                sqev = mk.op("scalar", lambda e, src=src: e.activation(
                    sqt, src, AF.Square), waits=[stat_ev], inc=sa)

                def stmm(e, src=src, c=c):
                    e.matmul(st.pb[4][0:1, :], st.ones_b, src,
                             start=(c == 0), stop=(c == CH - 1),
                             skip_group_check=True)
                    return e.matmul(st.pb[5][0:1, :], st.ones_b, sqt[:, 0:512],
                                    start=(c == 0), stop=(c == CH - 1),
                                    skip_group_check=True)
                stat_ev = mk.op("tensor", stmm, waits=[sqev], inc=sp)

            def fsmath(e):
                e.tensor_scalar(smu, st.pb[4][0:1, :], 1.0 / D, None,
                                ALU.mult)
                e.drain()
                e.tensor_tensor(srs, smu, smu, ALU.mult)
                e.drain()
                e.scalar_tensor_tensor(srs, st.pb[5][0:1, :], 1.0 / D,
                                       srs, ALU.mult, ALU.subtract)
                e.drain()
                e.tensor_scalar(srs, srs, LN_EPS, None, ALU.add)
                e.drain()
                return None
            ev = mk.op("vector", fsmath, waits=[stat_ev], inc=sv)
            ev = mk.op("scalar", lambda e: e.activation(
                srs, srs, AF.Sqrt), waits=[ev], inc=sa)
            ev = mk.op("vector", lambda e: e.reciprocal(srs, srs),
                       waits=[ev], inc=sv)

            def fbc(e):
                e.matmul(st.pb[4][:, :], st.ones_f, smu, start=True,
                         stop=True, skip_group_check=True)
                return e.matmul(st.pb[5][:, :], st.ones_f, srs[0:1, :], start=True,
                                stop=True, skip_group_check=True)
            ev = mk.op("tensor", fbc, waits=[ev], inc=sp)

            def fapply(e, t0=t0, l=l):
                for cp in range(0, CH, 2):
                    pair = ((cp, ptmp), (cp + 1, ptm2))
                    for c, t_ in pair:
                        srcc = (y[:, c * T + t0: c * T + t0 + 512] if c < 4
                                else hn[:, c * 512:(c + 1) * 512])
                        e.tensor_tensor(t_, srcc, st.pb[4][:, :], ALU.subtract)
                    e.drain()
                    for c, t_ in pair:
                        e.tensor_tensor(t_, t_, st.pb[5][:, :], ALU.mult)
                    e.drain()
                    for c, t_ in pair:
                        e.tensor_scalar(hn[:, c * 512:(c + 1) * 512], t_,
                                        lng_c[:, l * 8 + c: l * 8 + c + 1],
                                        lnb2_c[:, l * 8 + c: l * 8 + c + 1],
                                        ALU.mult, ALU.add)
                    e.drain()
                return None
            hn_ev = mk.op("vector", fapply, waits=[ev], inc=sv)

            agev = None
            for j in range(22):
                tokg = ws.load(lambda s_, j=j, l=l: (
                    s_[:, 0:1024], wd["w1gt"][(l * 22 + j) * 128:
                                              (l * 22 + j + 1) * 128, :]),
                    extra_waits=[hn_ev])
                toka = ws.load(lambda s_, j=j, l=l: (
                    s_[:, 0:1024], wd["w1at"][(l * 22 + j) * 128:
                                              (l * 22 + j + 1) * 128, :]),
                    extra_waits=[hn_ev])
                bg = st.pb[(j % 3) * 2]
                ba = st.pb[(j % 3) * 2 + 1]

                def agmm(e, tokg=tokg, toka=toka, bg=bg, ba=ba):
                    r = None
                    for (tok, bank) in ((tokg, bg), (toka, ba)):
                        slot, sem, val = tok
                        e.wait_ge(sem.h, val)
                        for c in range(CH):
                            r = e.matmul(bank[:, :],
                                         wslot[slot][:, c * 128:(c + 1) * 128],
                                         hn[:, c * 512:(c + 1) * 512],
                                         start=(c == 0), stop=(c == CH - 1),
                                         skip_group_check=True)
                    return r
                w = [hn_ev]
                if agev:
                    w.append(agev)
                mev = mk.op("tensor", agmm, waits=w, inc=sp)
                ws.consumed(tokg[0], mev)
                ws.consumed(toka[0], mev)
                gelev = mk.op("scalar", lambda e, j=j, l=l, bg=bg: e.activation(
                    ag[:, j * 512:(j + 1) * 512], bg[:, :], AF.Gelu,
                    bias=b1g_c[:, l * 22 + j: l * 22 + j + 1]),
                    waits=[mev], inc=sa)
                agev = mk.op("vector", lambda e, j=j, l=l, ba=ba:
                             e.scalar_tensor_tensor(
                                 ag[:, j * 512:(j + 1) * 512], ba[:, :],
                                 b1a_c[:, l * 22 + j: l * 22 + j + 1],
                                 ag[:, j * 512:(j + 1) * 512],
                                 ALU.add, ALU.mult), waits=[gelev], inc=sv)
            ag_done = agev

            ffn_done = None
            for o in range(CH):
                toks = [ws.load(lambda s_, o=o, l=l, hh=hh: (
                    s_, wd["w2t3"][(l * 8 + o) * 128:(l * 8 + o + 1) * 128,
                                   hh * 1408:(hh + 1) * 1408]),
                    extra_waits=[ag_done]) for hh in range(2)]

                def w2mm(e, toks=toks, o=o):
                    r = None
                    for hh in range(2):
                        slot, sem, val = toks[hh]
                        e.wait_ge(sem.h, val)
                        for kc in range(11):
                            kcg = hh * 11 + kc
                            r = e.matmul(st.pb[o][:, :],
                                         wslot[slot][:, kc * 128:(kc + 1) * 128],
                                         ag[:, kcg * 512:(kcg + 1) * 512],
                                         start=(hh == 0 and kc == 0),
                                         stop=(hh == 1 and kc == 10),
                                         skip_group_check=True)
                    return r
                w = [ag_done]
                if ffn_done:
                    w.append(ffn_done)
                mev = mk.op("tensor", w2mm, waits=w, inc=sp)
                for t_ in toks:
                    ws.consumed(t_[0], mev)
                ffn_done = mk.op("vector", lambda e, o=o, t0=t0, l=l:
                                 e.scalar_tensor_tensor(
                                     y[:, o * T + t0: o * T + t0 + 512],
                                     st.pb[o][:, :],
                                     b2_c[:, l * 8 + o: l * 8 + o + 1],
                                     y[:, o * T + t0: o * T + t0 + 512],
                                     ALU.add, ALU.add), waits=[mev], inc=sv)
        y_ev = ffn_done
        phase_n += 1
        if phase_n >= phases:
            break

    def xup(e):
        for c in range(CH):
            for nt in range(NT):
                xc = st.x[:, c * T + nt * 512: c * T + nt * 512 + 512]
                e.scalar_tensor_tensor(xc, y[:, c * T + nt * 512:
                                             c * T + nt * 512 + 512], 0.5,
                                       xc, ALU.mult, ALU.add)
                e.drain()
        return None
    done = mk.op("vector", xup, waits=[y_ev], inc=sv)
    st.A.pop(); st.B.pop(); st.Cf.pop(); st.W.pop()
    return done




def wtile(w):
    i, o = w.shape
    return np.ascontiguousarray(
        w.reshape(i // 128, 128, o // 128, 128).transpose(2, 1, 0, 3)
         .reshape(o, i))


def host_cols(a, nblk):
    E = a.shape[0]
    return np.ascontiguousarray(a.reshape(E, nblk, 128).transpose(2, 0, 1)
                                 .reshape(128, E * nblk))


HID3 = 2730
HIDP3 = 2816


def _prep_weights(I):
    f32 = lambda a: np.ascontiguousarray(np.asarray(a), dtype=np.float32)
    w = {}
    w["f0_w0t"] = f32(np.asarray(I["L0_ffn_weight"]).T)
    w["a0_hf"] = f32(np.asarray(I["L0_attn_h_filter"]).T)
    w["a0_gf"] = f32(np.asarray(I["L0_attn_g_filter"]).T)
    w["a0_hs"] = f32(np.asarray(I["L0_attn_h_synth"]).T)
    w["a0_gs"] = f32(np.asarray(I["L0_attn_g_synth"]).T)
    w["a0_gamma"] = f32(np.asarray(I["L0_attn_norm_gamma"]).reshape(8, 128).T)
    w["a0_beta"] = f32(np.asarray(I["L0_attn_norm_beta"]).reshape(8, 128).T)
    w["f1_rwt"] = f32(np.asarray(I["L1_ffn_router_w"]).T)
    w["f1_rb"] = f32(np.asarray(I["L1_ffn_router_b"]).reshape(8, 1))
    w["f1_ewt"] = f32(np.asarray(I["L1_ffn_expert_w"]).transpose(0, 2, 1)
                      .reshape(8 * 1024, 1024))
    w["f1_eb"] = f32(I["L1_ffn_expert_b"])
    w["a1_wqkv"] = wtile(f32(I["L1_attn_w_qkv"]))
    w["a1_wproj"] = wtile(f32(I["L1_attn_w_proj"]))
    w["a2_wq"] = wtile(f32(I["L2_attn_Wq"]))
    w["a2_wkv"] = wtile(f32(I["L2_attn_Wkv"]))
    w["a2_wout"] = wtile(f32(I["L2_attn_Wout"]))
    w["a2_bout"] = f32(np.asarray(I["L2_attn_bout"]).reshape(8, 128).T)
    W1 = f32(I["L2_ffn_mlp_W1"])
    W2 = f32(I["L2_ffn_mlp_W2"])
    W0 = f32(I["L2_ffn_kan_sl_W0"])
    W1k = f32(I["L2_ffn_kan_sl_W1"])
    w["f2_gw"] = f32(I["L2_ffn_gate_W"])
    w["f2_w1t"] = np.concatenate([wtile(W1[e]) for e in range(4)], 0)
    w["f2_w2t"] = np.concatenate([wtile(W2[e]) for e in range(4)], 0)
    W0r = W0.reshape(4, 1024, 8, 2048).transpose(0, 2, 1, 3)
    w["f2_w0t"] = np.concatenate([wtile(np.ascontiguousarray(W0r[e, g]))
                                  for e in range(4) for g in range(8)], 0)
    W1r = W1k.reshape(4, 2048, 8, 1024).transpose(0, 2, 1, 3)
    w["f2_w1kt"] = np.concatenate([wtile(np.ascontiguousarray(W1r[e, g]))
                                   for e in range(4) for g in range(8)], 0)
    w["f2_b1"] = host_cols(f32(I["L2_ffn_mlp_b1"]), 32)
    w["f2_b2r"] = f32(I["L2_ffn_mlp_b2"])
    w["f2_g0c"] = host_cols(f32(I["L2_ffn_kan_ln_g0"]), 8)
    w["f2_b0c"] = host_cols(f32(I["L2_ffn_kan_ln_b0"]), 8)
    w["f2_g1c"] = host_cols(f32(I["L2_ffn_kan_ln_g1"]), 16)
    w["f2_b1cc"] = host_cols(f32(I["L2_ffn_kan_ln_b1"]), 16)
    w["f2_b0k"] = host_cols(f32(I["L2_ffn_kan_sl_b0"]), 16)
    w["f2_b1k"] = host_cols(f32(I["L2_ffn_kan_sl_b1"]), 8)
    w["a3_wqkv"] = wtile(f32(I["L3_attn_Wqkv"]))
    w["a3_wout"] = wtile(f32(I["L3_attn_Wout"]))
    w["a3_bout"] = f32(np.asarray(I["L3_attn_bout"]).reshape(8, 128).T)
    w["a3_bqkv"] = f32(np.asarray(I["L3_attn_bqkv"]).reshape(24, 128).T)
    wqkv3 = f32(I["L3_ffn_wqkv"])
    wo3 = f32(I["L3_ffn_wo"])
    w13 = f32(I["L3_ffn_w1"])
    w23 = f32(I["L3_ffn_w2"])
    qs3 = f32(I["L3_ffn_q_scale"])
    ks3 = f32(I["L3_ffn_k_scale"])
    lg3 = f32(I["L3_ffn_ln_g"])
    lb3 = f32(I["L3_ffn_ln_b"])
    b13 = f32(I["L3_ffn_b1"])
    b23 = f32(I["L3_ffn_b2"])

    def padw(a, n):
        out = np.zeros((a.shape[0], n), np.float32)
        out[:, :a.shape[1]] = a
        return out

    def padc(vec, n):
        out = np.zeros(n, np.float32)
        out[:len(vec)] = vec
        return out
    w["f3_wqkvt"] = np.concatenate([wtile(wqkv3[l]) for l in range(2)], 0)
    w["f3_wot"] = np.concatenate([wtile(wo3[l]) for l in range(2)], 0)
    w["f3_w1at"] = np.concatenate(
        [wtile(padw(w13[l][:, :HID3], HIDP3)) for l in range(2)], 0)
    w["f3_w1gt"] = np.concatenate(
        [wtile(padw(w13[l][:, HID3:], HIDP3)) for l in range(2)], 0)
    w2p = []
    for l in range(2):
        t_ = np.zeros((HIDP3, 1024), np.float32)
        t_[:HID3] = w23[l]
        w2p.append(wtile(t_))
    w["f3_w2t3"] = np.concatenate(w2p, 0)
    w["f3_qs"] = np.ascontiguousarray(
        np.stack([np.tile(qs3[l], 2) for l in range(2)], 1))
    w["f3_ks"] = np.ascontiguousarray(
        np.stack([np.tile(ks3[l], 2) for l in range(2)], 1))
    w["f3_lng"] = np.ascontiguousarray(
        np.concatenate([lg3[l].reshape(8, 128).T for l in range(2)], 1))
    w["f3_lnb"] = np.ascontiguousarray(
        np.concatenate([lb3[l].reshape(8, 128).T for l in range(2)], 1))
    w["f3_b1a"] = np.ascontiguousarray(np.concatenate(
        [padc(b13[l][:HID3], HIDP3).reshape(22, 128).T for l in range(2)], 1))
    w["f3_b1g"] = np.ascontiguousarray(np.concatenate(
        [padc(b13[l][HID3:], HIDP3).reshape(22, 128).T for l in range(2)], 1))
    w["f3_b2c3"] = np.ascontiguousarray(np.concatenate(
        [b23[l].reshape(8, 128).T for l in range(2)], 1))
    return w


def build_full(wshapes):
    nc = bass.Bass("TRN2", target_bir_lowering=False, debug=False)
    x_t = nc.dram_tensor("x_t", [D, T], F32, kind="ExternalInput").ap()
    y_t = nc.dram_tensor("y_t", [D, T], F32, kind="ExternalOutput").ap()
    dr = {}
    for k, shp in wshapes.items():
        dr[k] = nc.dram_tensor(k, list(shp), F32, kind="ExternalInput").ap()

    mk = MK(nc)
    st = State(mk)
    sems = {k: mk.sem(k) for k in ["v", "a", "p", "g"]}
    ldx = mk.sem("ldx")
    for c in range(CH):
        mk.op("sync", lambda e, c=c: e.dma_start(
            out=st.x[:, c * T:(c + 1) * T],
            in_=x_t[c * 128:(c + 1) * 128, :]), inc=ldx, dma=True)
    d = ldx.ev

    sub = lambda pre: {k[len(pre):]: v for k, v in dr.items() if k.startswith(pre)}

    st.W.push()
    wsig = st.W.alloc(8192)
    f0p = emit_f0_prep(mk, st, dr["f0_w0t"], sems, wsig, gate=d)
    xln = emit_ln(mk, st, d, sems)
    d = emit_f0(mk, st, xln, f0p, wsig, sems)
    xln = emit_ln(mk, st, d, sems)
    d = emit_a0(mk, st, xln, sub("a0_"), sems)
    xln = emit_ln(mk, st, d, sems)
    d = emit_f0(mk, st, xln, f0p, wsig, sems)
    st.W.pop()
    xln = emit_ln(mk, st, d, sems)
    d = emit_f1(mk, st, d, xln, sub("f1_"), sems)
    xln = emit_ln(mk, st, d, sems)
    d = emit_attn_l1(mk, st, xln,
                     {"wqkv_t": dr["a1_wqkv"], "wproj_t": dr["a1_wproj"]}, sems)
    xln = emit_ln(mk, st, d, sems)
    d = emit_f1(mk, st, d, xln, sub("f1_"), sems)
    xln = emit_ln(mk, st, d, sems)
    d = emit_f2(mk, st, xln, sub("f2_"), sems)
    xln = emit_ln(mk, st, d, sems)
    d = emit_attn_l23(mk, st, xln,
                      {"wq_t": dr["a2_wq"], "wkv_t": dr["a2_wkv"],
                       "wout_t": dr["a2_wout"], "bout": dr["a2_bout"]}, sems, 2)
    xln = emit_ln(mk, st, d, sems)
    d = emit_f2(mk, st, xln, sub("f2_"), sems)
    xln = emit_ln(mk, st, d, sems)
    d = emit_f3(mk, st, xln, sub("f3_"), sems)
    xln = emit_ln(mk, st, d, sems)
    d = emit_attn_l23(mk, st, xln,
                      {"wqkv_t": dr["a3_wqkv"], "wout_t": dr["a3_wout"],
                       "bout": dr["a3_bout"], "bqkv": dr["a3_bqkv"]}, sems, 3)
    xln = emit_ln(mk, st, d, sems)
    d = emit_f3(mk, st, xln, sub("f3_"), sems)

    stc = mk.sem("stc")
    for c in range(CH):
        mk.op("gpsimd", lambda e, c=c: e.dma_start(
            out=y_t[c * 128:(c + 1) * 128, :],
            in_=st.x[:, c * T:(c + 1) * T]), waits=[d], inc=stc, dma=True)
    mk.op("gpsimd", lambda e: None, waits=[stc.ev])
    mk.finish()
    return nc


_BUILT = {}


def kernel(**inputs):
    w = _prep_weights(inputs)
    x = np.ascontiguousarray(np.asarray(inputs["L0_attn_x"]), dtype=np.float32)
    key = "full"
    if key not in _BUILT:
        _BUILT[key] = build_full({k: v.shape for k, v in w.items()})
    nc = _BUILT[key]
    in_maps = []
    for b in range(8):
        m = {"x_t": np.ascontiguousarray(x[b].T)}
        m.update(w)
        in_maps.append(m)
    res = run_bass_kernel_spmd(nc, in_maps, core_ids=list(range(8)))
    out = np.stack([np.asarray(res.results[b]["y_t"]).T for b in range(8)], 0)
    return np.ascontiguousarray(out).astype(np.float32)
